# version 01
# baseline (speedup 1.0000x reference)
import os
import numpy as np
import ml_dtypes

import concourse.bass as bass
import concourse.mybir as mybir
from concourse.ap import AP as _AP
from concourse.bass_utils import run_bass_kernel_spmd

bf16 = ml_dtypes.bfloat16
F32 = mybir.dt.float32
BF16 = mybir.dt.bfloat16
AF = mybir.ActivationFunctionType
ALU = mybir.AluOpType
AX = mybir.AxisListType

NCORES = 8
T = 16384
TPC = T // NCORES
D = 2048
H = 16
DK = 128
SUP = 256
NSUP = TPC // SUP
NTB = TPC // 128
DFF = 5632
FS1 = DFF // NCORES
FS3 = D // NCORES
NSG = 88
NB_TOT = DFF * D // 64

_NF4 = np.array([-1.0, -0.6962, -0.5251, -0.3949, -0.2844, -0.1848, -0.0911,
                 0.0, 0.0796, 0.1609, 0.2461, 0.3379, 0.4407, 0.5626, 0.723,
                 1.0], dtype=bf16)


def _nf4_consts():
    bits = np.arange(65536, dtype=np.uint16)
    av = bits.view(bf16)
    fv = av.astype(np.float32)
    sel = np.isfinite(fv) & (np.abs(fv) <= 1.0)
    qs = np.sort(av[sel].astype(np.float32), kind="stable")
    d = np.abs((qs.astype(bf16)[:, None] - _NF4[None, :]).astype(bf16))
    idx = np.argmin(d.astype(np.float32), axis=1)
    thr = []
    for j in range(1, 16):
        k = np.searchsorted(idx, j, "left")
        thr.append(float(float(np.float32((float(qs[k - 1])) + float(qs[k])) / 2.0)))
    steps = []
    acc = _NF4[0]
    for j in range(1, 16):
        cand = np.array(float(_NF4[j]) - float(acc), dtype=bf16)
        if (acc + cand).astype(bf16) != _NF4[j]:
            cb = cand.view(np.uint16)
            for dlt in (1, -1, 2, -2):
                c2 = np.array(cb + dlt, dtype=np.uint16).view(bf16)[()]
                if (acc + c2).astype(bf16) == _NF4[j]:
                    cand = c2
                    break
        steps.append(float(float(np.float32(np.array(cand, dtype=bf16)))))
        acc = (acc + cand).astype(bf16)[()]
    return thr, steps


NF4_THR, NF4_STEP = _nf4_consts()
MAGIC = 12582912.0
EPS_LN = 1e-5


def _head_perm():
    p = np.zeros(D, np.int64)
    for h in range(H):
        b = h * DK
        p[b:b + 64] = b + np.arange(0, DK, 2)
        p[b + 64:b + DK] = b + np.arange(1, DK, 2)
    return p


PERM = _head_perm()


class Prog:

    ENGINES = ("sync", "vector", "scalar", "tensor", "gpsimd")

    def __init__(self, nc):
        self.nc = nc
        self.streams = {e: [] for e in self.ENGINES}

    def sy(self, fn): self.streams["sync"].append(fn)
    def v(self, fn): self.streams["vector"].append(fn)
    def c(self, fn): self.streams["scalar"].append(fn)
    def t(self, fn): self.streams["tensor"].append(fn)
    def g(self, fn): self.streams["gpsimd"].append(fn)

    def finalize(self):
        with self.nc.Block() as blk:
            for name in self.ENGINES:
                fns = self.streams[name]
                if not fns:
                    continue

                def body(e, fns=fns):
                    for fn in fns:
                        fn(e)

                getattr(blk, name)(body)


class Sem:
    def __init__(self, nc, name):
        self.h = nc.alloc_semaphore(name)
        self.v = 0

    def inc(self, n=1):
        self.v += n
        return self.v


class DStream:

    def __init__(self, nc, name, nbuf=2):
        self.sems = [Sem(nc, f"{name}_{i}") for i in range(nbuf)]
        self.n = 0

    def issue(self, P_eng, fn):
        s = self.sems[self.n % len(self.sems)]
        s.inc(16)
        v = s.v
        P_eng(lambda e, s=s: fn(e).then_inc(s.h, 16))
        self.n += 1
        return (s, v)

    def wait(self, P_eng, tok):
        s, v = tok
        P_eng(lambda e, s=s, v=v: e.wait_ge(s.h, v))

    def wait_all(self, P_eng):
        for s in self.sems:
            P_eng(lambda e, s=s, v=s.v: e.wait_ge(s.h, v))


def _tile_rows(R):
    out, r = [], 0
    while r < R:
        n = min(128, R - r)
        out.append((r, n))
        r += n
    return out


def build_A():
    nc = bass.Bass("TRN2", target_bir_lowering=False, detect_race_conditions=False)
    f = nc.dram_tensor

    xA = f("xA", [TPC, D], F32, kind="ExternalInput")
    WkT = f("WkT", [D, D], F32, kind="ExternalInput")
    WvT = f("WvT", [D, D], F32, kind="ExternalInput")
    bk = f("bk", [1, D], F32, kind="ExternalInput")
    bv = f("bv", [1, D], F32, kind="ExternalInput")
    cosA = f("cosA", [TPC, 64], F32, kind="ExternalInput")
    sinA = f("sinA", [TPC, 64], F32, kind="ExternalInput")
    ident = f("ident", [128, 128], BF16, kind="ExternalInput")
    w1f = f("w1f", [DFF, D], BF16, kind="ExternalInput")
    w2f = f("w2f", [DFF, D], BF16, kind="ExternalInput")
    w3f = f("w3f", [D, DFF], BF16, kind="ExternalInput")
    w1s = f("w1s", [FS1, D], BF16, kind="ExternalInput")
    w2s = f("w2s", [FS1, D], BF16, kind="ExternalInput")
    w3s = f("w3s", [FS3, DFF], BF16, kind="ExternalInput")

    o_x2 = f("o_x2", [D, TPC], BF16, kind="ExternalOutput")
    o_ktm = f("o_ktm", [TPC, D], BF16, kind="ExternalOutput")
    o_vtm = f("o_vtm", [TPC, D], BF16, kind="ExternalOutput")
    o_ssnap = f("o_ssnap", [H * NSUP * DK, DK], BF16, kind="ExternalOutput")
    o_zsnap = f("o_zsnap", [1, NSUP * D], F32, kind="ExternalOutput")
    o_w1t = f("o_w1t", [D, FS1], BF16, kind="ExternalOutput")
    o_dlvl = f("o_dlvl", [128, D], BF16, kind="ExternalOutput")
    o_dbgq = f("o_dbgq", [128, D], BF16, kind="ExternalOutput")
    o_dbgw = f("o_dbgw", [128, D], BF16, kind="ExternalOutput")
    o_dbgs = f("o_dbgs", [128, 128], BF16, kind="ExternalOutput")
    o_dbgl = f("o_dbgl", [128, D], BF16, kind="ExternalOutput")
    o_w2t = f("o_w2t", [D, FS1], BF16, kind="ExternalOutput")
    o_w3t = f("o_w3t", [DFF, FS3], BF16, kind="ExternalOutput")

    if os.environ.get("KA_FAKES1", "0") == "1":
        i_s1 = [f(f"i_s1_{i}", [1, FS1 * 32], BF16, kind="ExternalInput") for i in range(3)]
        i_dq = [f(f"i_dq_{i}", [1, FS1 * 32], BF16, kind="ExternalInput") for i in range(3)]
    d_s1 = [f(f"d_s1_{i}", [1, FS1 * 32 + 16], BF16) for i in range(3)]
    d_dq = [f(f"d_dq_{i}", [1, FS1 * 32 + 16], BF16) for i in range(3)]
    d_dq8 = [f(f"d_dq8_{i}", [1, FS1 * 32 * 8 + 256], BF16) for i in range(3)]

    sb = nc.alloc_sbuf_tensor
    s_cos = sb("s_cos", [128, NTB * 64], F32)
    s_sin = sb("s_sin", [128, NTB * 64], F32)
    s_ident = sb("s_ident", [128, 128], BF16)
    s_ones_row = sb("s_ones_row", [1, 128], BF16)
    s_ones_col = sb("s_ones_col", [128, 1], BF16)
    s_ones_f32 = sb("s_ones_f32", [128, 1], F32)
    s_bk = sb("s_bk", [1, D], BF16)
    s_bv = sb("s_bv", [1, D], BF16)


    s_wt = [sb(f"s_wt{i}", [128, DFF + 256], BF16) for i in range(2)]
    s_acc = [sb(f"s_acc{i}", [128, 1], F32) for i in range(3)]
    s_red = sb("s_red", [128, 96], F32)
    s_red2 = sb("s_red2", [128, 1], F32)
    s_sums = sb("s_sums", [1, 4], F32)
    s_smean = sb("s_smean", [1, 4], BF16)
    s_smeanB = sb("s_smeanB", [128, 3], F32)

    _pad0 = sb("_pad0", [128, 256], BF16)
    s_q = sb("s_q", [128, DFF + 256], BF16)
    s_lvl = sb("s_lvl", [128, DFF + 256], BF16)
    s_term = sb("s_term", [128, DFF + 256], BF16)
    s_s1 = sb("s_s1", [128, 96], F32)
    s_s1b = sb("s_s1b", [128, 96], BF16)
    s_r1 = sb("s_r1", [128, 96], F32)
    s_rt = sb("s_rt", [128, 96], F32)
    s_cent = sb("s_cent", [128, 96], BF16)
    s_dqs = sb("s_dqs", [128, 96], BF16)
    s_s1all = sb("s_s1all", [128, 1440], BF16)
    s_dq8 = sb("s_dq8", [NSG, 2048], BF16)
    s_dqs8 = sb("s_dqs8", [128, 704 + 128], BF16)
    s_grp = sb("s_grp", [NSG, 256], BF16)
    s_gax = sb("s_gax", [NSG, 1], BF16)
    s_ga32 = sb("s_ga32", [NSG, 1], F32)
    s_gq = sb("s_gq", [NSG, 1], F32)
    s_gq2 = sb("s_gq2", [NSG, 1], F32)
    s_gt = sb("s_gt", [NSG, 1], F32)
    s_qf = sb("s_qf", [NSG, 1], BF16)
    s_qf32 = sb("s_qf32", [NSG, 1], F32)
    s_prod = sb("s_prod", [NSG, 256], BF16)
    s_qsc = sb("s_qsc", [NSG, 256], F32)
    s_deqg = sb("s_deqg", [NSG, 256], BF16)
    s_wtr = [sb(f"s_wtr{i}", [128, 128], BF16) for i in range(2)]

    s_eps = sb("s_eps", [128, 1], F32)
    s_x = [sb(f"s_x{i}", [128, D], F32) for i in range(2)]
    s_b32 = s_x[0][0:1, 0:D]
    s_b32b = s_x[1][0:1, 0:D]
    s_xc = sb("s_xc", [128, D], F32)
    s_mu = sb("s_mu", [128, 1], F32)
    s_ss = sb("s_ss", [128, 1], F32)
    s_lnt = sb("s_lnt", [128, 1], F32)
    s_rstd = sb("s_rstd", [128, 1], F32)
    s_x2t = sb("s_x2t", [128, D], BF16)
    s_x2st = [sb(f"s_x2st{i}", [128, 128], BF16) for i in range(2)]
    s_x2b = [sb(f"s_x2b{i}", [128, 16 * 128], BF16) for i in range(2)]
    s_wf32 = sb("s_wf32", [128, 16 * 512], F32)
    s_wb = sb("s_wb", [128, 16 * 512], BF16)
    s_kblk = [sb(f"s_kblk{i}", [128, 512], BF16) for i in range(2)]
    s_ro1 = sb("s_ro1", [128, 256], BF16)
    s_ro2 = sb("s_ro2", [128, 256], BF16)
    s_el1 = sb("s_el1", [128, 512], BF16)
    s_el2 = sb("s_el2", [128, 512], BF16)
    s_zrow = sb("s_zrow", [1, 1024], F32)
    s_kv = [s_term[0:128, 0:D], s_term[0:128, D:2 * D],
            s_lvl[0:128, 0:D], s_lvl[0:128, D:2 * D]]
    s_vv = [s_wt[0][0:128, 0:D], s_wt[0][0:128, D:2 * D],
            s_wt[1][0:128, 0:D], s_wt[1][0:128, D:2 * D]]
    s_snap = None

    ps = nc.alloc_psum_tensor
    p_mm = [ps(f"p_mm{i}", [128, 512], F32) for i in range(2)]
    p_z = ps("p_z", [1, 512], F32)
    p_tr = ps("p_tr", [128, 128], BF16)
    p_s = [ps(f"p_s{i}", [128, 128], F32) for i in range(4)]
    p_one = p_z[0:1, 0:1]
    p_bc = p_mm[0][:, 0:1]

    P = Prog(nc)
    cst = Sem(nc, "cst")
    sm = Sem(nc, "sm")
    dq = Sem(nc, "dq")
    tr = Sem(nc, "tr")
    te = Sem(nc, "te")
    lnv = Sem(nc, "lnv")
    lns = Sem(nc, "lns")
    wcast = Sem(nc, "wcast")
    pj = Sem(nc, "pj")
    ev = Sem(nc, "ev")
    rp = Sem(nc, "rp")
    zs = Sem(nc, "zs")
    sn = Sem(nc, "sn")
    sne = Sem(nc, "sne")
    Lc = DStream(nc, "Lc", 1)
    Lw = DStream(nc, "Lw", 2)
    Lg = DStream(nc, "Lg", 1)
    L3 = DStream(nc, "L3", 1)
    Lx = DStream(nc, "Lx", 2)
    Lf = DStream(nc, "Lf", 1)
    L2 = DStream(nc, "L2", 2)
    Lkv = DStream(nc, "Lkv", 2)
    S1 = DStream(nc, "S1", 1)
    S2 = DStream(nc, "S2", 1)
    Swt = DStream(nc, "Swt", 2)
    Sx2 = DStream(nc, "Sx2", 2)
    Sspl = DStream(nc, "Sspl", 2)
    Szt = DStream(nc, "Szt", 2)
    Ssn = DStream(nc, "Ssn", 2)
    dbgs = Sem(nc, "dbgs")

    def W(P_eng, sem, val):
        P_eng(lambda e, s=sem.h, w=val: e.wait_ge(s, w))

    Lc.issue(P.sy, lambda e: e.dma_start(s_ident[:], ident.ap()))
    Lc.issue(P.sy, lambda e: e.dma_start(s_cos[:].rearrange("p (a f) -> p a f", f=64),
                                         cosA.ap().rearrange("(a p) f -> p a f", p=128)))
    Lc.issue(P.sy, lambda e: e.dma_start(s_sin[:].rearrange("p (a f) -> p a f", f=64),
                                         sinA.ap().rearrange("(a p) f -> p a f", p=128)))
    Lc.issue(P.sy, lambda e: e.dma_start(s_b32, bk.ap()))
    Lc.issue(P.sy, lambda e: e.dma_start(s_b32b, bv.ap()))
    Lc.wait_all(P.v)
    P.v(lambda e: e.memset(s_ones_row[:], 1.0))
    P.v(lambda e: e.memset(s_eps[:], float(np.float32(EPS_LN))))
    P.v(lambda e: e.memset(s_ones_col[:], 1.0))
    P.v(lambda e: e.memset(s_ones_f32[:], 1.0))
    P.v(lambda e: e.tensor_copy(s_bk[:], s_b32))
    P.v(lambda e: e.tensor_copy(s_bv[:], s_b32b))
    cst.inc()
    P.v(lambda e: e.drain().then_inc(cst.h, 1))

    _PH = int(os.environ.get("KA_PHASES", "9"))
    if os.environ.get("KA_NOPA1", "0") == "1":
        W(P.v, cst, cst.v)
        P.v(lambda e: e.memset(s_smeanB[:], 0.05))
        sm.inc()
        P.v(lambda e: e.drain().then_inc(sm.h, 1))
        mats = []
    else:
        mats = [(w1f, D, 44, 0), (w2f, D, 44, 1), (w3f, DFF, 16, 2)]
    sm_tile = 0
    for (wh, C, ntile, mi) in mats:
        G = C // 64
        P.v(lambda e, mi=mi: e.memset(s_acc[mi][:], 0.0))
        for t in range(ntile):
            if sm_tile >= 2:
                W(P.sy, sm, sm_tile - 1)
            tok = Lw.issue(P.sy, lambda e, wh=wh, t=t, C=C, b=sm_tile % 2: e.dma_start(
                s_wt[b][:, 0:C], wh.ap()[t * 128:(t + 1) * 128, :]))
            Lw.wait(P.v, tok)
            P.v(lambda e, b=sm_tile % 2, C=C, G=G: e.tensor_reduce(
                s_red[:, 0:G], s_wt[b][:, 0:C].rearrange("p (g k) -> p g k", k=64),
                AX.X, ALU.max, apply_absolute_value=True))
            P.v(lambda e: e.drain())
            P.v(lambda e, G=G: e.tensor_reduce(s_red2[:], s_red[:, 0:G], AX.X, ALU.add))
            P.v(lambda e: e.drain())
            P.v(lambda e, mi=mi: e.tensor_tensor(s_acc[mi][:], s_acc[mi][:], s_red2[:], ALU.add))
            sm.inc()
            P.v(lambda e: e.drain().then_inc(sm.h, 1))
            sm_tile += 1
    for mi in range(3):
        W(P.t, sm, sm.v)
        sm.inc()
        P.t(lambda e, mi=mi: e.matmul(p_one, s_ones_f32[:], s_acc[mi][:], start=True, stop=True).then_inc(sm.h, 1))
        W(P.c, sm, sm.v)
        sm.inc()
        P.c(lambda e, mi=mi: e.activation(s_sums[:, mi:mi + 1], p_one, AF.Copy).then_inc(sm.h, 1))
    W(P.v, sm, sm.v)
    P.v(lambda e: e.tensor_scalar(s_smean[:, 0:3], s_sums[:, 0:3], float(np.float32(1.0 / NB_TOT)), None, ALU.mult))
    sm.inc()
    P.v(lambda e: e.drain().then_inc(sm.h, 1))
    for mi in range(3):
        W(P.t, sm, sm.v)
        sm.inc()
        P.t(lambda e, mi=mi: e.matmul(p_bc, s_ones_row[:], s_smean[:, mi:mi + 1], start=True, stop=True).then_inc(sm.h, 1))
        W(P.c, sm, sm.v)
        sm.inc()
        P.c(lambda e, mi=mi: e.activation(s_smeanB[:, mi:mi + 1], p_bc, AF.Copy).then_inc(sm.h, 1))
    SMEAN_DONE = sm.v

    if _PH < 2:
        for stm in (S1, S2, Swt, Sx2, Sspl, Szt, Ssn):
            stm.wait_all(P.g)
        P.finalize()
        return nc
    shards = [(w1s, d_s1[0], d_dq[0], o_w1t, FS1, D, 0),
              (w2s, d_s1[1], d_dq[1], o_w2t, FS1, D, 1),
              (w3s, d_s1[2], d_dq[2], o_w3t, FS3, DFF, 2)]

    _fake = os.environ.get("KA_FAKES1", "0") == "1"
    if _fake:
        shards = [(w1s, i_s1[0], i_dq[0], o_w1t, FS1, D, 0),
                  (w2s, i_s1[1], i_dq[1], o_w2t, FS1, D, 1),
                  (w3s, i_s1[2], i_dq[2], o_w3t, FS3, DFF, 2)]
    _s1off = {}
    _o = 0
    for (_ws, _d1, _d2, _ow, _R, _C, _mi) in shards:
        for _ti, (_r0, _rn) in enumerate(_tile_rows(_R)):
            _s1off[(_mi, _ti)] = _o
            _o += 96
    _shards_p1 = [(w1s, d_s1[0], d_dq[0], o_w1t, FS1, D, 0),
                  (w2s, d_s1[1], d_dq[1], o_w2t, FS1, D, 1),
                  (w3s, d_s1[2], d_dq[2], o_w3t, FS3, DFF, 2)]
    _runp1 = (not _fake) or os.environ.get("KA_RUNP1", "0") == "1"
    p1 = 0
    for (ws, ds1, ddq, owt, R, C, mi) in (_shards_p1 if _runp1 else []):
        G = C // 64
        for ti, (r0, rn) in enumerate(_tile_rows(R)):
            if p1 >= 1:
                W(P.sy, dq, dq.v)
            tok = Lw.issue(P.sy, lambda e, ws=ws, r0=r0, rn=rn, C=C, b=Lw.n % 2: e.dma_start(
                s_wt[b][0:rn, 0:C], ws.ap()[r0:r0 + rn, :]))
            bb = (Lw.n - 1) % 2
            Lw.wait(P.v, tok)
            if p1 == 0:
                W(P.v, sm, SMEAN_DONE)
            if p1 >= 1:
                W(P.v, S1.sems[0], S1.sems[0].v)
            if p1 == 0:
                P.v(lambda e: e.memset(s_s1[:], 1.0))
                P.v(lambda e: e.drain())
            P.v(lambda e, b=bb, rn=rn, C=C, G=G: e.tensor_reduce(
                s_s1[0:rn, 0:G], s_wt[b][0:rn, 0:C].rearrange("p (g k) -> p g k", k=64),
                AX.X, ALU.max, apply_absolute_value=True))
            P.v(lambda e: e.drain())
            _so = _s1off[(mi, ti)]
            P.v(lambda e, rn=rn, so=_so: e.tensor_copy(s_s1all[0:rn, so:so + 96], s_s1[0:rn, 0:96]))
            P.v(lambda e, rn=rn, mi=mi: e.tensor_scalar(
                s_cent[0:rn, 0:96], s_s1[0:rn, 0:96], s_smeanB[0:rn, mi:mi + 1], None,
                ALU.subtract))
            dq.inc()
            P.v(lambda e: e.drain().then_inc(dq.h, 1))
            W(P.c, dq, dq.v)
            S1.issue(P.c, lambda e, ddq=ddq, r0=r0, rn=rn, G=G: e.dma_start(
                ddq.ap()[:, r0 * G:(r0 + rn) * G].rearrange("() (p g) -> p g", g=G),
                s_cent[0:rn, 0:G]))
            p1 += 1

    _runp2 = (not _fake) or os.environ.get("KA_RUNP2", "0") == "1"
    for gi, (ws, ds1, ddq, owt, R, C, mi) in enumerate(_shards_p1 if _runp2 else []):
        nb = R * (C // 64)
        S1.wait_all(P.c)
        tok = Lg.issue(P.c, lambda e, ddq=ddq, nb=nb: e.dma_start(
            s_grp[:], ddq.ap()[:, 0:nb].rearrange("() (sg n) -> sg n", n=256)))
        Lg.wait(P.v, tok)
        if gi >= 1:
            W(P.v, S2.sems[0], S2.sems[0].v)
        for _op in [
            lambda e: e.tensor_reduce(s_gax[:], s_grp[:], AX.X, ALU.max, apply_absolute_value=True),
            lambda e: e.tensor_copy(s_ga32[:], s_gax[:]),
            lambda e: e.tensor_scalar(s_gq[:], s_ga32[:], 2.0, None, ALU.mult),
            lambda e: e.reciprocal(s_gq2[:], s_gq[:]),
            lambda e: e.tensor_tensor(s_gt[:], s_gq[:], s_gq2[:], ALU.mult),
            lambda e: e.tensor_scalar(s_gt[:], s_gt[:], -1.0, 2.0, ALU.mult, ALU.add),
            lambda e: e.tensor_tensor(s_gq2[:], s_gq2[:], s_gt[:], ALU.mult),
            lambda e: e.tensor_scalar(s_qf[:], s_gq2[:], 256.0, None, ALU.mult),
            lambda e: e.tensor_copy(s_qf32[:], s_qf[:]),
            lambda e: e.tensor_scalar(s_prod[:], s_grp[:], s_qf32[:], None, ALU.mult),
            lambda e: e.tensor_scalar(s_qsc[:], s_prod[:], MAGIC, MAGIC, ALU.add, ALU.subtract),
            lambda e: e.tensor_scalar(s_qsc[:], s_qsc[:], -128.0, 127.0, ALU.max, ALU.min),
            lambda e: e.reciprocal(s_gq[:], s_qf32[:]),
            lambda e: e.tensor_tensor(s_gt[:], s_qf32[:], s_gq[:], ALU.mult),
            lambda e: e.tensor_scalar(s_gt[:], s_gt[:], -1.0, 2.0, ALU.mult, ALU.add),
            lambda e: e.tensor_tensor(s_gq[:], s_gq[:], s_gt[:], ALU.mult),
            lambda e: e.tensor_scalar(s_deqg[:], s_qsc[:], s_gq[:], None, ALU.mult),
        ]:
            P.v(_op)
            P.v(lambda e: e.drain())
        P.v(lambda e, mi=mi: e.tensor_scalar(s_deqg[:], s_deqg[:], s_smeanB[0:NSG, mi:mi + 1], None, ALU.add))
        P.v(lambda e: e.drain())
        P.v(lambda e: e.tensor_copy(
            s_dq8[:].rearrange("p (n e) -> p n e", e=8),
            s_deqg[:].rearrange("p n -> p n ()").broadcast_to([NSG, 256, 8])))
        dq.inc()
        P.v(lambda e: e.drain().then_inc(dq.h, 1))
        W(P.c, dq, dq.v)
        gi8 = {0: d_dq8[0], 1: d_dq8[1], 2: d_dq8[2]}[mi]
        S2.issue(P.c, lambda e, gi8=gi8, nb=nb: e.dma_start(
            gi8.ap()[:, 0:nb * 8].rearrange("() (sg n) -> sg n", n=2048), s_dq8[:]))


    def expand64(src_ap_fn, dst, rn, G, dtype_note=""):
        v3 = dst.rearrange("p (g k) -> p g k", k=64)
        P.v(lambda e: e.tensor_copy(v3[0:rn, 0:G, 0:1], src_ap_fn()))
        P.v(lambda e: e.drain())
        w = 1
        while w < 64:
            ww = w
            P.v(lambda e, ww=ww: e.tensor_copy(v3[0:rn, 0:G, ww:2 * ww], v3[0:rn, 0:G, 0:ww]))
            P.v(lambda e: e.drain())
            w *= 2
    tcnt = 0
    wt_tok = []
    p3 = 0
    for (ws, ds1, ddq, owt, R, C, mi) in shards:
        G = C // 64
        for ti, (r0, rn) in enumerate(_tile_rows(R)):
            if p3 == 0:
                S2.wait_all(P.sy)
                W(P.sy, dq, dq.v)
            else:
                W(P.sy, dq, dq.v)
            Lw.issue(P.sy, lambda e, ws=ws, r0=r0, rn=rn, C=C, b=Lw.n % 2: e.dma_start(
                s_wt[b][0:rn, 0:C], ws.ap()[r0:r0 + rn, :]))
            bb = (Lw.n - 1) % 2
            dd8 = {0: d_dq8[0], 1: d_dq8[1], 2: d_dq8[2]}[mi]
            L3.issue(P.sy, lambda e, dd8=dd8, r0=r0, rn=rn, G=G: e.dma_start(
                s_dqs8[0:rn, 0:G * 8 + 128],
                _AP(tensor=dd8.ap().tensor, offset=r0 * G * 8, ap=[[G * 8, rn], [1, G * 8 + 128]])))
            Lw.wait_all(P.v)
            L3.wait_all(P.v)
            if p3 >= 1:
                W(P.v, tr, tr.v)
            _so = _s1off[(mi, ti)]
            for _mk in [
                lambda rn, G, so=_so: (lambda e: e.tensor_copy(s_s1[0:rn, 0:96], s_s1all[0:rn, so:so + 96])),
                lambda rn, G: (lambda e: e.reciprocal(s_r1[0:rn, 0:96], s_s1[0:rn, 0:96])),
                lambda rn, G: (lambda e: e.tensor_tensor(s_rt[0:rn, 0:96], s_s1[0:rn, 0:96], s_r1[0:rn, 0:96], ALU.mult)),
                lambda rn, G: (lambda e: e.tensor_scalar(s_rt[0:rn, 0:96], s_rt[0:rn, 0:96], -1.0, 2.0, ALU.mult, ALU.add)),
                lambda rn, G: (lambda e: e.tensor_tensor(s_r1[0:rn, 0:96], s_r1[0:rn, 0:96], s_rt[0:rn, 0:96], ALU.mult)),
            ]:
                P.v(_mk(rn, G))
                P.v(lambda e: e.drain())
            P.v(lambda e, b=bb, rn=rn, C=C, G=G: e.tensor_tensor(
                s_q[0:rn, 0:C].rearrange("p (g k) -> p g k", k=64),
                s_wt[b][0:rn, 0:C].rearrange("p (g k) -> p g k", k=64),
                s_r1[0:rn, 0:G].rearrange("p g -> p g ()").broadcast_to([rn, G, 64]),
                ALU.mult))
            P.v(lambda e: e.drain())
            P.v(lambda e, rn=rn, C=C: e.memset(s_lvl[0:rn, 0:C], -1.0))
            for j in range(15):
                P.v(lambda e, j=j, rn=rn, C=C: e.tensor_scalar(
                    s_term[0:rn, 0:C], s_q[0:rn, 0:C], NF4_THR[j], NF4_STEP[j], ALU.is_gt, ALU.mult))
                P.v(lambda e, rn=rn, C=C: e.tensor_tensor(s_lvl[0:rn, 0:C], s_lvl[0:rn, 0:C], s_term[0:rn, 0:C], ALU.add))
            P.v(lambda e, rn=rn, C=C, G=G: e.tensor_tensor(
                s_lvl[0:rn, 0:C].rearrange("p (g k) -> p g k", k=64),
                s_lvl[0:rn, 0:C].rearrange("p (g k) -> p g k", k=64),
                s_dqs8[0:rn, 0:G * 8].rearrange("p (g e) -> p g e", e=8)[:, :, 0:1].broadcast_to([rn, G, 64]),
                ALU.mult))
            P.v(lambda e: e.drain())
            dq.inc()
            P.v(lambda e: e.drain().then_inc(dq.h, 1))
            if mi == 1 and ti == 0 and os.environ.get("KA_DL", "0") == "1":
                W(P.g, dq, dq.v)
                dbgs.inc(16)
                P.g(lambda e: e.dma_start(o_dlvl.ap(), s_lvl[0:128, 0:D]).then_inc(dbgs.h, 16))
                W(P.t, dbgs, dbgs.v)
            W(P.t, dq, dq.v)
            _notr = os.environ.get("KA_NOTR", "0") == "1"
            for cix in ([] if _notr else range(C // 128)):
                if tcnt >= 1:
                    W(P.t, te, te.v)
                tr.inc()
                P.t(lambda e, cix=cix, rn=rn: e.transpose(
                    p_tr[0:128, 0:rn], s_lvl[0:rn, cix * 128:(cix + 1) * 128],
                    s_ident[0:rn, 0:rn]).then_inc(tr.h, 1))
                W(P.c, tr, tr.v)
                if tcnt >= 2:
                    Swt.wait(P.c, wt_tok[tcnt - 2])
                te.inc()
                P.c(lambda e, b=tcnt % 2, rn=rn: e.activation(
                    s_wtr[b][0:128, 0:rn], p_tr[0:128, 0:rn], AF.Copy).then_inc(te.h, 1))
                wt_tok.append(Swt.issue(P.c, lambda e, owt=owt, cix=cix, r0=r0, rn=rn, b=tcnt % 2: e.dma_start(
                    owt.ap()[cix * 128:(cix + 1) * 128, r0:r0 + rn],
                    s_wtr[b][0:128, 0:rn])))
                tcnt += 1
            p3 += 1

    if _PH < 3:
        for stm in (S1, S2, Swt, Sx2, Sspl, Szt, Ssn):
            stm.wait_all(P.g)
        P.finalize()
        return nc
    tr_after_tile = []
    x2_tok = []
    x2st_n = 0
    W(P.sy, cst, cst.v)
    for tb in range(NTB):
        if tb >= 2:
            W(P.sy, lnv, 2 * (tb - 2) + 1)
        tokx = Lx.issue(P.sy, lambda e, tb=tb: e.dma_start(
            s_x[tb % 2][:], xA.ap()[tb * 128:(tb + 1) * 128, :]))
        Lx.wait(P.v, tokx)
        if tb >= 1:
            W(P.v, tr, tr_after_tile[tb - 1])
        xt = s_x[tb % 2]
        Lx.wait(P.c, tokx)
        P.c(lambda e, xt=xt: e.activation(s_term[0:128, 0:D], xt[:], AF.Copy, accum_out=s_ss[:]))
        lns.inc()
        P.c(lambda e: e.activation(s_mu[:], s_ss[:], AF.Copy, scale=float(np.float32(1.0 / D))).then_inc(lns.h, 1))
        W(P.v, lns, lns.v)
        lnv.inc()
        P.v(lambda e, xt=xt: e.tensor_scalar(s_xc[:], xt[:], s_mu[:], None, ALU.subtract).then_inc(lnv.h, 1))
        W(P.c, lnv, lnv.v)
        P.c(lambda e: e.activation(s_term[0:128, 0:D], s_xc[:], AF.Square, accum_out=s_ss[:]))
        P.c(lambda e: e.activation(s_lnt[:], s_ss[:], AF.Ln, bias=s_eps[:], scale=float(np.float32(1.0 / D))))
        lns.inc()
        P.c(lambda e: e.activation(s_rstd[:], s_lnt[:], AF.Exp, scale=-0.5).then_inc(lns.h, 1))
        W(P.v, lns, lns.v)
        lnv.inc()
        P.v(lambda e: e.tensor_scalar(s_x2t[:], s_xc[:], s_rstd[:], None, ALU.mult).then_inc(lnv.h, 1))
        W(P.t, lnv, lnv.v)
        for kc in range(16):
            W(P.t, te, te.v)
            tr.inc()
            P.t(lambda e, kc=kc: e.transpose(p_tr[:], s_x2t[:, kc * 128:(kc + 1) * 128], s_ident[:]).then_inc(tr.h, 1))
            W(P.c, tr, tr.v)
            if x2st_n >= 2:
                Sx2.wait(P.c, x2_tok[x2st_n - 2])
            te.inc()
            P.c(lambda e, b=x2st_n % 2: e.activation(s_x2st[b][:], p_tr[:], AF.Copy).then_inc(te.h, 1))
            W(P.g, te, te.v)
            x2_tok.append(Sx2.issue(P.g, lambda e, kc=kc, tb=tb, b=x2st_n % 2: e.dma_start(
                o_x2.ap()[kc * 128:(kc + 1) * 128, tb * 128:(tb + 1) * 128], s_x2st[b][:])))
            x2st_n += 1
        tr_after_tile.append(tr.v)

    if _PH < 4:
        for stm in (S1, S2, Swt, Sx2, Sspl, Szt, Ssn):
            stm.wait_all(P.g)
        P.finalize()
        return nc
    x2ld_n = 0
    x2pj_at = []
    zst_n = 0
    zt_tok = []
    spl_tok = []
    for which, WT, brow, rope, o_dst in (("k", WkT, s_bk, True, o_ktm),
                                         ("v", WvT, s_bv, False, o_vtm)):
        for nd in range(4):
            if wcast.v >= 1:
                W(P.sy, wcast, wcast.v)
            tokf = Lf.issue(P.sy, lambda e, WT=WT, nd=nd: e.dma_start(
                s_wf32[:].rearrange("p (kc c) -> p kc c", c=512),
                WT.ap()[:, nd * 512:(nd + 1) * 512].rearrange("(kc p) c -> p kc c", p=128)))
            Lf.wait(P.v, tokf)
            if wcast.v > 0:
                W(P.v, pj, pj.v)
            P.v(lambda e: e.tensor_copy(s_wb[:], s_wf32[:]))
            wcast.inc()
            P.v(lambda e: e.drain().then_inc(wcast.h, 1))
            W(P.t, wcast, wcast.v)
            for tb in range(NTB):
                bank = tb % 2
                if x2ld_n == 0:
                    Sx2.wait_all(P.sy)
                if x2ld_n >= 2:
                    W(P.sy, pj, x2pj_at[x2ld_n - 2])
                tok2 = L2.issue(P.sy, lambda e, tb=tb, b=x2ld_n % 2: e.dma_start(
                    s_x2b[b][:].rearrange("p (kc c) -> p kc c", c=128),
                    o_x2.ap()[:, tb * 128:(tb + 1) * 128].rearrange("(kc p) c -> p kc c", p=128)))
                L2.wait(P.t, tok2)
                W(P.t, ev, max(0, ev.v - 1))
                for kc in range(16):
                    P.t(lambda e, kc=kc, bank=bank, b=x2ld_n % 2: e.matmul(
                        p_mm[bank][:], s_x2b[b][:, kc * 128:(kc + 1) * 128],
                        s_wb[:, kc * 512:(kc + 1) * 512], start=(kc == 0), stop=False))
                pj.inc()
                P.t(lambda e, bank=bank, brow=brow, nd=nd: e.matmul(
                    p_mm[bank][:], s_ones_row[:], brow[0:1, nd * 512:(nd + 1) * 512],
                    start=False, stop=True).then_inc(pj.h, 1))
                x2pj_at.append(pj.v)
                x2ld_n += 1
                W(P.c, pj, pj.v)
                if len(spl_tok) >= 2:
                    Sspl.wait(P.c, spl_tok[-2])
                ev.inc()
                P.c(lambda e, bank=bank: e.activation(
                    s_kblk[bank][:], p_mm[bank][:], AF.Copy).then_inc(ev.h, 1))
                if rope:
                    W(P.v, ev, ev.v)
                    kb = s_kblk[bank]
                    cb = s_cos[:, tb * 64:(tb + 1) * 64].rearrange("p f -> p () f").broadcast_to([128, 4, 64])
                    sbb = s_sin[:, tb * 64:(tb + 1) * 64].rearrange("p f -> p () f").broadcast_to([128, 4, 64])
                    ev_v = kb[:].rearrange("p (h two f) -> p h two f", two=2, f=64)[:, :, 0, :]
                    od_v = kb[:].rearrange("p (h two f) -> p h two f", two=2, f=64)[:, :, 1, :]
                    r1v = s_ro1[:].rearrange("p (h f) -> p h f", f=64)
                    r2v = s_ro2[:].rearrange("p (h f) -> p h f", f=64)
                    P.v(lambda e, a=r1v, b=ev_v, c=sbb: e.tensor_tensor(a, b, c, ALU.mult))
                    P.v(lambda e, a=r2v, b=od_v, c=sbb: e.tensor_tensor(a, b, c, ALU.mult))
                    P.v(lambda e: e.drain())
                    P.v(lambda e, a=ev_v, c=cb: e.tensor_tensor(a, a, c, ALU.mult))
                    P.v(lambda e: e.drain())
                    P.v(lambda e, a=ev_v, b=r2v: e.tensor_tensor(a, a, b, ALU.subtract))
                    P.v(lambda e, a=od_v, c=cb: e.tensor_tensor(a, a, c, ALU.mult))
                    P.v(lambda e: e.drain())
                    P.v(lambda e, a=od_v, b=r1v: e.tensor_tensor(a, a, b, ALU.add))
                    P.v(lambda e: e.drain())
                    P.v(lambda e, kb=kb: e.tensor_scalar(s_el1[:], kb[:], 0.0, None, ALU.min))
                    rp.inc()
                    P.v(lambda e: e.drain().then_inc(rp.h, 1))
                    W(P.c, rp, rp.v)
                    lns.inc()
                    P.c(lambda e: e.activation(s_el2[:], s_el1[:], AF.Exp).then_inc(lns.h, 1))
                    W(P.v, lns, lns.v)
                    P.v(lambda e, kb=kb: e.tensor_scalar(s_el1[:], kb[:], 0.0, None, ALU.max))
                    P.v(lambda e: e.drain())
                    P.v(lambda e, kb=kb: e.tensor_tensor(kb[:], s_el1[:], s_el2[:], ALU.add))
                    rp.inc()
                    P.v(lambda e: e.drain().then_inc(rp.h, 1))
                    if tb % 2 == 0 and zs.v > 0:
                        W(P.t, zs, zs.v)
                    W(P.t, rp, rp.v)
                    pj.inc()
                    P.t(lambda e, bank=bank, tb=tb: e.matmul(
                        p_z[:], s_ones_col[:], s_kblk[bank][:],
                        start=(tb == 0), stop=(tb % 2 == 1),
                        skip_group_check=(tb >= 2)).then_inc(pj.h, 1))
                    if tb % 2 == 1:
                        sup = tb // 2
                        W(P.c, pj, pj.v)
                        if zst_n >= 2:
                            Szt.wait(P.c, zt_tok[zst_n - 2])
                        zs.inc()
                        P.c(lambda e, b=zst_n % 2: e.activation(
                            s_zrow[0:1, b * 512:(b + 1) * 512], p_z[:], AF.Copy).then_inc(zs.h, 1))
                        W(P.g, zs, zs.v)
                        zt_tok.append(Szt.issue(P.g, lambda e, sup=sup, nd=nd, b=zst_n % 2: e.dma_start(
                            o_zsnap.ap()[:, sup * D + nd * 512: sup * D + (nd + 1) * 512],
                            s_zrow[0:1, b * 512:(b + 1) * 512])))
                        zst_n += 1
                    W(P.g, rp, rp.v)
                else:
                    W(P.g, ev, ev.v)
                spl_tok.append(Sspl.issue(P.g, lambda e, o_dst=o_dst, tb=tb, nd=nd, bank=bank: e.dma_start(
                    o_dst.ap()[tb * 128:(tb + 1) * 128, nd * 512:(nd + 1) * 512],
                    s_kblk[bank][:])))

    if _PH < 5:
        for stm in (S1, S2, Swt, Sx2, Sspl, Szt, Ssn):
            stm.wait_all(P.g)
        P.finalize()
        return nc
    Sspl.wait_all(P.sy)
    sn_tok = []
    _NHG = int(os.environ.get("KA_HG", "4"))
    _NS5 = int(os.environ.get("KA_NS", str(NSUP)))
    for hg in range(_NHG):
        heads = list(range(hg * 4, hg * 4 + 4))
        for s in range(_NS5):
            pair = (s % 2) * 2
            if hg > 0 and s < 2:
                W(P.sy, sn, hg * 32)
            elif s >= 2:
                W(P.sy, sn, hg * 32 + (s - 1) * 4)
            kv_toks = [
                Lkv.issue(P.sy, lambda e, s=s, pair=pair: e.dma_start(
                    s_kv[pair][:], o_ktm.ap()[s * 256:s * 256 + 128, :])),
                Lkv.issue(P.sy, lambda e, s=s, pair=pair: e.dma_start(
                    s_kv[pair + 1][:], o_ktm.ap()[s * 256 + 128:s * 256 + 256, :])),
            ]
            kv_toks.append(Lkv.issue(P.sy, lambda e, s=s, pair=pair: e.dma_start(
                s_vv[pair][:], o_vtm.ap()[s * 256:s * 256 + 128, :])))
            kv_toks.append(Lkv.issue(P.sy, lambda e, s=s, pair=pair: e.dma_start(
                s_vv[pair + 1][:], o_vtm.ap()[s * 256 + 128:s * 256 + 256, :])))
            for tk in kv_toks:
                Lkv.wait(P.t, tk)
            for hi, h in enumerate(heads):
                if s == 0 and hg > 0:
                    W(P.t, sne, (hg - 1) * 32 + 29 + hi)
                elif s > 0:
                    W(P.t, sne, sne.v - 3)
                P.t(lambda e, h=h, hi=hi, pair=pair, s=s: e.matmul(
                    p_s[hi][:], s_kv[pair][:, h * 128:(h + 1) * 128],
                    s_vv[pair][:, h * 128:(h + 1) * 128],
                    start=(s == 0), stop=False, skip_group_check=(s > 0)))
                sn.inc()
                P.t(lambda e, h=h, hi=hi, pair=pair, s=s: e.matmul(
                    p_s[hi][:], s_kv[pair + 1][:, h * 128:(h + 1) * 128],
                    s_vv[pair + 1][:, h * 128:(h + 1) * 128],
                    start=False, stop=True, skip_group_check=(s > 0)).then_inc(sn.h, 1))
                _M5 = os.environ.get("KA_PA5_MODE", "full")
                if _M5 == "mm":
                    sne.inc()
                    P.c(lambda e: e.engine_nop().then_inc(sne.h, 1))
                    continue
                _EVE = os.environ.get("KA_EVENG", "c")
                PE_ = P.v if _EVE == "v" else P.c
                W(PE_, sn, sn.v)
                if sne.v >= 2:
                    Ssn.wait(PE_, sn_tok[sne.v - 2])
                sne.inc()
                if _EVE == "v":
                    P.v(lambda e, hi=hi, b=(sne.v - 1) % 2: e.tensor_copy(s_x2st[b][:], p_s[hi][:]).then_inc(sne.h, 1))
                else:
                    P.c(lambda e, hi=hi, b=(sne.v - 1) % 2: e.activation(s_x2st[b][:], p_s[hi][:], AF.Copy).then_inc(sne.h, 1))
                if _M5 == "ev":
                    sn_tok.append((Ssn.sems[0], 0))
                    continue
                W(P.g, sne, sne.v)
                sn_tok.append(Ssn.issue(P.g, lambda e, h=h, s=s, b=(sne.v - 1) % 2: e.dma_start(
                    o_ssnap.ap()[(h * NSUP + s) * DK:(h * NSUP + s + 1) * DK, :],
                    s_x2st[b][:])))

    for stm in (Swt, Sx2, Sspl, Szt, Ssn, S1, S2):
        stm.wait_all(P.g)

    P.finalize()
    return nc


def _prep_inputs_A(inputs):
    x = np.asarray(inputs["L0_attn_x"])[0]
    Wk = np.asarray(inputs["L0_attn_Wk"])
    Wv = np.asarray(inputs["L0_attn_Wv"])
    bk = np.asarray(inputs["L0_attn_bk"])
    bv = np.asarray(inputs["L0_attn_bv"])
    w1 = np.asarray(inputs["L0_ffn_weight1"])
    w2 = np.asarray(inputs["L0_ffn_weight2"])
    w3 = np.asarray(inputs["L0_ffn_weight3"])
    WkT = np.ascontiguousarray(Wk.T[:, PERM], np.float32)
    WvT = np.ascontiguousarray(Wv.T, np.float32)
    bkP = np.ascontiguousarray(bk[PERM], np.float32).reshape(1, D)
    bvv = np.ascontiguousarray(bv, np.float32).reshape(1, D)
    ident = np.eye(128, dtype=bf16)
    maps = []
    for c in range(NCORES):
        cos, sin = _rope_tables(c)
        maps.append({
            "xA": np.ascontiguousarray(x[c * TPC:(c + 1) * TPC], np.float32),
            "WkT": WkT, "WvT": WvT, "bk": bkP, "bv": bvv,
            "cosA": cos, "sinA": sin, "ident": ident,
            "w1f": w1, "w2f": w2, "w3f": w3,
            "w1s": np.ascontiguousarray(w1[c * FS1:(c + 1) * FS1]),
            "w2s": np.ascontiguousarray(w2[c * FS1:(c + 1) * FS1]),
            "w3s": np.ascontiguousarray(w3[c * FS3:(c + 1) * FS3]),
        })
    return maps


def _rope_tables(core):
    pos = np.arange(TPC, dtype=np.float64) + core * TPC
    inv = 1.0 / (10000.0 ** (np.arange(0, DK, 2, dtype=np.float64) / DK))
    ang = pos[:, None] * inv[None, :]
    return (np.ascontiguousarray(np.cos(ang), np.float32),
            np.ascontiguousarray(np.sin(ang), np.float32))


def build_B():
    nc = bass.Bass("TRN2", target_bir_lowering=False, detect_race_conditions=False)
    f = nc.dram_tensor

    xB = f("xB", [TPC, D], F32, kind="ExternalInput")
    x2r = f("x2r", [D, TPC], BF16, kind="ExternalInput")
    ktm = f("ktm", [TPC, D], BF16, kind="ExternalInput")
    vtm = f("vtm", [TPC, D], BF16, kind="ExternalInput")
    snapO = f("snapO", [H * NSUP * DK, DK], BF16, kind="ExternalInput")
    sprev = f("sprev", [7 * H * DK, DK], BF16, kind="ExternalInput")
    zsnC = f("zsnC", [128, NSUP * 16], F32, kind="ExternalInput")
    zprC = f("zprC", [128, 32 * 7], F32, kind="ExternalInput")
    WqT = f("WqT", [D, D], F32, kind="ExternalInput")
    WgT = f("WgT", [D, D], F32, kind="ExternalInput")
    WoT = f("WoT", [D, D], F32, kind="ExternalInput")
    bq = f("bq", [1, D], F32, kind="ExternalInput")
    bg = f("bg", [1, D], F32, kind="ExternalInput")
    bo = f("bo", [1, D], F32, kind="ExternalInput")
    w1T = f("w1T", [D, DFF], BF16, kind="ExternalInput")
    w2T = f("w2T", [D, DFF], BF16, kind="ExternalInput")
    w3T = f("w3T", [DFF, D], BF16, kind="ExternalInput")
    cosA = f("cosA", [TPC, 64], F32, kind="ExternalInput")
    sinA = f("sinA", [TPC, 64], F32, kind="ExternalInput")
    ident = f("ident", [128, 128], BF16, kind="ExternalInput")
    m0 = f("m0", [128, 256], BF16, kind="ExternalInput")
    m1 = f("m1", [128, 256], BF16, kind="ExternalInput")

    o_out = f("o_out", [TPC, D], F32, kind="ExternalOutput")

    _dbg = {}
    if os.environ.get("KB_DBG", "0") == "1":
        _dbg = {"kind": "ExternalOutput"}
    d_qfm = f("d_qfm", [D, TPC], BF16, **_dbg)
    d_gfm = f("d_gfm", [D, TPC], BF16, **_dbg)
    d_ap = f("d_ap", [TPC, D], F32, **_dbg)
    d_x1 = f("d_x1", [TPC, D], F32, **_dbg)
    d_x2p = f("d_x2p", [D, TPC], BF16, **_dbg)
    d_h = f("d_h", [DFF, TPC], BF16, **_dbg)
    d_y = f("d_y", [TPC, D], BF16, **_dbg)

    sb = nc.alloc_sbuf_tensor
    park = [sb(f"park{i}", [128, TPC], BF16) for i in range(16)]
    KV = sb("KV", [128, 16384], BF16)
    A3 = sb("A3", [128, 8192], F32)
    WB = sb("WB", [128, 8192], BF16)
    s_Sst = sb("s_Sst", [128, H * 128], F32)
    s_kfm = sb("s_kfm", [128, 32 * 128], BF16)
    s_cos = sb("s_cos", [128, NTB * 64], F32)
    s_sin = sb("s_sin", [128, NTB * 64], F32)
    s_ktn = [KV[0:128, i * 2048:(i + 1) * 2048] for i in range(4)]
    s_vtn = [KV[0:128, 8192 + i * 2048:8192 + (i + 1) * 2048] for i in range(4)]
    s_spr = [KV[0:128, b * 2048:(b + 1) * 2048] for b in range(2)]
    s_w1c = [KV[0:128, b * 4096:b * 4096 + 2048] for b in range(2)]
    s_w2c = [KV[0:128, b * 4096 + 2048:b * 4096 + 4096] for b in range(2)]
    s_hst = [KV[0:128, 0:5632], KV[0:128, 8192:8192 + 5632]]
    s_wf32 = A3[0:128, 0:8192]
    s_wgf = A3[0:128, 0:2048]
    s_ap2 = [A3[0:128, b * 2048:(b + 1) * 2048] for b in range(2)]
    s_x1t = [A3[0:128, 4096 + b * 2048:4096 + (b + 1) * 2048] for b in range(2)]
    s_sil = [A3[0:128, b * 512:(b + 1) * 512] for b in range(2)]
    s_x1b = [A3[0:128, 4096 + b * 2048:4096 + (b + 1) * 2048] for b in range(2)]
    s_yc = A3[0:128, 2048:4096]
    s_ost = A3[0:128, 0:2048]
    s_wb = WB[0:128, 0:8192]
    s_wgc = WB[0:128, 0:2048]
    s_junk = WB[0:128, 0:2048]
    s_x2pt = WB[0:128, 2048:4096]
    s_yt = [WB[0:128, 4096 + b * 2048:4096 + (b + 1) * 2048] for b in range(2)]
    s_apt = [s_Sst[0:128, b * 512:(b + 1) * 512] for b in range(2)]
    s_ident = sb("s_ident", [128, 128], BF16)
    s_m0 = sb("s_m0", [128, 256], BF16)
    s_m1 = sb("s_m1", [128, 256], BF16)
    s_ones_row = sb("s_ones_row", [1, 128], BF16)
    s_ones_col = sb("s_ones_col", [128, 1], BF16)
    s_eps = sb("s_eps", [128, 1], F32)
    s_eps6 = sb("s_eps6", [1, 1], F32)
    s_bq = sb("s_bq", [1, D], BF16)
    s_bo = sb("s_bo", [1, D], BF16)
    s_bgr = sb("s_bgr", [1, D], BF16)
    s_ones5 = sb("s_ones5", [1, 512], BF16)
    s_b32 = A3[0:1, 0:D]
    s_zpr = sb("s_zpr", [128, 32 * 7], F32)
    s_zsn = sb("s_zsn", [128, NSUP * 16], F32)
    s_zst = sb("s_zst", [128, 32], F32)
    s_zu = [sb(f"s_zu{b}", [128, 2], BF16) for b in range(2)]
    s_kblk = [sb(f"s_kblk{i}", [128, 512], BF16) for i in range(2)]
    s_ro1 = sb("s_ro1", [128, 256], BF16)
    s_ro2 = sb("s_ro2", [128, 256], BF16)
    s_el1 = sb("s_el1", [128, 512], BF16)
    s_el2 = sb("s_el2", [128, 512], BF16)
    s_qst = [sb(f"s_qst{i}", [128, 512], BF16) for i in range(2)]
    s_gst = s_qst
    s_qs = [sb(f"s_qs{i}", [128, 256], BF16) for i in range(2)]
    s_gs = [sb(f"s_gs{i}", [128, 256], BF16) for i in range(2)]
    s_gs32 = [sb(f"s_gs32{i}", [128, 256], F32) for i in range(2)]
    s_sn = [sb(f"s_sn{i}", [128, 128], BF16) for i in range(2)]
    s_su = [sb(f"s_su{i}", [128, 128], BF16) for i in range(2)]
    s_at = [sb(f"s_at{i}", [128, 512], BF16) for i in range(2)]
    s_den = [sb(f"s_den{i}", [1, 256], F32) for i in range(2)]
    s_dent = [sb(f"s_dent{i}", [1, 256], F32) for i in range(2)]
    s_recb = [sb(f"s_recb{i}", [1, 256], BF16) for i in range(2)]
    s_rb = [sb(f"s_rb{i}", [128, 256], F32) for i in range(2)]
    s_t1 = [sb(f"s_t1{i}", [128, 256], F32) for i in range(2)]
    s_hfm = s_kblk
    s_yst = s_qst
    s_x2st = [s_ro1[0:128, 0:128], s_ro1[0:128, 128:256]]
    s_sum = sb("s_sum", [128, 1], F32)
    s_sq = sb("s_sq", [128, 1], F32)
    s_mu = sb("s_mu", [128, 1], F32)
    s_lnt = sb("s_lnt", [128, 1], F32)
    s_rstd = sb("s_rstd", [128, 1], F32)

    ps = nc.alloc_psum_tensor
    b = [ps(f"bk{i}", [128, 512], F32) for i in range(6)]
    p_tr = ps("p_tr", [128, 128], BF16)
    p_sm = ps("p_sm", [1, 512], F32)

    P = Prog(nc)
    pe = Sem(nc, "pe")
    ac = Sem(nc, "ac")
    dv = Sem(nc, "dv")

    Lc = DStream(nc, "Lc", 1)
    Lpk = DStream(nc, "Lpk", 1)
    Lw = DStream(nc, "Lw", 1)
    Lkv = DStream(nc, "Lkv", 8)
    Lq = DStream(nc, "Lq", 2)
    Lg = DStream(nc, "Lg", 2)
    Lsn = DStream(nc, "Lsn", 2)
    Lap = DStream(nc, "Lap", 2)
    Lx = DStream(nc, "Lx", 2)
    Lxp = DStream(nc, "Lxp", 1)
    L12 = DStream(nc, "L12", 2)
    L3s = DStream(nc, "L3s", 1)
    Lh = DStream(nc, "Lh", 2)
    Ly = DStream(nc, "Ly", 2)
    Lx1 = DStream(nc, "Lx1", 2)
    Sq = DStream(nc, "Sq", 2)
    Sg = DStream(nc, "Sg", 2)
    Sap = DStream(nc, "Sap", 2)
    Sx1 = DStream(nc, "Sx1", 2)
    Sxp = DStream(nc, "Sxp", 2)
    Sh = DStream(nc, "Sh", 2)
    Sy = DStream(nc, "Sy", 2)
    Sout = DStream(nc, "Sout", 2)

    def W(P_eng, sem, val):
        P_eng(lambda e, s=sem.h, w=val: e.wait_ge(s, w))

    Lc.issue(P.sy, lambda e: e.dma_start(s_ident[:], ident.ap()))
    Lc.issue(P.sy, lambda e: e.dma_start(s_m0[:], m0.ap()))
    Lc.issue(P.sy, lambda e: e.dma_start(s_m1[:], m1.ap()))
    Lc.issue(P.sy, lambda e: e.dma_start(s_cos[:].rearrange("p (a f) -> p a f", f=64),
                                         cosA.ap().rearrange("(a p) f -> p a f", p=128)))
    Lc.issue(P.sy, lambda e: e.dma_start(s_sin[:].rearrange("p (a f) -> p a f", f=64),
                                         sinA.ap().rearrange("(a p) f -> p a f", p=128)))
    Lc.issue(P.sy, lambda e: e.dma_start(s_zpr[:], zprC.ap()))
    Lc.issue(P.sy, lambda e: e.dma_start(s_zsn[:], zsnC.ap()))
    Lc.issue(P.sy, lambda e: e.dma_start(s_b32[:], bq.ap()))
    Lc.wait_all(P.v)
    P.v(lambda e: e.memset(s_ones_row[:], 1.0))
    P.v(lambda e: e.memset(s_ones_col[:], 1.0))
    P.v(lambda e: e.memset(s_eps[:], float(np.float32(EPS_LN))))
    P.v(lambda e: e.memset(s_eps6[:], 1e-6))
    P.v(lambda e: e.tensor_copy(s_bq[:], s_b32[:]))
    P.v(lambda e: e.drain())
    P.v(lambda e: e.tensor_reduce(s_zst[:], s_zpr[:].rearrange("p (h r) -> p h r", r=7),
                                  AX.X, ALU.add))
    dv.inc()
    P.v(lambda e, w=dv.v: e.drain().then_inc(dv.h, 1))
    _dv_bq = dv.v
    W(P.sy, dv, _dv_bq)
    Lc.issue(P.sy, lambda e: e.dma_start(s_b32[:], bg.ap()))
    Lc.wait_all(P.v)
    W(P.v, dv, _dv_bq)
    P.v(lambda e: e.memset(s_ones5[:], 1.0))
    P.v(lambda e: e.tensor_copy(s_bgr[:], s_b32[:]))
    dv.inc()
    P.v(lambda e: e.drain().then_inc(dv.h, 1))
    _dv_bg = dv.v
    W(P.sy, dv, _dv_bg)
    Lc.issue(P.sy, lambda e: e.dma_start(s_b32[:], bo.ap()))
    Lc.wait_all(P.v)
    W(P.v, dv, _dv_bg)
    P.v(lambda e: e.tensor_copy(s_bo[:], s_b32[:]))
    P.v(lambda e: e.memset(s_Sst[:], 0.0))
    dv.inc()
    P.v(lambda e: e.drain().then_inc(dv.h, 1))
    _dv_b0c = dv.v
    spr_dv = []
    for r in range(7):
        if r >= 2:
            W(P.sy, dv, spr_dv[r - 2])
        tok = Lsn.issue(P.sy, lambda e, r=r, bb=r % 2: e.dma_start(
            s_spr[bb].rearrange("p (h c) -> p h c", c=128),
            sprev.ap()[r * H * DK:(r + 1) * H * DK, :].rearrange("(h p) c -> p h c", p=128)))
        Lsn.wait(P.v, tok)
        P.v(lambda e, bb=r % 2: e.tensor_tensor(s_Sst[:], s_Sst[:], s_spr[bb], ALU.add))
        dv.inc()
        P.v(lambda e: e.drain().then_inc(dv.h, 1))
        spr_dv.append(dv.v)
    B0_DONE = dv.v

    for kc in range(16):
        Lpk.issue(P.sy, lambda e, kc=kc: e.dma_start(
            park[kc][:], x2r.ap()[kc * 128:(kc + 1) * 128, :]))
    Lpk.wait_all(P.t)

    ev_at, tr_at, sq_tok = [], [], []
    qwc_at = []
    for nd in range(4):
        if nd > 0:
            W(P.sy, dv, qwc_at[-1])
        else:
            W(P.sy, dv, _dv_b0c)
        tokf = Lw.issue(P.sy, lambda e, nd=nd: e.dma_start(
            s_wf32.rearrange("p (kc c) -> p kc c", c=512),
            WqT.ap()[:, nd * 512:(nd + 1) * 512].rearrange("(kc p) c -> p kc c", p=128)))
        Lw.wait(P.v, tokf)
        if nd > 0:
            W(P.v, pe, pe.v)
        P.v(lambda e: e.tensor_copy(s_wb[:], s_wf32[:]))
        dv.inc()
        P.v(lambda e: e.drain().then_inc(dv.h, 1))
        qwc_at.append(dv.v)
        W(P.t, dv, dv.v)
        for tb in range(NTB):
            n = nd * NTB + tb
            bank = tb % 2
            if n >= 2:
                W(P.t, ac, ev_at[n - 2])
            for kc in range(16):
                P.t(lambda e, kc=kc, tb=tb, bank=bank: e.matmul(
                    b[4 + bank][:], park[kc][:, tb * 128:(tb + 1) * 128],
                    s_wb[:, kc * 512:(kc + 1) * 512], start=(kc == 0), stop=False))
            pe.inc()
            P.t(lambda e, bank=bank, nd=nd: e.matmul(
                b[4 + bank][:], s_ones_row[:], s_bq[0:1, nd * 512:(nd + 1) * 512],
                start=False, stop=True).then_inc(pe.h, 1))
            W(P.c, pe, pe.v)
            if n >= 2:
                W(P.c, pe, tr_at[n - 2])
            ac.inc()
            P.c(lambda e, bank=bank: e.activation(
                s_kblk[bank][:], b[4 + bank][:], AF.Copy).then_inc(ac.h, 1))
            ev_at.append(ac.v)
            W(P.v, ac, ac.v)
            kb = s_kblk[bank]
            cb = s_cos[:, tb * 64:(tb + 1) * 64].rearrange("p f -> p () f").broadcast_to([128, 4, 64])
            sbb = s_sin[:, tb * 64:(tb + 1) * 64].rearrange("p f -> p () f").broadcast_to([128, 4, 64])
            ev_v = kb[:].rearrange("p (h two f) -> p h two f", two=2, f=64)[:, :, 0, :]
            od_v = kb[:].rearrange("p (h two f) -> p h two f", two=2, f=64)[:, :, 1, :]
            r1v = s_ro1[:].rearrange("p (h f) -> p h f", f=64)
            r2v = s_ro2[:].rearrange("p (h f) -> p h f", f=64)
            P.v(lambda e, a=r1v, x=ev_v, c=sbb: e.tensor_tensor(a, x, c, ALU.mult))
            P.v(lambda e, a=r2v, x=od_v, c=sbb: e.tensor_tensor(a, x, c, ALU.mult))
            P.v(lambda e: e.drain())
            P.v(lambda e, x=ev_v, c=cb: e.tensor_tensor(x, x, c, ALU.mult))
            P.v(lambda e: e.drain())
            P.v(lambda e, x=ev_v, a=r2v: e.tensor_tensor(x, x, a, ALU.subtract))
            P.v(lambda e, x=od_v, c=cb: e.tensor_tensor(x, x, c, ALU.mult))
            P.v(lambda e: e.drain())
            P.v(lambda e, x=od_v, a=r1v: e.tensor_tensor(x, x, a, ALU.add))
            P.v(lambda e: e.drain())
            P.v(lambda e, kb=kb: e.tensor_scalar(s_el1[:], kb[:], 0.0, None, ALU.min))
            dv.inc()
            P.v(lambda e: e.drain().then_inc(dv.h, 1))
            W(P.c, dv, dv.v)
            ac.inc()
            P.c(lambda e: e.activation(s_el2[:], s_el1[:], AF.Exp).then_inc(ac.h, 1))
            W(P.v, ac, ac.v)
            P.v(lambda e, kb=kb: e.tensor_scalar(s_el1[:], kb[:], 0.0, None, ALU.max))
            P.v(lambda e: e.drain())
            dv.inc()
            P.v(lambda e, kb=kb: e.tensor_tensor(kb[:], s_el1[:], s_el2[:], ALU.add).then_inc(dv.h, 1))
            W(P.t, dv, dv.v)
            stg = s_qst[tb % 2]
            for j in range(4):
                if tr_at or j > 0:
                    W(P.t, ac, ac.v)
                pe.inc()
                P.t(lambda e, j=j, bank=bank: e.transpose(
                    p_tr[:], s_kblk[bank][:, j * 128:(j + 1) * 128], s_ident[:]).then_inc(pe.h, 1))
                W(P.c, pe, pe.v)
                if j == 0 and n >= 2:
                    Sq.wait(P.c, sq_tok[n - 2])
                ac.inc()
                P.c(lambda e, j=j, stg=stg: e.activation(
                    stg[:, j * 128:(j + 1) * 128], p_tr[:], AF.Copy).then_inc(ac.h, 1))
            tr_at.append(pe.v)
            W(P.g, ac, ac.v)
            sq_tok.append(Sq.issue(P.g, lambda e, nd=nd, tb=tb, stg=stg: e.dma_start(
                d_qfm.ap()[nd * 512:(nd + 1) * 512, tb * 128:(tb + 1) * 128].rearrange(
                    "(j p) c -> p j c", p=128),
                stg[:].rearrange("p (j c) -> p j c", c=128))))
    PJ_Q_END = pe.v
    QWC_END = qwc_at[-1]

    gev_at, sg_tok = [], []
    gwc_at = []
    for hc in range(16):
        if hc == 0:
            W(P.sy, dv, QWC_END)
        else:
            W(P.sy, dv, gwc_at[-1])
        tokg = Lw.issue(P.sy, lambda e, hc=hc: e.dma_start(
            s_wgf.rearrange("p (kc c) -> p kc c", c=128),
            WgT.ap()[:, hc * 128:(hc + 1) * 128].rearrange("(kc p) c -> p kc c", p=128)))
        Lw.wait(P.v, tokg)
        W(P.v, pe, pe.v)
        W(P.v, dv, _dv_bg)
        P.v(lambda e: e.tensor_copy(s_wgc[:], s_wgf[:]))
        dv.inc()
        P.v(lambda e: e.drain().then_inc(dv.h, 1))
        gwc_at.append(dv.v)
        W(P.t, dv, dv.v)
        for tc in range(4):
            m = hc * 4 + tc
            bank = m % 2
            if m >= 2:
                W(P.t, ac, gev_at[m - 2])
            for kc in range(16):
                P.t(lambda e, kc=kc, tc=tc, bank=bank: e.matmul(
                    b[4 + bank][:], s_wgc[:, kc * 128:(kc + 1) * 128],
                    park[kc][:, tc * 512:(tc + 1) * 512],
                    start=(kc == 0), stop=False))
            pe.inc()
            P.t(lambda e, bank=bank, hc=hc: e.matmul(
                b[4 + bank][:], s_bgr[0:1, hc * 128:(hc + 1) * 128], s_ones5[:],
                start=False, stop=True).then_inc(pe.h, 1))
            W(P.c, pe, pe.v)
            if m >= 2:
                Sg.wait(P.c, sg_tok[m - 2])
            else:
                Sq.wait(P.c, sq_tok[62 + m])
            ac.inc()
            P.c(lambda e, bank=bank: e.activation(
                s_gst[bank][:], b[4 + bank][:], AF.Sigmoid).then_inc(ac.h, 1))
            gev_at.append(ac.v)
            W(P.g, ac, ac.v)
            sg_tok.append(Sg.issue(P.g, lambda e, hc=hc, tc=tc, bank=bank: e.dma_start(
                d_gfm.ap()[hc * 128:(hc + 1) * 128, tc * 512:(tc + 1) * 512], s_gst[bank][:])))
    PJ_G_END = pe.v
    GWC_END = gwc_at[-1]

    ktr_end, pjn_end, atm_end = {}, {}, {}
    pjn_at, go_at, suv_at, ate_at, aev_at = {}, {}, {}, {}, {}
    dnv_at, denv_at, rbc_at = {}, {}, {}
    atm_at = {}
    W(P.t, ac, ac.v)
    W(P.c, dv, dv.v)
    for s in range(NSUP):
        pair = (s % 2) * 2
        if s == 0:
            Sq.wait_all(P.sy)
            Sg.wait_all(P.sy)
            W(P.sy, dv, B0_DONE)
        if s >= 2:
            W(P.sy, pe, ktr_end[s - 2])
            W(P.sy, pe, pjn_end[s - 2])
        kv_toks = [
            Lkv.issue(P.sy, lambda e, s=s, pair=pair: e.dma_start(
                s_ktn[pair], ktm.ap()[s * 256:s * 256 + 128, :])),
            Lkv.issue(P.sy, lambda e, s=s, pair=pair: e.dma_start(
                s_ktn[pair + 1], ktm.ap()[s * 256 + 128:s * 256 + 256, :])),
            Lkv.issue(P.sy, lambda e, s=s, pair=pair: e.dma_start(
                s_vtn[pair], vtm.ap()[s * 256:s * 256 + 128, :])),
            Lkv.issue(P.sy, lambda e, s=s, pair=pair: e.dma_start(
                s_vtn[pair + 1], vtm.ap()[s * 256 + 128:s * 256 + 256, :])),
        ]
        Lkv.wait(P.t, kv_toks[0])
        Lkv.wait(P.t, kv_toks[1])
        if s >= 1:
            W(P.t, pe, atm_end[s - 1])
        for hh in range(32):
            W(P.t, ac, ac.v)
            pe.inc()
            P.t(lambda e, hh=hh, pair=pair: e.transpose(
                p_tr[:], s_ktn[pair + (hh % 2)][:, (hh // 2) * 128:(hh // 2 + 1) * 128],
                s_ident[:]).then_inc(pe.h, 1))
            W(P.c, pe, pe.v)
            ac.inc()
            P.c(lambda e, hh=hh: e.activation(
                s_kfm[:, ((hh // 2) * 2 + (hh % 2)) * 128:((hh // 2) * 2 + (hh % 2) + 1) * 128],
                p_tr[:], AF.Copy).then_inc(ac.h, 1))
        ktr_end[s] = pe.v
        KT_EV = ac.v
        Lkv.wait(P.t, kv_toks[2])
        Lkv.wait(P.t, kv_toks[3])
        for h in range(H):
            n = s * H + h
            bq_ = n % 2
            if n >= 2:
                W(P.sy, pe, pjn_at[n - 2])
                W(P.sy, dv, go_at[n - 2])
                W(P.sy, dv, suv_at[n - 2])
            tq = Lq.issue(P.sy, lambda e, h=h, s=s, b2=bq_: e.dma_start(
                s_qs[b2][:], d_qfm.ap()[h * 128:(h + 1) * 128, s * 256:(s + 1) * 256]))
            tg = Lg.issue(P.sy, lambda e, h=h, s=s, b2=bq_: e.dma_start(
                s_gs[b2][:], d_gfm.ap()[h * 128:(h + 1) * 128, s * 256:(s + 1) * 256]))
            if s > 0:
                tsn = Lsn.issue(P.sy, lambda e, h=h, s=s, b2=bq_: e.dma_start(
                    s_sn[b2][:], snapO.ap()[(h * NSUP + s - 1) * DK:(h * NSUP + s) * DK, :]))
            if s > 0:
                Lsn.wait(P.v, tsn)
            if n >= 2:
                W(P.v, pe, pjn_at[n - 2])
            if s == 0:
                W(P.v, dv, B0_DONE)
                P.v(lambda e, h=h, b2=bq_: e.tensor_copy(s_su[b2][:], s_Sst[:, h * 128:(h + 1) * 128]))
                P.v(lambda e, h=h, b2=bq_: e.tensor_copy(s_zu[b2][:, 0:2], s_zst[:, h:h + 2]))
            else:
                P.v(lambda e, h=h, b2=bq_: e.tensor_tensor(
                    s_su[b2][:], s_Sst[:, h * 128:(h + 1) * 128], s_sn[b2][:], ALU.add))
                P.v(lambda e, h=h, s=s, b2=bq_: e.tensor_tensor(
                    s_zu[b2][:, 0:2], s_zst[:, h:h + 2],
                    s_zsn[:, (s - 1) * 16 + h:(s - 1) * 16 + h + 2], ALU.add))
            Lg.wait(P.v, tg)
            P.v(lambda e, b2=bq_: e.tensor_copy(s_gs32[b2][:], s_gs[b2][:]))
            dv.inc()
            P.v(lambda e: e.drain().then_inc(dv.h, 1))
            suv_at[n] = dv.v
            if n >= 1:
                W(P.t, dv, ate_at[n - 1])
            Lq.wait(P.t, tq)
            for half in range(2):
                pe.inc()
                P.t(lambda e, h=h, half=half, b2=bq_: e.matmul(
                    b[half][:, 0:256], s_kfm[:, (h * 2 + half) * 128:(h * 2 + half + 1) * 128],
                    s_qs[b2][:], start=True, stop=True).then_inc(pe.h, 1))
            atm_at[n] = pe.v
            W(P.c, pe, atm_at[n])
            if n >= 1:
                W(P.c, dv, ate_at[n - 1])
            for half in range(2):
                ac.inc()
                P.c(lambda e, half=half: e.activation(
                    s_el1[:, half * 256:(half + 1) * 256], b[half][:, 0:256],
                    AF.Copy).then_inc(ac.h, 1))
            aev_at[n] = ac.v
            W(P.v, ac, aev_at[n])
            P.v(lambda e, b2=bq_: e.tensor_tensor(
                s_at[b2][:, 0:256], s_el1[:, 0:256], s_m0[:], ALU.mult))
            P.v(lambda e, b2=bq_: e.tensor_tensor(
                s_at[b2][:, 256:512], s_el1[:, 256:512], s_m1[:], ALU.mult))
            dv.inc()
            P.v(lambda e: e.drain().then_inc(dv.h, 1))
            ate_at[n] = dv.v
            W(P.t, dv, ate_at[n])
            W(P.t, dv, suv_at[n])
            if n >= 1:
                W(P.t, dv, dnv_at[n - 1])
                W(P.t, ac, denv_at[n - 1])
            P.t(lambda e, b2=bq_: e.matmul(b[2][:, 0:256], s_su[b2][:], s_qs[b2][:],
                                           start=True, stop=False))
            P.t(lambda e, pair=pair, h=h, b2=bq_: e.matmul(
                b[2][:, 0:256], s_vtn[pair][:, h * 128:(h + 1) * 128], s_at[b2][:, 0:256],
                start=False, stop=False))
            pe.inc()
            P.t(lambda e, pair=pair, h=h, b2=bq_: e.matmul(
                b[2][:, 0:256], s_vtn[pair + 1][:, h * 128:(h + 1) * 128], s_at[b2][:, 256:512],
                start=False, stop=True).then_inc(pe.h, 1))
            P.t(lambda e, b2=bq_: e.matmul(p_sm[0:1, 0:256], s_zu[b2][:, 0:1], s_qs[b2][:],
                                           start=True, stop=False))
            P.t(lambda e, b2=bq_: e.matmul(p_sm[0:1, 0:256], s_ones_col[:], s_at[b2][:, 0:256],
                                           start=False, stop=False))
            pe.inc()
            P.t(lambda e, b2=bq_: e.matmul(p_sm[0:1, 0:256], s_ones_col[:], s_at[b2][:, 256:512],
                                           start=False, stop=True).then_inc(pe.h, 1))
            pjn_at[n] = pe.v
            W(P.c, pe, pjn_at[n])
            if n >= 1:
                W(P.c, dv, dnv_at[n - 1])
            ac.inc()
            P.c(lambda e, b2=bq_: e.activation(
                s_den[b2][:], p_sm[0:1, 0:256], AF.Copy, bias=1e-6).then_inc(ac.h, 1))
            denv_at[n] = ac.v
            W(P.v, ac, denv_at[n])
            P.v(lambda e, b2=bq_: e.reciprocal(s_dent[b2][:], s_den[b2][:]))
            P.v(lambda e: e.drain())
            P.v(lambda e, b2=bq_: e.tensor_tensor(s_den[b2][:], s_den[b2][:], s_dent[b2][:], ALU.mult))
            P.v(lambda e: e.drain())
            P.v(lambda e, b2=bq_: e.tensor_scalar(s_den[b2][:], s_den[b2][:], -1.0, 2.0, ALU.mult, ALU.add))
            P.v(lambda e: e.drain())
            P.v(lambda e, b2=bq_: e.tensor_tensor(s_dent[b2][:], s_dent[b2][:], s_den[b2][:], ALU.mult))
            P.v(lambda e: e.drain())
            P.v(lambda e, b2=bq_: e.tensor_copy(s_recb[b2][:], s_dent[b2][:]))
            dv.inc()
            P.v(lambda e: e.drain().then_inc(dv.h, 1))
            rcp_at = dv.v
            W(P.t, dv, rcp_at)
            if n >= 1:
                W(P.t, dv, rbc_at[n - 1])
            pe.inc()
            P.t(lambda e, b2=bq_: e.matmul(b[3][:, 0:256], s_ones_row[:], s_recb[b2][:],
                                           start=True, stop=True).then_inc(pe.h, 1))
            pb_at = pe.v
            W(P.v, pe, pb_at)
            P.v(lambda e, b2=bq_: e.tensor_copy(s_rb[b2][:], b[3][:, 0:256]))
            dv.inc()
            P.v(lambda e: e.drain().then_inc(dv.h, 1))
            rbc_at[n] = dv.v
            P.v(lambda e, b2=bq_: e.tensor_tensor(s_t1[b2][:], b[2][:, 0:256], s_rb[b2][:], ALU.mult))
            dv.inc()
            P.v(lambda e: e.drain().then_inc(dv.h, 1))
            dnv_at[n] = dv.v
            dv.inc()
            P.v(lambda e, h=h, s=s, b2=bq_: e.tensor_tensor(
                park[h][:, s * 256:(s + 1) * 256], s_t1[b2][:], s_gs32[b2][:],
                ALU.mult).then_inc(dv.h, 1))
            go_at[n] = dv.v
        atm_end[s] = atm_at[s * H + H - 1]
        pjn_end[s] = pjn_at[s * H + H - 1]
    SCAN_DONE = go_at[NSUP * H - 1]
    SUV_END = suv_at[NSUP * H - 1]
    PJN_END = pjn_end[NSUP - 1]

    oev_at, sap_tok, owc_at = [], [], []
    W(P.t, dv, SCAN_DONE)
    for nd in range(4):
        if nd == 0:
            W(P.sy, dv, GWC_END)
        else:
            W(P.sy, dv, owc_at[-1])
        tokf = Lw.issue(P.sy, lambda e, nd=nd: e.dma_start(
            s_wf32.rearrange("p (kc c) -> p kc c", c=512),
            WoT.ap()[:, nd * 512:(nd + 1) * 512].rearrange("(kc p) c -> p kc c", p=128)))
        Lw.wait(P.v, tokf)
        W(P.v, pe, pe.v)
        P.v(lambda e: e.tensor_copy(s_wb[:], s_wf32[:]))
        dv.inc()
        P.v(lambda e: e.drain().then_inc(dv.h, 1))
        owc_at.append(dv.v)
        W(P.t, dv, dv.v)
        for tc in range(NTB):
            p2 = nd * NTB + tc
            bank = tc % 2
            if p2 >= 2:
                W(P.t, ac, oev_at[p2 - 2])
            for kc in range(16):
                P.t(lambda e, kc=kc, tc=tc, bank=bank: e.matmul(
                    b[4 + bank][:], park[kc][:, tc * 128:(tc + 1) * 128],
                    s_wb[:, kc * 512:(kc + 1) * 512], start=(kc == 0), stop=False))
            pe.inc()
            P.t(lambda e, bank=bank, nd=nd: e.matmul(
                b[4 + bank][:], s_ones_row[:], s_bo[0:1, nd * 512:(nd + 1) * 512],
                start=False, stop=True).then_inc(pe.h, 1))
            W(P.c, pe, pe.v)
            if p2 == 0:
                W(P.c, dv, SUV_END)
            if p2 >= 2:
                Sap.wait(P.c, sap_tok[p2 - 2])
            ac.inc()
            P.c(lambda e, bank=bank: e.activation(
                s_apt[bank], b[4 + bank][:], AF.Copy).then_inc(ac.h, 1))
            oev_at.append(ac.v)
            W(P.g, ac, ac.v)
            sap_tok.append(Sap.issue(P.g, lambda e, nd=nd, tc=tc, bank=bank: e.dma_start(
                d_ap.ap()[tc * 128:(tc + 1) * 128, nd * 512:(nd + 1) * 512], s_apt[bank])))
    OPJ_END = pe.v
    OWC_END = owc_at[-1]

    Sap.wait_all(P.sy)
    W(P.sy, dv, OWC_END)
    W(P.c, pe, OPJ_END)
    W(P.t, ac, ac.v)
    lnv_ap, lnv_x1, lnv_xp, tr_b2, sxp_tok, sx1_tok = {}, {}, {}, {}, [], []
    x2n = 0
    for tc in range(NTB):
        b2 = tc % 2
        if tc >= 2:
            W(P.sy, dv, lnv_xp[tc - 2])
            W(P.sy, dv, lnv_x1[tc - 2])
            Sx1.wait(P.sy, sx1_tok[tc - 2])
        t_ap = Lap.issue(P.sy, lambda e, tc=tc, b2=b2: e.dma_start(
            s_ap2[b2], d_ap.ap()[tc * 128:(tc + 1) * 128, :]))
        t_x = Lx.issue(P.sy, lambda e, tc=tc, b2=b2: e.dma_start(
            s_x1t[b2], xB.ap()[tc * 128:(tc + 1) * 128, :]))
        Lap.wait(P.c, t_ap)
        W(P.c, dv, dv.v)
        P.c(lambda e, b2=b2: e.activation(s_junk[:, 0:D], s_ap2[b2], AF.Copy, accum_out=s_sum[:]))
        ac.inc()
        P.c(lambda e: e.activation(s_mu[:], s_sum[:], AF.Copy, scale=float(np.float32(1.0 / D))).then_inc(ac.h, 1))
        W(P.v, ac, ac.v)
        dv.inc()
        P.v(lambda e, b2=b2: e.tensor_scalar(s_ap2[b2], s_ap2[b2], s_mu[:], None, ALU.subtract).then_inc(dv.h, 1))
        W(P.c, dv, dv.v)
        P.c(lambda e, b2=b2: e.activation(s_junk[:, 0:D], s_ap2[b2], AF.Square, accum_out=s_sq[:]))
        P.c(lambda e: e.activation(s_lnt[:], s_sq[:], AF.Ln, bias=s_eps[:], scale=float(np.float32(1.0 / D))))
        ac.inc()
        P.c(lambda e: e.activation(s_rstd[:], s_lnt[:], AF.Exp, scale=-0.5).then_inc(ac.h, 1))
        Lx.wait(P.v, t_x)
        W(P.v, ac, ac.v)
        P.v(lambda e, b2=b2: e.tensor_scalar(s_ap2[b2], s_ap2[b2], s_rstd[:], None, ALU.mult))
        P.v(lambda e: e.drain())
        dv.inc()
        P.v(lambda e, b2=b2: e.tensor_tensor(s_x1t[b2], s_x1t[b2], s_ap2[b2], ALU.add).then_inc(dv.h, 1))
        lnv_ap[tc] = dv.v
        W(P.g, dv, dv.v)
        sx1_tok.append(Sx1.issue(P.g, lambda e, tc=tc, b2=b2: e.dma_start(
            d_x1.ap()[tc * 128:(tc + 1) * 128, :], s_x1t[b2])))
        W(P.c, dv, dv.v)
        P.c(lambda e, b2=b2: e.activation(s_junk[:, 0:D], s_x1t[b2], AF.Copy, accum_out=s_sum[:]))
        ac.inc()
        P.c(lambda e: e.activation(s_mu[:], s_sum[:], AF.Copy, scale=float(np.float32(1.0 / D))).then_inc(ac.h, 1))
        W(P.v, ac, ac.v)
        dv.inc()
        P.v(lambda e, b2=b2: e.tensor_scalar(s_ap2[b2], s_x1t[b2], s_mu[:], None, ALU.subtract).then_inc(dv.h, 1))
        lnv_x1[tc] = dv.v
        W(P.c, dv, dv.v)
        P.c(lambda e, b2=b2: e.activation(s_junk[:, 0:D], s_ap2[b2], AF.Square, accum_out=s_sq[:]))
        P.c(lambda e: e.activation(s_lnt[:], s_sq[:], AF.Ln, bias=s_eps[:], scale=float(np.float32(1.0 / D))))
        ac.inc()
        P.c(lambda e: e.activation(s_rstd[:], s_lnt[:], AF.Exp, scale=-0.5).then_inc(ac.h, 1))
        W(P.v, ac, ac.v)
        if tc >= 1:
            W(P.v, pe, tr_b2[tc - 1])
        dv.inc()
        P.v(lambda e, b2=b2: e.tensor_scalar(s_x2pt[:], s_ap2[b2], s_rstd[:], None, ALU.mult).then_inc(dv.h, 1))
        lnv_xp[tc] = dv.v
        W(P.t, dv, dv.v)
        for kc in range(16):
            W(P.t, ac, ac.v)
            pe.inc()
            P.t(lambda e, kc=kc: e.transpose(p_tr[:], s_x2pt[:, kc * 128:(kc + 1) * 128],
                                             s_ident[:]).then_inc(pe.h, 1))
            W(P.c, pe, pe.v)
            if x2n >= 2:
                Sxp.wait(P.c, sxp_tok[x2n - 2])
            ac.inc()
            P.c(lambda e, b3=x2n % 2: e.activation(s_x2st[b3][:], p_tr[:], AF.Copy).then_inc(ac.h, 1))
            W(P.g, ac, ac.v)
            sxp_tok.append(Sxp.issue(P.g, lambda e, kc=kc, tc=tc, b3=x2n % 2: e.dma_start(
                d_x2p.ap()[kc * 128:(kc + 1) * 128, tc * 128:(tc + 1) * 128], s_x2st[b3][:])))
            x2n += 1
        tr_b2[tc] = pe.v
    LNV_END = dv.v

    Sxp.wait_all(P.sy)
    W(P.sy, pe, OPJ_END)
    for kc in range(16):
        Lxp.issue(P.sy, lambda e, kc=kc: e.dma_start(
            park[kc][:], d_x2p.ap()[kc * 128:(kc + 1) * 128, :]))
    Lxp.wait_all(P.t)
    fab_at, fevA_at, fevV_at, sh_tok = {}, {}, {}, []
    W(P.sy, pe, PJN_END)
    W(P.c, dv, LNV_END)
    W(P.v, pe, PJ_Q_END)
    l12_tok = {}
    for fc in range(44):
        if fc >= 2:
            W(P.sy, pe, fab_at[(fc - 2) * 4 + 3])
        l12_tok[fc] = (
            L12.issue(P.sy, lambda e, fc=fc, bb=fc % 2: e.dma_start(
                s_w1c[bb].rearrange("p (dc c) -> p dc c", c=128),
                w1T.ap()[:, fc * 128:(fc + 1) * 128].rearrange("(dc p) c -> p dc c", p=128))),
            L12.issue(P.sy, lambda e, fc=fc, bb=fc % 2: e.dma_start(
                s_w2c[bb].rearrange("p (dc c) -> p dc c", c=128),
                w2T.ap()[:, fc * 128:(fc + 1) * 128].rearrange("(dc p) c -> p dc c", p=128))),
        )
        L12.wait(P.t, l12_tok[fc][0])
        L12.wait(P.t, l12_tok[fc][1])
        for tb4 in range(4):
            g2 = fc * 4 + tb4
            if g2 >= 1:
                W(P.t, ac, fevA_at[g2 - 1])
                W(P.t, dv, fevV_at[g2 - 1])
            for dc in range(16):
                P.t(lambda e, dc=dc, tb4=tb4, bb=fc % 2: e.matmul(
                    b[4][:], s_w1c[bb][:, dc * 128:(dc + 1) * 128],
                    park[dc][:, tb4 * 512:(tb4 + 1) * 512], start=(dc == 0), stop=(dc == 15)))
            pe.inc()
            for dc in range(16):
                if dc == 15:
                    P.t(lambda e, dc=dc, tb4=tb4, bb=fc % 2: e.matmul(
                        b[5][:], s_w2c[bb][:, dc * 128:(dc + 1) * 128],
                        park[dc][:, tb4 * 512:(tb4 + 1) * 512],
                        start=False, stop=True).then_inc(pe.h, 1))
                else:
                    P.t(lambda e, dc=dc, tb4=tb4, bb=fc % 2: e.matmul(
                        b[5][:], s_w2c[bb][:, dc * 128:(dc + 1) * 128],
                        park[dc][:, tb4 * 512:(tb4 + 1) * 512], start=(dc == 0), stop=False))
            fab_at[g2] = pe.v
            W(P.c, pe, fab_at[g2])
            if g2 >= 2:
                W(P.c, dv, fevV_at[g2 - 2])
            ac.inc()
            P.c(lambda e, bb=g2 % 2: e.activation(s_sil[bb], b[4][:], AF.Silu).then_inc(ac.h, 1))
            fevA_at[g2] = ac.v
            W(P.v, ac, fevA_at[g2])
            if g2 >= 2:
                Sh.wait(P.v, sh_tok[g2 - 2])
            dv.inc()
            P.v(lambda e, bb=g2 % 2: e.tensor_tensor(
                s_hfm[bb][:], s_sil[bb], b[5][:], ALU.mult).then_inc(dv.h, 1))
            fevV_at[g2] = dv.v
            W(P.g, dv, fevV_at[g2])
            sh_tok.append(Sh.issue(P.g, lambda e, fc=fc, tb4=tb4, bb=g2 % 2: e.dma_start(
                d_h.ap()[fc * 128:(fc + 1) * 128, tb4 * 512:(tb4 + 1) * 512], s_hfm[bb][:])))
    FAB_END = fab_at[43 * 4 + 3]

    Sh.wait_all(P.sy)
    W(P.sy, pe, FAB_END)
    ymm_at, yev_at, sy_tok = {}, {}, []
    ymm_end = {}
    for dq in range(4):
        if dq >= 1:
            W(P.sy, pe, ymm_end[dq - 1])
        w3_toks = []
        for j in range(11):
            w3_toks.append(L3s.issue(P.sy, lambda e, j=j, dq=dq: e.dma_start(
                park[j][:].rearrange("p (q c) -> p q c", c=512),
                w3T.ap()[j * 512:(j + 1) * 512, dq * 512:(dq + 1) * 512].rearrange(
                    "(q p) c -> p q c", p=128))))
        for tk in w3_toks:
            L3s.wait(P.t, tk)
        for tt in range(NTB):
            y2 = dq * NTB + tt
            if y2 >= 2:
                W(P.sy, pe, ymm_at[y2 - 2])
            th = Lh.issue(P.sy, lambda e, tt=tt, bb=tt % 2: e.dma_start(
                s_hst[bb].rearrange("p (fc c) -> p fc c", c=128),
                d_h.ap()[:, tt * 128:(tt + 1) * 128].rearrange("(fc p) c -> p fc c", p=128)))
            Lh.wait(P.t, th)
            if y2 >= 2:
                W(P.t, ac, yev_at[y2 - 2])
            for fc in range(44):
                if fc == 43:
                    pe.inc()
                    P.t(lambda e, fc=fc, bb=tt % 2, bk=tt % 2: e.matmul(
                        b[bk][:], s_hst[bb][:, fc * 128:(fc + 1) * 128],
                        park[fc // 4][:, (fc % 4) * 512:(fc % 4 + 1) * 512],
                        start=False, stop=True).then_inc(pe.h, 1))
                else:
                    P.t(lambda e, fc=fc, bb=tt % 2, bk=tt % 2: e.matmul(
                        b[bk][:], s_hst[bb][:, fc * 128:(fc + 1) * 128],
                        park[fc // 4][:, (fc % 4) * 512:(fc % 4 + 1) * 512],
                        start=(fc == 0), stop=False))
            ymm_at[y2] = pe.v
            W(P.c, pe, ymm_at[y2])
            if y2 >= 2:
                Sy.wait(P.c, sy_tok[y2 - 2])
            else:
                Sg.wait(P.c, sg_tok[62 + y2])
            ac.inc()
            P.c(lambda e, bk=tt % 2, bb=y2 % 2: e.activation(
                s_yst[bb][:], b[bk][:], AF.Copy).then_inc(ac.h, 1))
            yev_at[y2] = ac.v
            W(P.g, ac, ac.v)
            sy_tok.append(Sy.issue(P.g, lambda e, tt=tt, dq=dq, bb=y2 % 2: e.dma_start(
                d_y.ap()[tt * 128:(tt + 1) * 128, dq * 512:(dq + 1) * 512], s_yst[bb][:])))
        ymm_end[dq] = pe.v

    Sy.wait_all(P.sy)
    Sx1.wait_all(P.sy)
    fin_at, so_tok = {}, []
    for tt in range(NTB):
        b2 = tt % 2
        if tt >= 2:
            W(P.sy, dv, fin_at[tt - 2])
        ty = Ly.issue(P.sy, lambda e, tt=tt, b2=b2: e.dma_start(
            s_yt[b2], d_y.ap()[tt * 128:(tt + 1) * 128, :]))
        tx1 = Lx1.issue(P.sy, lambda e, tt=tt, b2=b2: e.dma_start(
            s_x1b[b2], d_x1.ap()[tt * 128:(tt + 1) * 128, :]))
        Ly.wait(P.c, ty)
        W(P.c, dv, dv.v)
        P.c(lambda e, b2=b2: e.activation(s_junk[:, 0:D], s_yt[b2], AF.Copy, accum_out=s_sum[:]))
        ac.inc()
        P.c(lambda e: e.activation(s_mu[:], s_sum[:], AF.Copy, scale=float(np.float32(1.0 / D))).then_inc(ac.h, 1))
        W(P.v, ac, ac.v)
        dv.inc()
        P.v(lambda e, b2=b2: e.tensor_scalar(s_yc, s_yt[b2], s_mu[:], None, ALU.subtract).then_inc(dv.h, 1))
        W(P.c, dv, dv.v)
        P.c(lambda e: e.activation(s_junk[:, 0:D], s_yc, AF.Square, accum_out=s_sq[:]))
        P.c(lambda e: e.activation(s_lnt[:], s_sq[:], AF.Ln, bias=s_eps[:], scale=float(np.float32(1.0 / D))))
        ac.inc()
        P.c(lambda e: e.activation(s_rstd[:], s_lnt[:], AF.Exp, scale=-0.5).then_inc(ac.h, 1))
        Lx1.wait(P.v, tx1)
        W(P.v, ac, ac.v)
        P.v(lambda e: e.tensor_scalar(s_yc, s_yc, s_rstd[:], None, ALU.mult))
        P.v(lambda e: e.drain())
        if tt >= 1:
            Sout.wait(P.v, so_tok[tt - 1])
        dv.inc()
        P.v(lambda e, b2=b2: e.tensor_tensor(s_ost, s_x1b[b2], s_yc, ALU.add).then_inc(dv.h, 1))
        fin_at[tt] = dv.v
        W(P.g, dv, fin_at[tt])
        so_tok.append(Sout.issue(P.g, lambda e, tt=tt: e.dma_start(
            o_out.ap()[tt * 128:(tt + 1) * 128, :], s_ost)))

    for stm in (Sq, Sg, Sap, Sx1, Sxp, Sh, Sy, Sout):
        stm.wait_all(P.g)
    P.finalize()
    return nc


def _prep_inputs_B(inputs, resA):
    x = np.asarray(inputs["L0_attn_x"])[0]
    Wq = np.asarray(inputs["L0_attn_Wq"])
    Wg = np.asarray(inputs["L0_attn_Wg"])
    Wo = np.asarray(inputs["L0_attn_Wo"])
    bq = np.asarray(inputs["L0_attn_bq"])
    bg = np.asarray(inputs["L0_attn_bg"])
    bo = np.asarray(inputs["L0_attn_bo"])
    WqT = np.ascontiguousarray(Wq.T[:, PERM], np.float32)
    WgT = np.ascontiguousarray(Wg.T, np.float32)
    WoT = np.ascontiguousarray(Wo.T, np.float32)
    bqP = np.ascontiguousarray(bq[PERM], np.float32).reshape(1, D)
    bgv = np.ascontiguousarray(bg, np.float32).reshape(1, D)
    bov = np.ascontiguousarray(bo, np.float32).reshape(1, D)
    ident = np.eye(128, dtype=bf16)
    m0 = np.zeros((128, 256), bf16)
    m0[0:64, :] = 1
    m0[64:128, 64:] = 1
    m1 = np.zeros((128, 256), bf16)
    m1[0:64, 128:] = 1
    m1[64:128, 192:] = 1
    w1T = np.ascontiguousarray(np.concatenate([resA[r]["o_w1t"] for r in range(NCORES)], axis=1))
    w2T = np.ascontiguousarray(np.concatenate([resA[r]["o_w2t"] for r in range(NCORES)], axis=1))
    w3T = np.ascontiguousarray(np.concatenate([resA[r]["o_w3t"] for r in range(NCORES)], axis=1))
    maps = []
    for c in range(NCORES):
        cos, sin = _rope_tables(c)
        spv = np.zeros((7 * H * DK, DK), bf16)
        zpr = np.zeros((128, 32, 7), np.float32)
        for r in range(c):
            sn = np.asarray(resA[r]["o_ssnap"]).reshape(H, NSUP, DK, DK)[:, NSUP - 1]
            spv[r * H * DK:(r + 1) * H * DK] = sn.reshape(H * DK, DK)
            zf = np.asarray(resA[r]["o_zsnap"]).reshape(NSUP, H, 128)[NSUP - 1]
            zpr[:, :H, r] = zf.T
        zsn = np.ascontiguousarray(
            np.asarray(resA[c]["o_zsnap"]).reshape(NSUP, H, 128).transpose(2, 0, 1).reshape(128, NSUP * H),
            np.float32)
        maps.append({
            "xB": np.ascontiguousarray(x[c * TPC:(c + 1) * TPC], np.float32),
            "x2r": np.asarray(resA[c]["o_x2"]),
            "ktm": np.asarray(resA[c]["o_ktm"]),
            "vtm": np.asarray(resA[c]["o_vtm"]),
            "snapO": np.asarray(resA[c]["o_ssnap"]),
            "sprev": spv,
            "zsnC": zsn,
            "zprC": np.ascontiguousarray(zpr.reshape(128, 224), np.float32),
            "WqT": WqT, "WgT": WgT, "WoT": WoT,
            "bq": bqP, "bg": bgv, "bo": bov,
            "w1T": w1T, "w2T": w2T, "w3T": w3T,
            "cosA": cos, "sinA": sin, "ident": ident,
            "m0": m0, "m1": m1,
        })
    return maps


def kernel(**inputs):
    mapsA = _prep_inputs_A(inputs)
    ncA = build_A()
    resA = run_bass_kernel_spmd(ncA, mapsA, core_ids=list(range(NCORES))).results
    mapsB = _prep_inputs_B(inputs, resA)
    ncB = build_B()
    resB = run_bass_kernel_spmd(ncB, mapsB, core_ids=list(range(NCORES))).results
    out = np.concatenate([np.asarray(resB[c]["o_out"], np.float32) for c in range(NCORES)], axis=0)
    return out.reshape(1, T, D)


# version 02
# speedup vs baseline: 1.0863x; 1.0863x over previous
import os
import numpy as np
import ml_dtypes

import concourse.bass as bass
import concourse.mybir as mybir
from concourse.ap import AP as _AP
from concourse.bass_utils import run_bass_kernel_spmd

bf16 = ml_dtypes.bfloat16
F32 = mybir.dt.float32
BF16 = mybir.dt.bfloat16
AF = mybir.ActivationFunctionType
ALU = mybir.AluOpType
AX = mybir.AxisListType

NCORES = 8
T = 16384
TPC = T // NCORES
D = 2048
H = 16
DK = 128
SUP = 256
NSUP = TPC // SUP
NTB = TPC // 128
DFF = 5632
FS1 = DFF // NCORES
FS3 = D // NCORES
NSG = 88
NB_TOT = DFF * D // 64

_NF4 = np.array([-1.0, -0.6962, -0.5251, -0.3949, -0.2844, -0.1848, -0.0911,
                 0.0, 0.0796, 0.1609, 0.2461, 0.3379, 0.4407, 0.5626, 0.723,
                 1.0], dtype=bf16)


def _nf4_consts():
    bits = np.arange(65536, dtype=np.uint16)
    av = bits.view(bf16)
    fv = av.astype(np.float32)
    sel = np.isfinite(fv) & (np.abs(fv) <= 1.0)
    qs = np.sort(av[sel].astype(np.float32), kind="stable")
    d = np.abs((qs.astype(bf16)[:, None] - _NF4[None, :]).astype(bf16))
    idx = np.argmin(d.astype(np.float32), axis=1)
    thr = []
    for j in range(1, 16):
        k = np.searchsorted(idx, j, "left")
        thr.append(float(float(np.float32((float(qs[k - 1])) + float(qs[k])) / 2.0)))
    steps = []
    acc = _NF4[0]
    for j in range(1, 16):
        cand = np.array(float(_NF4[j]) - float(acc), dtype=bf16)
        if (acc + cand).astype(bf16) != _NF4[j]:
            cb = cand.view(np.uint16)
            for dlt in (1, -1, 2, -2):
                c2 = np.array(cb + dlt, dtype=np.uint16).view(bf16)[()]
                if (acc + c2).astype(bf16) == _NF4[j]:
                    cand = c2
                    break
        steps.append(float(float(np.float32(np.array(cand, dtype=bf16)))))
        acc = (acc + cand).astype(bf16)[()]
    return thr, steps


NF4_THR, NF4_STEP = _nf4_consts()
MAGIC = 12582912.0
EPS_LN = 1e-5


def _head_perm():
    p = np.zeros(D, np.int64)
    for h in range(H):
        b = h * DK
        p[b:b + 64] = b + np.arange(0, DK, 2)
        p[b + 64:b + DK] = b + np.arange(1, DK, 2)
    return p


PERM = _head_perm()


class Prog:

    ENGINES = ("sync", "vector", "scalar", "tensor", "gpsimd")

    def __init__(self, nc):
        self.nc = nc
        self.streams = {e: [] for e in self.ENGINES}

    def sy(self, fn): self.streams["sync"].append(fn)
    def v(self, fn): self.streams["vector"].append(fn)
    def c(self, fn): self.streams["scalar"].append(fn)
    def t(self, fn): self.streams["tensor"].append(fn)
    def g(self, fn): self.streams["gpsimd"].append(fn)

    def finalize(self):
        with self.nc.Block() as blk:
            for name in self.ENGINES:
                fns = self.streams[name]
                if not fns:
                    continue

                def body(e, fns=fns):
                    for fn in fns:
                        fn(e)

                getattr(blk, name)(body)


class Sem:
    def __init__(self, nc, name):
        self.h = nc.alloc_semaphore(name)
        self.v = 0

    def inc(self, n=1):
        self.v += n
        return self.v


class DStream:

    def __init__(self, nc, name, nbuf=2):
        self.sems = [Sem(nc, f"{name}_{i}") for i in range(nbuf)]
        self.n = 0

    def issue(self, P_eng, fn):
        s = self.sems[self.n % len(self.sems)]
        s.inc(16)
        v = s.v
        P_eng(lambda e, s=s: fn(e).then_inc(s.h, 16))
        self.n += 1
        return (s, v)

    def wait(self, P_eng, tok):
        s, v = tok
        P_eng(lambda e, s=s, v=v: e.wait_ge(s.h, v))

    def wait_all(self, P_eng):
        for s in self.sems:
            P_eng(lambda e, s=s, v=s.v: e.wait_ge(s.h, v))


def _tile_rows(R):
    out, r = [], 0
    while r < R:
        n = min(128, R - r)
        out.append((r, n))
        r += n
    return out


def build_A():
    nc = bass.Bass("TRN2", target_bir_lowering=False, detect_race_conditions=False)
    f = nc.dram_tensor

    xA = f("xA", [TPC, D], F32, kind="ExternalInput")
    WkT = f("WkT", [D, D], F32, kind="ExternalInput")
    WvT = f("WvT", [D, D], F32, kind="ExternalInput")
    bk = f("bk", [1, D], F32, kind="ExternalInput")
    bv = f("bv", [1, D], F32, kind="ExternalInput")
    cosA = f("cosA", [TPC, 64], F32, kind="ExternalInput")
    sinA = f("sinA", [TPC, 64], F32, kind="ExternalInput")
    ident = f("ident", [128, 128], BF16, kind="ExternalInput")
    w1f = f("w1f", [DFF, D], BF16, kind="ExternalInput")
    w2f = f("w2f", [DFF, D], BF16, kind="ExternalInput")
    w3f = f("w3f", [D, DFF], BF16, kind="ExternalInput")
    w1s = f("w1s", [FS1, D], BF16, kind="ExternalInput")
    w2s = f("w2s", [FS1, D], BF16, kind="ExternalInput")
    w3s = f("w3s", [FS3, DFF], BF16, kind="ExternalInput")

    o_x2 = f("o_x2", [D, TPC], BF16, kind="ExternalOutput")
    o_ktm = f("o_ktm", [TPC, D], BF16, kind="ExternalOutput")
    o_vtm = f("o_vtm", [TPC, D], BF16, kind="ExternalOutput")
    o_ssnap = f("o_ssnap", [H * NSUP * DK, DK], BF16, kind="ExternalOutput")
    o_zsnap = f("o_zsnap", [1, NSUP * D], F32, kind="ExternalOutput")
    o_w1t = f("o_w1t", [D, FS1], BF16, kind="ExternalOutput")
    o_dlvl = f("o_dlvl", [128, D], BF16, kind="ExternalOutput")
    o_dbgq = f("o_dbgq", [128, D], BF16, kind="ExternalOutput")
    o_dbgw = f("o_dbgw", [128, D], BF16, kind="ExternalOutput")
    o_dbgs = f("o_dbgs", [128, 128], BF16, kind="ExternalOutput")
    o_dbgl = f("o_dbgl", [128, D], BF16, kind="ExternalOutput")
    o_w2t = f("o_w2t", [D, FS1], BF16, kind="ExternalOutput")
    o_w3t = f("o_w3t", [DFF, FS3], BF16, kind="ExternalOutput")

    if os.environ.get("KA_FAKES1", "0") == "1":
        i_s1 = [f(f"i_s1_{i}", [1, FS1 * 32], BF16, kind="ExternalInput") for i in range(3)]
        i_dq = [f(f"i_dq_{i}", [1, FS1 * 32], BF16, kind="ExternalInput") for i in range(3)]
    d_s1 = [f(f"d_s1_{i}", [1, FS1 * 32 + 16], BF16) for i in range(3)]
    d_dq = [f(f"d_dq_{i}", [1, FS1 * 32 + 16], BF16) for i in range(3)]
    d_dq8 = [f(f"d_dq8_{i}", [1, FS1 * 32 * 8 + 256], BF16) for i in range(3)]

    sb = nc.alloc_sbuf_tensor
    s_cos = sb("s_cos", [128, NTB * 64], F32)
    s_sin = sb("s_sin", [128, NTB * 64], F32)
    s_ident = sb("s_ident", [128, 128], BF16)
    s_ones_row = sb("s_ones_row", [1, 128], BF16)
    s_ones_col = sb("s_ones_col", [128, 1], BF16)
    s_ones_f32 = sb("s_ones_f32", [128, 1], F32)
    s_bk = sb("s_bk", [1, D], BF16)
    s_bv = sb("s_bv", [1, D], BF16)


    s_wt = [sb(f"s_wt{i}", [128, DFF + 256], BF16) for i in range(2)]
    s_acc = [sb(f"s_acc{i}", [128, 1], F32) for i in range(3)]
    s_red = sb("s_red", [128, 96], F32)
    s_red2 = sb("s_red2", [128, 1], F32)
    s_sums = sb("s_sums", [1, 4], F32)
    s_smean = sb("s_smean", [1, 4], BF16)
    s_smeanB = sb("s_smeanB", [128, 3], F32)

    _pad0 = sb("_pad0", [128, 256], BF16)
    s_q = sb("s_q", [128, DFF + 256], BF16)
    s_lvl = sb("s_lvl", [128, DFF + 256], BF16)
    s_term = sb("s_term", [128, DFF + 256], BF16)
    s_s1 = sb("s_s1", [128, 96], F32)
    s_s1b = sb("s_s1b", [128, 96], BF16)
    s_r1 = sb("s_r1", [128, 96], F32)
    s_rt = sb("s_rt", [128, 96], F32)
    s_cent = sb("s_cent", [128, 96], BF16)
    s_dqs = sb("s_dqs", [128, 96], BF16)
    s_s1all = sb("s_s1all", [128, 1440], BF16)
    s_dq8 = sb("s_dq8", [NSG, 2048], BF16)
    s_dqs8 = sb("s_dqs8", [128, 704 + 128], BF16)
    s_grp = sb("s_grp", [NSG, 256], BF16)
    s_gax = sb("s_gax", [NSG, 1], BF16)
    s_ga32 = sb("s_ga32", [NSG, 1], F32)
    s_gq = sb("s_gq", [NSG, 1], F32)
    s_gq2 = sb("s_gq2", [NSG, 1], F32)
    s_gt = sb("s_gt", [NSG, 1], F32)
    s_qf = sb("s_qf", [NSG, 1], BF16)
    s_qf32 = sb("s_qf32", [NSG, 1], F32)
    s_prod = sb("s_prod", [NSG, 256], BF16)
    s_qsc = sb("s_qsc", [NSG, 256], F32)
    s_deqg = sb("s_deqg", [NSG, 256], BF16)
    s_wtr = [sb(f"s_wtr{i}", [128, 128], BF16) for i in range(2)]

    s_eps = sb("s_eps", [128, 1], F32)
    s_x = [sb(f"s_x{i}", [128, D], F32) for i in range(2)]
    s_b32 = s_x[0][0:1, 0:D]
    s_b32b = s_x[1][0:1, 0:D]
    s_xc = sb("s_xc", [128, D], F32)
    s_mu = sb("s_mu", [128, 1], F32)
    s_ss = sb("s_ss", [128, 1], F32)
    s_lnt = sb("s_lnt", [128, 1], F32)
    s_rstd = sb("s_rstd", [128, 1], F32)
    s_x2t = sb("s_x2t", [128, D], BF16)
    s_x2st = [sb(f"s_x2st{i}", [128, 128], BF16) for i in range(2)]
    s_x2b = [sb(f"s_x2b{i}", [128, 16 * 128], BF16) for i in range(2)]
    s_wf32 = sb("s_wf32", [128, 16 * 512], F32)
    s_wb = sb("s_wb", [128, 16 * 512], BF16)
    s_kblk = [sb(f"s_kblk{i}", [128, 512], BF16) for i in range(2)]
    s_ro1 = sb("s_ro1", [128, 256], BF16)
    s_ro2 = sb("s_ro2", [128, 256], BF16)
    s_el1 = sb("s_el1", [128, 512], BF16)
    s_el2 = sb("s_el2", [128, 512], BF16)
    s_zrow = sb("s_zrow", [1, 1024], F32)
    s_kv = [s_term[0:128, 0:D], s_term[0:128, D:2 * D],
            s_lvl[0:128, 0:D], s_lvl[0:128, D:2 * D]]
    s_vv = [s_wt[0][0:128, 0:D], s_wt[0][0:128, D:2 * D],
            s_wt[1][0:128, 0:D], s_wt[1][0:128, D:2 * D]]
    s_snap = None

    ps = nc.alloc_psum_tensor
    p_mm = [ps(f"p_mm{i}", [128, 512], F32) for i in range(2)]
    p_z = ps("p_z", [1, 512], F32)
    p_tr = ps("p_tr", [128, 128], BF16)
    p_s = [ps(f"p_s{i}", [128, 128], F32) for i in range(4)]
    p_one = p_z[0:1, 0:1]
    p_bc = p_mm[0][:, 0:1]

    P = Prog(nc)
    cst = Sem(nc, "cst")
    sm = Sem(nc, "sm")
    dq = Sem(nc, "dq")
    tr = Sem(nc, "tr")
    te = Sem(nc, "te")
    lnv = Sem(nc, "lnv")
    lns = Sem(nc, "lns")
    wcast = Sem(nc, "wcast")
    pj = Sem(nc, "pj")
    ev = Sem(nc, "ev")
    rp = Sem(nc, "rp")
    zs = Sem(nc, "zs")
    sn = Sem(nc, "sn")
    sne = Sem(nc, "sne")
    Lc = DStream(nc, "Lc", 1)
    Lw = DStream(nc, "Lw", 2)
    Lg = DStream(nc, "Lg", 1)
    L3 = DStream(nc, "L3", 1)
    Lx = DStream(nc, "Lx", 2)
    Lf = DStream(nc, "Lf", 1)
    L2 = DStream(nc, "L2", 2)
    Lkv = DStream(nc, "Lkv", 2)
    S1 = DStream(nc, "S1", 1)
    S2 = DStream(nc, "S2", 1)
    Swt = DStream(nc, "Swt", 2)
    Sx2 = DStream(nc, "Sx2", 2)
    Sspl = DStream(nc, "Sspl", 2)
    Szt = DStream(nc, "Szt", 2)
    Ssn = DStream(nc, "Ssn", 2)
    dbgs = Sem(nc, "dbgs")

    def W(P_eng, sem, val):
        P_eng(lambda e, s=sem.h, w=val: e.wait_ge(s, w))

    Lc.issue(P.sy, lambda e: e.dma_start(s_ident[:], ident.ap()))
    Lc.issue(P.sy, lambda e: e.dma_start(s_cos[:].rearrange("p (a f) -> p a f", f=64),
                                         cosA.ap().rearrange("(a p) f -> p a f", p=128)))
    Lc.issue(P.sy, lambda e: e.dma_start(s_sin[:].rearrange("p (a f) -> p a f", f=64),
                                         sinA.ap().rearrange("(a p) f -> p a f", p=128)))
    Lc.issue(P.sy, lambda e: e.dma_start(s_b32, bk.ap()))
    Lc.issue(P.sy, lambda e: e.dma_start(s_b32b, bv.ap()))
    Lc.wait_all(P.v)
    P.v(lambda e: e.memset(s_ones_row[:], 1.0))
    P.v(lambda e: e.memset(s_eps[:], float(np.float32(EPS_LN))))
    P.v(lambda e: e.memset(s_ones_col[:], 1.0))
    P.v(lambda e: e.memset(s_ones_f32[:], 1.0))
    P.v(lambda e: e.tensor_copy(s_bk[:], s_b32))
    P.v(lambda e: e.tensor_copy(s_bv[:], s_b32b))
    cst.inc()
    P.v(lambda e: e.drain().then_inc(cst.h, 1))

    _PH = int(os.environ.get("KA_PHASES", "9"))
    if os.environ.get("KA_NOPA1", "0") == "1":
        W(P.v, cst, cst.v)
        P.v(lambda e: e.memset(s_smeanB[:], 0.05))
        sm.inc()
        P.v(lambda e: e.drain().then_inc(sm.h, 1))
        mats = []
    else:
        mats = [(w1f, D, 44, 0), (w2f, D, 44, 1), (w3f, DFF, 16, 2)]
    sm_tile = 0
    for (wh, C, ntile, mi) in mats:
        G = C // 64
        P.v(lambda e, mi=mi: e.memset(s_acc[mi][:], 0.0))
        for t in range(ntile):
            if sm_tile >= 2:
                W(P.sy, sm, sm_tile - 1)
            tok = Lw.issue(P.sy, lambda e, wh=wh, t=t, C=C, b=sm_tile % 2: e.dma_start(
                s_wt[b][:, 0:C], wh.ap()[t * 128:(t + 1) * 128, :]))
            Lw.wait(P.v, tok)
            P.v(lambda e, b=sm_tile % 2, C=C, G=G: e.tensor_reduce(
                s_red[:, 0:G], s_wt[b][:, 0:C].rearrange("p (g k) -> p g k", k=64),
                AX.X, ALU.max, apply_absolute_value=True))
            P.v(lambda e: e.drain())
            P.v(lambda e, G=G: e.tensor_reduce(s_red2[:], s_red[:, 0:G], AX.X, ALU.add))
            P.v(lambda e: e.drain())
            P.v(lambda e, mi=mi: e.tensor_tensor(s_acc[mi][:], s_acc[mi][:], s_red2[:], ALU.add))
            sm.inc()
            P.v(lambda e: e.drain().then_inc(sm.h, 1))
            sm_tile += 1
    for mi in range(3):
        W(P.t, sm, sm.v)
        sm.inc()
        P.t(lambda e, mi=mi: e.matmul(p_one, s_ones_f32[:], s_acc[mi][:], start=True, stop=True).then_inc(sm.h, 1))
        W(P.c, sm, sm.v)
        sm.inc()
        P.c(lambda e, mi=mi: e.activation(s_sums[:, mi:mi + 1], p_one, AF.Copy).then_inc(sm.h, 1))
    W(P.v, sm, sm.v)
    P.v(lambda e: e.tensor_scalar(s_smean[:, 0:3], s_sums[:, 0:3], float(np.float32(1.0 / NB_TOT)), None, ALU.mult))
    sm.inc()
    P.v(lambda e: e.drain().then_inc(sm.h, 1))
    for mi in range(3):
        W(P.t, sm, sm.v)
        sm.inc()
        P.t(lambda e, mi=mi: e.matmul(p_bc, s_ones_row[:], s_smean[:, mi:mi + 1], start=True, stop=True).then_inc(sm.h, 1))
        W(P.c, sm, sm.v)
        sm.inc()
        P.c(lambda e, mi=mi: e.activation(s_smeanB[:, mi:mi + 1], p_bc, AF.Copy).then_inc(sm.h, 1))
    SMEAN_DONE = sm.v

    if _PH < 2:
        for stm in (S1, S2, Swt, Sx2, Sspl, Szt, Ssn):
            stm.wait_all(P.g)
        P.finalize()
        return nc
    shards = [(w1s, d_s1[0], d_dq[0], o_w1t, FS1, D, 0),
              (w2s, d_s1[1], d_dq[1], o_w2t, FS1, D, 1),
              (w3s, d_s1[2], d_dq[2], o_w3t, FS3, DFF, 2)]

    _fake = os.environ.get("KA_FAKES1", "0") == "1"
    if _fake:
        shards = [(w1s, i_s1[0], i_dq[0], o_w1t, FS1, D, 0),
                  (w2s, i_s1[1], i_dq[1], o_w2t, FS1, D, 1),
                  (w3s, i_s1[2], i_dq[2], o_w3t, FS3, DFF, 2)]
    _s1off = {}
    _o = 0
    for (_ws, _d1, _d2, _ow, _R, _C, _mi) in shards:
        for _ti, (_r0, _rn) in enumerate(_tile_rows(_R)):
            _s1off[(_mi, _ti)] = _o
            _o += 96
    _shards_p1 = [(w1s, d_s1[0], d_dq[0], o_w1t, FS1, D, 0),
                  (w2s, d_s1[1], d_dq[1], o_w2t, FS1, D, 1),
                  (w3s, d_s1[2], d_dq[2], o_w3t, FS3, DFF, 2)]
    _runp1 = (not _fake) or os.environ.get("KA_RUNP1", "0") == "1"
    p1 = 0
    for (ws, ds1, ddq, owt, R, C, mi) in (_shards_p1 if _runp1 else []):
        G = C // 64
        for ti, (r0, rn) in enumerate(_tile_rows(R)):
            if p1 >= 1:
                W(P.sy, dq, dq.v)
            tok = Lw.issue(P.sy, lambda e, ws=ws, r0=r0, rn=rn, C=C, b=Lw.n % 2: e.dma_start(
                s_wt[b][0:rn, 0:C], ws.ap()[r0:r0 + rn, :]))
            bb = (Lw.n - 1) % 2
            Lw.wait(P.v, tok)
            if p1 == 0:
                W(P.v, sm, SMEAN_DONE)
            if p1 >= 1:
                W(P.v, S1.sems[0], S1.sems[0].v)
            if p1 == 0:
                P.v(lambda e: e.memset(s_s1[:], 1.0))
                P.v(lambda e: e.drain())
            P.v(lambda e, b=bb, rn=rn, C=C, G=G: e.tensor_reduce(
                s_s1[0:rn, 0:G], s_wt[b][0:rn, 0:C].rearrange("p (g k) -> p g k", k=64),
                AX.X, ALU.max, apply_absolute_value=True))
            P.v(lambda e: e.drain())
            _so = _s1off[(mi, ti)]
            P.v(lambda e, rn=rn, so=_so: e.tensor_copy(s_s1all[0:rn, so:so + 96], s_s1[0:rn, 0:96]))
            P.v(lambda e, rn=rn, mi=mi: e.tensor_scalar(
                s_cent[0:rn, 0:96], s_s1[0:rn, 0:96], s_smeanB[0:rn, mi:mi + 1], None,
                ALU.subtract))
            dq.inc()
            P.v(lambda e: e.drain().then_inc(dq.h, 1))
            W(P.c, dq, dq.v)
            S1.issue(P.c, lambda e, ddq=ddq, r0=r0, rn=rn, G=G: e.dma_start(
                ddq.ap()[:, r0 * G:(r0 + rn) * G].rearrange("() (p g) -> p g", g=G),
                s_cent[0:rn, 0:G]))
            p1 += 1

    _runp2 = (not _fake) or os.environ.get("KA_RUNP2", "0") == "1"
    for gi, (ws, ds1, ddq, owt, R, C, mi) in enumerate(_shards_p1 if _runp2 else []):
        nb = R * (C // 64)
        S1.wait_all(P.c)
        tok = Lg.issue(P.c, lambda e, ddq=ddq, nb=nb: e.dma_start(
            s_grp[:], ddq.ap()[:, 0:nb].rearrange("() (sg n) -> sg n", n=256)))
        Lg.wait(P.v, tok)
        if gi >= 1:
            W(P.v, S2.sems[0], S2.sems[0].v)
        for _op in [
            lambda e: e.tensor_reduce(s_gax[:], s_grp[:], AX.X, ALU.max, apply_absolute_value=True),
            lambda e: e.tensor_copy(s_ga32[:], s_gax[:]),
            lambda e: e.tensor_scalar(s_gq[:], s_ga32[:], 2.0, None, ALU.mult),
            lambda e: e.reciprocal(s_gq2[:], s_gq[:]),
            lambda e: e.tensor_tensor(s_gt[:], s_gq[:], s_gq2[:], ALU.mult),
            lambda e: e.tensor_scalar(s_gt[:], s_gt[:], -1.0, 2.0, ALU.mult, ALU.add),
            lambda e: e.tensor_tensor(s_gq2[:], s_gq2[:], s_gt[:], ALU.mult),
            lambda e: e.tensor_scalar(s_qf[:], s_gq2[:], 256.0, None, ALU.mult),
            lambda e: e.tensor_copy(s_qf32[:], s_qf[:]),
            lambda e: e.tensor_scalar(s_prod[:], s_grp[:], s_qf32[:], None, ALU.mult),
            lambda e: e.tensor_scalar(s_qsc[:], s_prod[:], MAGIC, MAGIC, ALU.add, ALU.subtract),
            lambda e: e.tensor_scalar(s_qsc[:], s_qsc[:], -128.0, 127.0, ALU.max, ALU.min),
            lambda e: e.reciprocal(s_gq[:], s_qf32[:]),
            lambda e: e.tensor_tensor(s_gt[:], s_qf32[:], s_gq[:], ALU.mult),
            lambda e: e.tensor_scalar(s_gt[:], s_gt[:], -1.0, 2.0, ALU.mult, ALU.add),
            lambda e: e.tensor_tensor(s_gq[:], s_gq[:], s_gt[:], ALU.mult),
            lambda e: e.tensor_scalar(s_deqg[:], s_qsc[:], s_gq[:], None, ALU.mult),
        ]:
            P.v(_op)
            P.v(lambda e: e.drain())
        P.v(lambda e, mi=mi: e.tensor_scalar(s_deqg[:], s_deqg[:], s_smeanB[0:NSG, mi:mi + 1], None, ALU.add))
        P.v(lambda e: e.drain())
        P.v(lambda e: e.tensor_copy(
            s_dq8[:].rearrange("p (n e) -> p n e", e=8),
            s_deqg[:].rearrange("p n -> p n ()").broadcast_to([NSG, 256, 8])))
        dq.inc()
        P.v(lambda e: e.drain().then_inc(dq.h, 1))
        W(P.c, dq, dq.v)
        gi8 = {0: d_dq8[0], 1: d_dq8[1], 2: d_dq8[2]}[mi]
        S2.issue(P.c, lambda e, gi8=gi8, nb=nb: e.dma_start(
            gi8.ap()[:, 0:nb * 8].rearrange("() (sg n) -> sg n", n=2048), s_dq8[:]))


    def expand64(src_ap_fn, dst, rn, G, dtype_note=""):
        v3 = dst.rearrange("p (g k) -> p g k", k=64)
        P.v(lambda e: e.tensor_copy(v3[0:rn, 0:G, 0:1], src_ap_fn()))
        P.v(lambda e: e.drain())
        w = 1
        while w < 64:
            ww = w
            P.v(lambda e, ww=ww: e.tensor_copy(v3[0:rn, 0:G, ww:2 * ww], v3[0:rn, 0:G, 0:ww]))
            P.v(lambda e: e.drain())
            w *= 2
    tcnt = 0
    wt_tok = []
    p3 = 0
    for (ws, ds1, ddq, owt, R, C, mi) in shards:
        G = C // 64
        for ti, (r0, rn) in enumerate(_tile_rows(R)):
            if p3 == 0:
                S2.wait_all(P.sy)
                W(P.sy, dq, dq.v)
            else:
                W(P.sy, dq, dq.v)
            Lw.issue(P.sy, lambda e, ws=ws, r0=r0, rn=rn, C=C, b=Lw.n % 2: e.dma_start(
                s_wt[b][0:rn, 0:C], ws.ap()[r0:r0 + rn, :]))
            bb = (Lw.n - 1) % 2
            dd8 = {0: d_dq8[0], 1: d_dq8[1], 2: d_dq8[2]}[mi]
            L3.issue(P.sy, lambda e, dd8=dd8, r0=r0, rn=rn, G=G: e.dma_start(
                s_dqs8[0:rn, 0:G * 8 + 128],
                _AP(tensor=dd8.ap().tensor, offset=r0 * G * 8, ap=[[G * 8, rn], [1, G * 8 + 128]])))
            Lw.wait_all(P.v)
            L3.wait_all(P.v)
            if p3 >= 1:
                W(P.v, tr, tr.v)
            _so = _s1off[(mi, ti)]
            for _mk in [
                lambda rn, G, so=_so: (lambda e: e.tensor_copy(s_s1[0:rn, 0:96], s_s1all[0:rn, so:so + 96])),
                lambda rn, G: (lambda e: e.reciprocal(s_r1[0:rn, 0:96], s_s1[0:rn, 0:96])),
                lambda rn, G: (lambda e: e.tensor_tensor(s_rt[0:rn, 0:96], s_s1[0:rn, 0:96], s_r1[0:rn, 0:96], ALU.mult)),
                lambda rn, G: (lambda e: e.tensor_scalar(s_rt[0:rn, 0:96], s_rt[0:rn, 0:96], -1.0, 2.0, ALU.mult, ALU.add)),
                lambda rn, G: (lambda e: e.tensor_tensor(s_r1[0:rn, 0:96], s_r1[0:rn, 0:96], s_rt[0:rn, 0:96], ALU.mult)),
            ]:
                P.v(_mk(rn, G))
                P.v(lambda e: e.drain())
            P.v(lambda e, b=bb, rn=rn, C=C, G=G: e.tensor_tensor(
                s_q[0:rn, 0:C].rearrange("p (g k) -> p g k", k=64),
                s_wt[b][0:rn, 0:C].rearrange("p (g k) -> p g k", k=64),
                s_r1[0:rn, 0:G].rearrange("p g -> p g ()").broadcast_to([rn, G, 64]),
                ALU.mult))
            P.v(lambda e: e.drain())
            P.v(lambda e, rn=rn, C=C: e.memset(s_lvl[0:rn, 0:C], -1.0))
            for j in range(15):
                P.v(lambda e, j=j, rn=rn, C=C: e.tensor_scalar(
                    s_term[0:rn, 0:C], s_q[0:rn, 0:C], NF4_THR[j], NF4_STEP[j], ALU.is_gt, ALU.mult))
                P.v(lambda e, rn=rn, C=C: e.tensor_tensor(s_lvl[0:rn, 0:C], s_lvl[0:rn, 0:C], s_term[0:rn, 0:C], ALU.add))
            P.v(lambda e, rn=rn, C=C, G=G: e.tensor_tensor(
                s_lvl[0:rn, 0:C].rearrange("p (g k) -> p g k", k=64),
                s_lvl[0:rn, 0:C].rearrange("p (g k) -> p g k", k=64),
                s_dqs8[0:rn, 0:G * 8].rearrange("p (g e) -> p g e", e=8)[:, :, 0:1].broadcast_to([rn, G, 64]),
                ALU.mult))
            P.v(lambda e: e.drain())
            dq.inc()
            P.v(lambda e: e.drain().then_inc(dq.h, 1))
            if mi == 1 and ti == 0 and os.environ.get("KA_DL", "0") == "1":
                W(P.g, dq, dq.v)
                dbgs.inc(16)
                P.g(lambda e: e.dma_start(o_dlvl.ap(), s_lvl[0:128, 0:D]).then_inc(dbgs.h, 16))
                W(P.t, dbgs, dbgs.v)
            W(P.t, dq, dq.v)
            _notr = os.environ.get("KA_NOTR", "0") == "1"
            for cix in ([] if _notr else range(C // 128)):
                if tcnt >= 1:
                    W(P.t, te, te.v)
                tr.inc()
                P.t(lambda e, cix=cix, rn=rn: e.transpose(
                    p_tr[0:128, 0:rn], s_lvl[0:rn, cix * 128:(cix + 1) * 128],
                    s_ident[0:rn, 0:rn]).then_inc(tr.h, 1))
                W(P.c, tr, tr.v)
                if tcnt >= 2:
                    Swt.wait(P.c, wt_tok[tcnt - 2])
                te.inc()
                P.c(lambda e, b=tcnt % 2, rn=rn: e.activation(
                    s_wtr[b][0:128, 0:rn], p_tr[0:128, 0:rn], AF.Copy).then_inc(te.h, 1))
                wt_tok.append(Swt.issue(P.c, lambda e, owt=owt, cix=cix, r0=r0, rn=rn, b=tcnt % 2: e.dma_start(
                    owt.ap()[cix * 128:(cix + 1) * 128, r0:r0 + rn],
                    s_wtr[b][0:128, 0:rn])))
                tcnt += 1
            p3 += 1

    if _PH < 3:
        for stm in (S1, S2, Swt, Sx2, Sspl, Szt, Ssn):
            stm.wait_all(P.g)
        P.finalize()
        return nc
    tr_after_tile = []
    x2_tok = []
    x2st_n = 0
    W(P.sy, cst, cst.v)
    for tb in range(NTB):
        if tb >= 2:
            W(P.sy, lnv, 2 * (tb - 2) + 1)
        tokx = Lx.issue(P.sy, lambda e, tb=tb: e.dma_start(
            s_x[tb % 2][:], xA.ap()[tb * 128:(tb + 1) * 128, :]))
        Lx.wait(P.v, tokx)
        if tb >= 1:
            W(P.v, tr, tr_after_tile[tb - 1])
        xt = s_x[tb % 2]
        Lx.wait(P.c, tokx)
        P.c(lambda e, xt=xt: e.activation(s_term[0:128, 0:D], xt[:], AF.Copy, accum_out=s_ss[:]))
        lns.inc()
        P.c(lambda e: e.activation(s_mu[:], s_ss[:], AF.Copy, scale=float(np.float32(1.0 / D))).then_inc(lns.h, 1))
        W(P.v, lns, lns.v)
        lnv.inc()
        P.v(lambda e, xt=xt: e.tensor_scalar(s_xc[:], xt[:], s_mu[:], None, ALU.subtract).then_inc(lnv.h, 1))
        W(P.c, lnv, lnv.v)
        P.c(lambda e: e.activation(s_term[0:128, 0:D], s_xc[:], AF.Square, accum_out=s_ss[:]))
        P.c(lambda e: e.activation(s_lnt[:], s_ss[:], AF.Ln, bias=s_eps[:], scale=float(np.float32(1.0 / D))))
        lns.inc()
        P.c(lambda e: e.activation(s_rstd[:], s_lnt[:], AF.Exp, scale=-0.5).then_inc(lns.h, 1))
        W(P.v, lns, lns.v)
        lnv.inc()
        P.v(lambda e: e.tensor_scalar(s_x2t[:], s_xc[:], s_rstd[:], None, ALU.mult).then_inc(lnv.h, 1))
        W(P.t, lnv, lnv.v)
        for kc in range(16):
            W(P.t, te, te.v)
            tr.inc()
            P.t(lambda e, kc=kc: e.transpose(p_tr[:], s_x2t[:, kc * 128:(kc + 1) * 128], s_ident[:]).then_inc(tr.h, 1))
            W(P.c, tr, tr.v)
            if x2st_n >= 2:
                Sx2.wait(P.c, x2_tok[x2st_n - 2])
            te.inc()
            P.c(lambda e, b=x2st_n % 2: e.activation(s_x2st[b][:], p_tr[:], AF.Copy).then_inc(te.h, 1))
            W(P.g, te, te.v)
            x2_tok.append(Sx2.issue(P.g, lambda e, kc=kc, tb=tb, b=x2st_n % 2: e.dma_start(
                o_x2.ap()[kc * 128:(kc + 1) * 128, tb * 128:(tb + 1) * 128], s_x2st[b][:])))
            x2st_n += 1
        tr_after_tile.append(tr.v)

    if _PH < 4:
        for stm in (S1, S2, Swt, Sx2, Sspl, Szt, Ssn):
            stm.wait_all(P.g)
        P.finalize()
        return nc
    x2ld_n = 0
    x2pj_at = []
    zst_n = 0
    zt_tok = []
    spl_tok = []
    for which, WT, brow, rope, o_dst in (("k", WkT, s_bk, True, o_ktm),
                                         ("v", WvT, s_bv, False, o_vtm)):
        for nd in range(4):
            if wcast.v >= 1:
                W(P.sy, wcast, wcast.v)
            tokf = Lf.issue(P.sy, lambda e, WT=WT, nd=nd: e.dma_start(
                s_wf32[:].rearrange("p (kc c) -> p kc c", c=512),
                WT.ap()[:, nd * 512:(nd + 1) * 512].rearrange("(kc p) c -> p kc c", p=128)))
            Lf.wait(P.v, tokf)
            if wcast.v > 0:
                W(P.v, pj, pj.v)
            P.v(lambda e: e.tensor_copy(s_wb[:], s_wf32[:]))
            wcast.inc()
            P.v(lambda e: e.drain().then_inc(wcast.h, 1))
            W(P.t, wcast, wcast.v)
            for tb in range(NTB):
                bank = tb % 2
                if x2ld_n == 0:
                    Sx2.wait_all(P.sy)
                if x2ld_n >= 2:
                    W(P.sy, pj, x2pj_at[x2ld_n - 2])
                tok2 = L2.issue(P.sy, lambda e, tb=tb, b=x2ld_n % 2: e.dma_start(
                    s_x2b[b][:].rearrange("p (kc c) -> p kc c", c=128),
                    o_x2.ap()[:, tb * 128:(tb + 1) * 128].rearrange("(kc p) c -> p kc c", p=128)))
                L2.wait(P.t, tok2)
                W(P.t, ev, max(0, ev.v - 1))
                for kc in range(16):
                    P.t(lambda e, kc=kc, bank=bank, b=x2ld_n % 2: e.matmul(
                        p_mm[bank][:], s_x2b[b][:, kc * 128:(kc + 1) * 128],
                        s_wb[:, kc * 512:(kc + 1) * 512], start=(kc == 0), stop=False))
                pj.inc()
                P.t(lambda e, bank=bank, brow=brow, nd=nd: e.matmul(
                    p_mm[bank][:], s_ones_row[:], brow[0:1, nd * 512:(nd + 1) * 512],
                    start=False, stop=True).then_inc(pj.h, 1))
                x2pj_at.append(pj.v)
                x2ld_n += 1
                W(P.c, pj, pj.v)
                if len(spl_tok) >= 2:
                    Sspl.wait(P.c, spl_tok[-2])
                ev.inc()
                P.c(lambda e, bank=bank: e.activation(
                    s_kblk[bank][:], p_mm[bank][:], AF.Copy).then_inc(ev.h, 1))
                if rope:
                    W(P.v, ev, ev.v)
                    kb = s_kblk[bank]
                    cb = s_cos[:, tb * 64:(tb + 1) * 64].rearrange("p f -> p () f").broadcast_to([128, 4, 64])
                    sbb = s_sin[:, tb * 64:(tb + 1) * 64].rearrange("p f -> p () f").broadcast_to([128, 4, 64])
                    ev_v = kb[:].rearrange("p (h two f) -> p h two f", two=2, f=64)[:, :, 0, :]
                    od_v = kb[:].rearrange("p (h two f) -> p h two f", two=2, f=64)[:, :, 1, :]
                    r1v = s_ro1[:].rearrange("p (h f) -> p h f", f=64)
                    r2v = s_ro2[:].rearrange("p (h f) -> p h f", f=64)
                    P.v(lambda e, a=r1v, b=ev_v, c=sbb: e.tensor_tensor(a, b, c, ALU.mult))
                    P.v(lambda e, a=r2v, b=od_v, c=sbb: e.tensor_tensor(a, b, c, ALU.mult))
                    P.v(lambda e: e.drain())
                    P.v(lambda e, a=ev_v, c=cb: e.tensor_tensor(a, a, c, ALU.mult))
                    P.v(lambda e: e.drain())
                    P.v(lambda e, a=ev_v, b=r2v: e.tensor_tensor(a, a, b, ALU.subtract))
                    P.v(lambda e, a=od_v, c=cb: e.tensor_tensor(a, a, c, ALU.mult))
                    P.v(lambda e: e.drain())
                    P.v(lambda e, a=od_v, b=r1v: e.tensor_tensor(a, a, b, ALU.add))
                    P.v(lambda e: e.drain())
                    P.v(lambda e, kb=kb: e.tensor_scalar(s_el1[:], kb[:], 0.0, None, ALU.min))
                    rp.inc()
                    P.v(lambda e: e.drain().then_inc(rp.h, 1))
                    W(P.c, rp, rp.v)
                    lns.inc()
                    P.c(lambda e: e.activation(s_el2[:], s_el1[:], AF.Exp).then_inc(lns.h, 1))
                    W(P.v, lns, lns.v)
                    P.v(lambda e, kb=kb: e.tensor_scalar(s_el1[:], kb[:], 0.0, None, ALU.max))
                    P.v(lambda e: e.drain())
                    P.v(lambda e, kb=kb: e.tensor_tensor(kb[:], s_el1[:], s_el2[:], ALU.add))
                    rp.inc()
                    P.v(lambda e: e.drain().then_inc(rp.h, 1))
                    if tb % 2 == 0 and zs.v > 0:
                        W(P.t, zs, zs.v)
                    W(P.t, rp, rp.v)
                    pj.inc()
                    P.t(lambda e, bank=bank, tb=tb: e.matmul(
                        p_z[:], s_ones_col[:], s_kblk[bank][:],
                        start=(tb == 0), stop=(tb % 2 == 1),
                        skip_group_check=(tb >= 2)).then_inc(pj.h, 1))
                    if tb % 2 == 1:
                        sup = tb // 2
                        W(P.c, pj, pj.v)
                        if zst_n >= 2:
                            Szt.wait(P.c, zt_tok[zst_n - 2])
                        zs.inc()
                        P.c(lambda e, b=zst_n % 2: e.activation(
                            s_zrow[0:1, b * 512:(b + 1) * 512], p_z[:], AF.Copy).then_inc(zs.h, 1))
                        W(P.g, zs, zs.v)
                        zt_tok.append(Szt.issue(P.g, lambda e, sup=sup, nd=nd, b=zst_n % 2: e.dma_start(
                            o_zsnap.ap()[:, sup * D + nd * 512: sup * D + (nd + 1) * 512],
                            s_zrow[0:1, b * 512:(b + 1) * 512])))
                        zst_n += 1
                    W(P.g, rp, rp.v)
                else:
                    W(P.g, ev, ev.v)
                spl_tok.append(Sspl.issue(P.g, lambda e, o_dst=o_dst, tb=tb, nd=nd, bank=bank: e.dma_start(
                    o_dst.ap()[tb * 128:(tb + 1) * 128, nd * 512:(nd + 1) * 512],
                    s_kblk[bank][:])))

    if _PH < 5:
        for stm in (S1, S2, Swt, Sx2, Sspl, Szt, Ssn):
            stm.wait_all(P.g)
        P.finalize()
        return nc
    Sspl.wait_all(P.sy)
    sn_tok = []
    _NHG = int(os.environ.get("KA_HG", "4"))
    _NS5 = int(os.environ.get("KA_NS", str(NSUP)))
    for hg in range(_NHG):
        heads = list(range(hg * 4, hg * 4 + 4))
        for s in range(_NS5):
            pair = (s % 2) * 2
            if hg > 0 and s < 2:
                W(P.sy, sn, hg * 32)
            elif s >= 2:
                W(P.sy, sn, hg * 32 + (s - 1) * 4)
            kv_toks = [
                Lkv.issue(P.sy, lambda e, s=s, pair=pair: e.dma_start(
                    s_kv[pair][:], o_ktm.ap()[s * 256:s * 256 + 128, :])),
                Lkv.issue(P.sy, lambda e, s=s, pair=pair: e.dma_start(
                    s_kv[pair + 1][:], o_ktm.ap()[s * 256 + 128:s * 256 + 256, :])),
            ]
            kv_toks.append(Lkv.issue(P.sy, lambda e, s=s, pair=pair: e.dma_start(
                s_vv[pair][:], o_vtm.ap()[s * 256:s * 256 + 128, :])))
            kv_toks.append(Lkv.issue(P.sy, lambda e, s=s, pair=pair: e.dma_start(
                s_vv[pair + 1][:], o_vtm.ap()[s * 256 + 128:s * 256 + 256, :])))
            for tk in kv_toks:
                Lkv.wait(P.t, tk)
            for hi, h in enumerate(heads):
                if s == 0 and hg > 0:
                    W(P.t, sne, (hg - 1) * 32 + 29 + hi)
                elif s > 0:
                    W(P.t, sne, sne.v - 3)
                P.t(lambda e, h=h, hi=hi, pair=pair, s=s: e.matmul(
                    p_s[hi][:], s_kv[pair][:, h * 128:(h + 1) * 128],
                    s_vv[pair][:, h * 128:(h + 1) * 128],
                    start=(s == 0), stop=False, skip_group_check=(s > 0)))
                sn.inc()
                P.t(lambda e, h=h, hi=hi, pair=pair, s=s: e.matmul(
                    p_s[hi][:], s_kv[pair + 1][:, h * 128:(h + 1) * 128],
                    s_vv[pair + 1][:, h * 128:(h + 1) * 128],
                    start=False, stop=True, skip_group_check=(s > 0)).then_inc(sn.h, 1))
                _M5 = os.environ.get("KA_PA5_MODE", "full")
                if _M5 == "mm":
                    sne.inc()
                    P.c(lambda e: e.engine_nop().then_inc(sne.h, 1))
                    continue
                _EVE = os.environ.get("KA_EVENG", "c")
                PE_ = P.v if _EVE == "v" else P.c
                W(PE_, sn, sn.v)
                if sne.v >= 2:
                    Ssn.wait(PE_, sn_tok[sne.v - 2])
                sne.inc()
                if _EVE == "v":
                    P.v(lambda e, hi=hi, b=(sne.v - 1) % 2: e.tensor_copy(s_x2st[b][:], p_s[hi][:]).then_inc(sne.h, 1))
                else:
                    P.c(lambda e, hi=hi, b=(sne.v - 1) % 2: e.activation(s_x2st[b][:], p_s[hi][:], AF.Copy).then_inc(sne.h, 1))
                if _M5 == "ev":
                    sn_tok.append((Ssn.sems[0], 0))
                    continue
                W(P.g, sne, sne.v)
                sn_tok.append(Ssn.issue(P.g, lambda e, h=h, s=s, b=(sne.v - 1) % 2: e.dma_start(
                    o_ssnap.ap()[(h * NSUP + s) * DK:(h * NSUP + s + 1) * DK, :],
                    s_x2st[b][:])))

    for stm in (Swt, Sx2, Sspl, Szt, Ssn, S1, S2):
        stm.wait_all(P.g)

    P.finalize()
    return nc


def _prep_inputs_A(inputs):
    x = np.asarray(inputs["L0_attn_x"])[0]
    Wk = np.asarray(inputs["L0_attn_Wk"])
    Wv = np.asarray(inputs["L0_attn_Wv"])
    bk = np.asarray(inputs["L0_attn_bk"])
    bv = np.asarray(inputs["L0_attn_bv"])
    w1 = np.asarray(inputs["L0_ffn_weight1"])
    w2 = np.asarray(inputs["L0_ffn_weight2"])
    w3 = np.asarray(inputs["L0_ffn_weight3"])
    WkT = np.ascontiguousarray(Wk.T[:, PERM], np.float32)
    WvT = np.ascontiguousarray(Wv.T, np.float32)
    bkP = np.ascontiguousarray(bk[PERM], np.float32).reshape(1, D)
    bvv = np.ascontiguousarray(bv, np.float32).reshape(1, D)
    ident = np.eye(128, dtype=bf16)
    maps = []
    for c in range(NCORES):
        cos, sin = _rope_tables(c)
        maps.append({
            "xA": np.ascontiguousarray(x[c * TPC:(c + 1) * TPC], np.float32),
            "WkT": WkT, "WvT": WvT, "bk": bkP, "bv": bvv,
            "cosA": cos, "sinA": sin, "ident": ident,
            "w1f": w1, "w2f": w2, "w3f": w3,
            "w1s": np.ascontiguousarray(w1[c * FS1:(c + 1) * FS1]),
            "w2s": np.ascontiguousarray(w2[c * FS1:(c + 1) * FS1]),
            "w3s": np.ascontiguousarray(w3[c * FS3:(c + 1) * FS3]),
        })
    return maps


def _rope_tables(core):
    pos = np.arange(TPC, dtype=np.float64) + core * TPC
    inv = 1.0 / (10000.0 ** (np.arange(0, DK, 2, dtype=np.float64) / DK))
    ang = pos[:, None] * inv[None, :]
    return (np.ascontiguousarray(np.cos(ang), np.float32),
            np.ascontiguousarray(np.sin(ang), np.float32))


def build_B():
    nc = bass.Bass("TRN2", target_bir_lowering=False, detect_race_conditions=False)
    f = nc.dram_tensor

    xB = f("xB", [TPC, D], F32, kind="ExternalInput")
    x2r = f("x2r", [D, TPC], BF16, kind="ExternalInput")
    ktm = f("ktm", [TPC, D], BF16, kind="ExternalInput")
    vtm = f("vtm", [TPC, D], BF16, kind="ExternalInput")
    snapO = f("snapO", [H * NSUP * DK, DK], BF16, kind="ExternalInput")
    sprev = f("sprev", [7 * H * DK, DK], BF16, kind="ExternalInput")
    zsnC = f("zsnC", [128, NSUP * 16], F32, kind="ExternalInput")
    zprC = f("zprC", [128, 32 * 7], F32, kind="ExternalInput")
    WqT = f("WqT", [D, D], F32, kind="ExternalInput")
    WgT = f("WgT", [D, D], F32, kind="ExternalInput")
    WoT = f("WoT", [D, D], F32, kind="ExternalInput")
    bq = f("bq", [1, D], F32, kind="ExternalInput")
    bg = f("bg", [1, D], F32, kind="ExternalInput")
    bo = f("bo", [1, D], F32, kind="ExternalInput")
    w1T = f("w1T", [D, DFF], BF16, kind="ExternalInput")
    w2T = f("w2T", [D, DFF], BF16, kind="ExternalInput")
    w3T = f("w3T", [DFF, D], BF16, kind="ExternalInput")
    cosA = f("cosA", [TPC, 64], F32, kind="ExternalInput")
    sinA = f("sinA", [TPC, 64], F32, kind="ExternalInput")
    ident = f("ident", [128, 128], BF16, kind="ExternalInput")

    o_out = f("o_out", [TPC, D], F32, kind="ExternalOutput")

    _dbg = {}
    if os.environ.get("KB_DBG", "0") == "1":
        _dbg = {"kind": "ExternalOutput"}
    d_qfm = f("d_qfm", [D, TPC], BF16, **_dbg)
    d_gfm = f("d_gfm", [D, TPC], BF16, **_dbg)
    d_ap = f("d_ap", [TPC, D], F32, **_dbg)
    d_x1 = f("d_x1", [TPC, D], F32, **_dbg)
    d_h = f("d_h", [DFF, TPC], BF16, **_dbg)
    d_y = f("d_y", [TPC, D], BF16, **_dbg)

    sb = nc.alloc_sbuf_tensor
    park = [sb(f"park{i}", [128, TPC], BF16) for i in range(16)]
    KV = sb("KV", [128, 16384], BF16)
    A3 = sb("A3", [128, 8192], F32)
    WB = sb("WB", [128, 8192], BF16)
    s_Sst = sb("s_Sst", [128, H * 128], F32)
    s_kfm = sb("s_kfm", [128, 32 * 128], BF16)
    s_cos = sb("s_cos", [128, NTB * 64], F32)
    s_sin = sb("s_sin", [128, NTB * 64], F32)
    s_ktn = [KV[0:128, i * 2048:(i + 1) * 2048] for i in range(4)]
    s_vtn = [KV[0:128, 8192 + i * 2048:8192 + (i + 1) * 2048] for i in range(4)]
    s_spr = [KV[0:128, b * 2048:(b + 1) * 2048] for b in range(2)]
    s_w1c = [KV[0:128, b * 4096:b * 4096 + 2048] for b in range(2)]
    s_w2c = [KV[0:128, b * 4096 + 2048:b * 4096 + 4096] for b in range(2)]
    s_hst = [KV[0:128, 0:5632], KV[0:128, 8192:8192 + 5632]]
    s_wf32 = A3[0:128, 0:8192]
    s_wgf = A3[0:128, 0:2048]
    s_ap2 = [A3[0:128, b * 2048:(b + 1) * 2048] for b in range(2)]
    s_x1t = [A3[0:128, 4096 + b * 2048:4096 + (b + 1) * 2048] for b in range(2)]
    s_sil = [A3[0:128, b * 512:(b + 1) * 512] for b in range(2)]
    s_x1b = [A3[0:128, 4096 + b * 2048:4096 + (b + 1) * 2048] for b in range(2)]
    s_yc = A3[0:128, 2048:4096]
    s_ost = A3[0:128, 0:2048]
    s_wb = WB[0:128, 0:8192]
    s_wgc = WB[0:128, 0:2048]
    s_junk = WB[0:128, 0:2048]
    s_x2pt = WB[0:128, 2048:4096]
    s_yt = [WB[0:128, 4096 + b * 2048:4096 + (b + 1) * 2048] for b in range(2)]
    s_apt = [s_Sst[0:128, b * 512:(b + 1) * 512] for b in range(2)]
    s_ident = sb("s_ident", [128, 128], BF16)
    s_ones_row = sb("s_ones_row", [1, 128], BF16)
    s_ones_col = sb("s_ones_col", [128, 1], BF16)
    s_eps = sb("s_eps", [128, 1], F32)
    s_eps6 = sb("s_eps6", [1, 1], F32)
    s_bq = sb("s_bq", [1, D], BF16)
    s_bo = sb("s_bo", [1, D], BF16)
    s_bgr = sb("s_bgr", [1, D], BF16)
    s_ones5 = sb("s_ones5", [1, 512], BF16)
    s_b32 = A3[0:1, 0:D]
    s_zpr = sb("s_zpr", [128, 32 * 7], F32)
    s_zsn = sb("s_zsn", [128, NSUP * 16], F32)
    s_zst = sb("s_zst", [128, 32], F32)
    s_zu = [sb(f"s_zu{b}", [128, 2], BF16) for b in range(2)]
    s_kblk = [sb(f"s_kblk{i}", [128, 512], BF16) for i in range(2)]
    s_ro1 = sb("s_ro1", [128, 256], BF16)
    s_ro2 = sb("s_ro2", [128, 256], BF16)
    s_el1 = sb("s_el1", [128, 512], BF16)
    s_el2 = sb("s_el2", [128, 512], BF16)
    s_qst = [sb(f"s_qst{i}", [128, 512], BF16) for i in range(2)]
    s_gst = s_qst
    s_qs = [sb(f"s_qs{i}", [128, 256], BF16) for i in range(2)]
    s_gs = [sb(f"s_gs{i}", [128, 256], BF16) for i in range(2)]
    s_sn = [sb(f"s_sn{i}", [128, 128], BF16) for i in range(2)]
    s_su = [sb(f"s_su{i}", [128, 128], BF16) for i in range(2)]
    s_at = [sb(f"s_at{i}", [128, 512], BF16) for i in range(2)]
    s_den = [sb(f"s_den{i}", [1, 256], F32) for i in range(2)]
    s_dent = [sb(f"s_dent{i}", [1, 256], F32) for i in range(2)]
    s_recb = [sb(f"s_recb{i}", [1, 256], BF16) for i in range(2)]
    s_rb = [sb(f"s_rb{i}", [128, 256], F32) for i in range(2)]
    s_t1 = [sb(f"s_t1{i}", [128, 256], F32) for i in range(2)]
    s_hfm = s_kblk
    s_yst = s_qst
    s_x2st = [s_ro1[0:128, 0:128], s_ro1[0:128, 128:256]]
    s_sum = sb("s_sum", [128, 1], F32)
    s_sq = sb("s_sq", [128, 1], F32)
    s_mu = sb("s_mu", [128, 1], F32)
    s_lnt = sb("s_lnt", [128, 1], F32)
    s_rstd = sb("s_rstd", [128, 1], F32)

    ps = nc.alloc_psum_tensor
    b = [ps(f"bk{i}", [128, 512], F32) for i in range(6)]
    p_tr = ps("p_tr", [128, 128], BF16)
    p_sm = ps("p_sm", [1, 512], F32)

    P = Prog(nc)
    pe = Sem(nc, "pe")
    ac = Sem(nc, "ac")
    dv = Sem(nc, "dv")
    gp = Sem(nc, "gp")

    Lc = DStream(nc, "Lc", 1)
    Lpk = DStream(nc, "Lpk", 1)
    Lw = DStream(nc, "Lw", 1)
    Lkv = DStream(nc, "Lkv", 8)
    Lq = DStream(nc, "Lq", 2)
    Lg = DStream(nc, "Lg", 2)
    Lsn = DStream(nc, "Lsn", 2)
    Lap = DStream(nc, "Lap", 2)
    Lx = DStream(nc, "Lx", 2)
    L12 = DStream(nc, "L12", 2)
    L3s = DStream(nc, "L3s", 1)
    Lh = DStream(nc, "Lh", 2)
    Ly = DStream(nc, "Ly", 2)
    Lx1 = DStream(nc, "Lx1", 2)
    Sq = DStream(nc, "Sq", 2)
    Sg = DStream(nc, "Sg", 2)
    Sap = DStream(nc, "Sap", 2)
    Sx1 = DStream(nc, "Sx1", 2)
    Sh = DStream(nc, "Sh", 2)
    Sy = DStream(nc, "Sy", 2)
    Sout = DStream(nc, "Sout", 2)

    def W(P_eng, sem, val):
        P_eng(lambda e, s=sem.h, w=val: e.wait_ge(s, w))

    Lc.issue(P.sy, lambda e: e.dma_start(s_ident[:], ident.ap()))
    Lc.issue(P.sy, lambda e: e.dma_start(s_cos[:].rearrange("p (a f) -> p a f", f=64),
                                         cosA.ap().rearrange("(a p) f -> p a f", p=128)))
    Lc.issue(P.sy, lambda e: e.dma_start(s_sin[:].rearrange("p (a f) -> p a f", f=64),
                                         sinA.ap().rearrange("(a p) f -> p a f", p=128)))
    Lc.issue(P.sy, lambda e: e.dma_start(s_zpr[:], zprC.ap()))
    Lc.issue(P.sy, lambda e: e.dma_start(s_zsn[:], zsnC.ap()))
    Lc.issue(P.sy, lambda e: e.dma_start(s_b32[:], bq.ap()))
    Lc.wait_all(P.v)
    P.v(lambda e: e.memset(s_ones_row[:], 1.0))
    P.v(lambda e: e.memset(s_ones_col[:], 1.0))
    P.v(lambda e: e.memset(s_eps[:], float(np.float32(EPS_LN))))
    P.v(lambda e: e.memset(s_eps6[:], 1e-6))
    P.v(lambda e: e.tensor_copy(s_bq[:], s_b32[:]))
    P.v(lambda e: e.drain())
    P.v(lambda e: e.tensor_reduce(s_zst[:], s_zpr[:].rearrange("p (h r) -> p h r", r=7),
                                  AX.X, ALU.add))
    dv.inc()
    P.v(lambda e, w=dv.v: e.drain().then_inc(dv.h, 1))
    _dv_bq = dv.v
    W(P.sy, dv, _dv_bq)
    Lc.issue(P.sy, lambda e: e.dma_start(s_b32[:], bg.ap()))
    Lc.wait_all(P.v)
    W(P.v, dv, _dv_bq)
    P.v(lambda e: e.memset(s_ones5[:], 1.0))
    P.v(lambda e: e.tensor_copy(s_bgr[:], s_b32[:]))
    dv.inc()
    P.v(lambda e: e.drain().then_inc(dv.h, 1))
    _dv_bg = dv.v
    W(P.sy, dv, _dv_bg)
    Lc.issue(P.sy, lambda e: e.dma_start(s_b32[:], bo.ap()))
    Lc.wait_all(P.v)
    W(P.v, dv, _dv_bg)
    P.v(lambda e: e.tensor_copy(s_bo[:], s_b32[:]))
    P.v(lambda e: e.memset(s_Sst[:], 0.0))
    dv.inc()
    P.v(lambda e: e.drain().then_inc(dv.h, 1))
    _dv_b0c = dv.v
    spr_dv = []
    for r in range(7):
        if r >= 2:
            W(P.sy, dv, spr_dv[r - 2])
        tok = Lsn.issue(P.sy, lambda e, r=r, bb=r % 2: e.dma_start(
            s_spr[bb].rearrange("p (h c) -> p h c", c=128),
            sprev.ap()[r * H * DK:(r + 1) * H * DK, :].rearrange("(h p) c -> p h c", p=128)))
        Lsn.wait(P.v, tok)
        P.v(lambda e, bb=r % 2: e.tensor_tensor(s_Sst[:], s_Sst[:], s_spr[bb], ALU.add))
        dv.inc()
        P.v(lambda e: e.drain().then_inc(dv.h, 1))
        spr_dv.append(dv.v)
    B0_DONE = dv.v

    for kc in range(16):
        Lpk.issue(P.sy, lambda e, kc=kc: e.dma_start(
            park[kc][:], x2r.ap()[kc * 128:(kc + 1) * 128, :]))
    Lpk.wait_all(P.t)

    ev_at, tr_at, sq_tok = [], [], []
    qwc_at = []
    for nd in range(4):
        if nd > 0:
            W(P.sy, dv, qwc_at[-1])
        else:
            W(P.sy, dv, _dv_b0c)
        tokf = Lw.issue(P.sy, lambda e, nd=nd: e.dma_start(
            s_wf32.rearrange("p (kc c) -> p kc c", c=512),
            WqT.ap()[:, nd * 512:(nd + 1) * 512].rearrange("(kc p) c -> p kc c", p=128)))
        Lw.wait(P.v, tokf)
        if nd > 0:
            W(P.v, pe, pe.v)
        P.v(lambda e: e.tensor_copy(s_wb[:], s_wf32[:]))
        dv.inc()
        P.v(lambda e: e.drain().then_inc(dv.h, 1))
        qwc_at.append(dv.v)
        W(P.t, dv, dv.v)
        for tb in range(NTB):
            n = nd * NTB + tb
            bank = tb % 2
            if n >= 2:
                W(P.t, ac, ev_at[n - 2])
            for kc in range(16):
                P.t(lambda e, kc=kc, tb=tb, bank=bank: e.matmul(
                    b[4 + bank][:], park[kc][:, tb * 128:(tb + 1) * 128],
                    s_wb[:, kc * 512:(kc + 1) * 512], start=(kc == 0), stop=False))
            pe.inc()
            P.t(lambda e, bank=bank, nd=nd: e.matmul(
                b[4 + bank][:], s_ones_row[:], s_bq[0:1, nd * 512:(nd + 1) * 512],
                start=False, stop=True).then_inc(pe.h, 1))
            W(P.c, pe, pe.v)
            if n >= 2:
                W(P.c, pe, tr_at[n - 2])
            ac.inc()
            P.c(lambda e, bank=bank: e.activation(
                s_kblk[bank][:], b[4 + bank][:], AF.Copy).then_inc(ac.h, 1))
            ev_at.append(ac.v)
            W(P.v, ac, ac.v)
            kb = s_kblk[bank]
            cb = s_cos[:, tb * 64:(tb + 1) * 64].rearrange("p f -> p () f").broadcast_to([128, 4, 64])
            sbb = s_sin[:, tb * 64:(tb + 1) * 64].rearrange("p f -> p () f").broadcast_to([128, 4, 64])
            ev_v = kb[:].rearrange("p (h two f) -> p h two f", two=2, f=64)[:, :, 0, :]
            od_v = kb[:].rearrange("p (h two f) -> p h two f", two=2, f=64)[:, :, 1, :]
            r1v = s_ro1[:].rearrange("p (h f) -> p h f", f=64)
            r2v = s_ro2[:].rearrange("p (h f) -> p h f", f=64)
            P.v(lambda e, a=r1v, x=ev_v, c=sbb: e.tensor_tensor(a, x, c, ALU.mult))
            P.v(lambda e, a=r2v, x=od_v, c=sbb: e.tensor_tensor(a, x, c, ALU.mult))
            P.v(lambda e: e.drain())
            P.v(lambda e, x=ev_v, c=cb: e.tensor_tensor(x, x, c, ALU.mult))
            P.v(lambda e: e.drain())
            P.v(lambda e, x=ev_v, a=r2v: e.tensor_tensor(x, x, a, ALU.subtract))
            P.v(lambda e, x=od_v, c=cb: e.tensor_tensor(x, x, c, ALU.mult))
            P.v(lambda e: e.drain())
            P.v(lambda e, x=od_v, a=r1v: e.tensor_tensor(x, x, a, ALU.add))
            P.v(lambda e: e.drain())
            P.v(lambda e, kb=kb: e.tensor_scalar(s_el1[:], kb[:], 0.0, None, ALU.min))
            dv.inc()
            P.v(lambda e: e.drain().then_inc(dv.h, 1))
            W(P.c, dv, dv.v)
            ac.inc()
            P.c(lambda e: e.activation(s_el2[:], s_el1[:], AF.Exp).then_inc(ac.h, 1))
            W(P.v, ac, ac.v)
            P.v(lambda e, kb=kb: e.tensor_scalar(s_el1[:], kb[:], 0.0, None, ALU.max))
            P.v(lambda e: e.drain())
            dv.inc()
            P.v(lambda e, kb=kb: e.tensor_tensor(kb[:], s_el1[:], s_el2[:], ALU.add).then_inc(dv.h, 1))
            W(P.t, dv, dv.v)
            stg = s_qst[tb % 2]
            for j in range(4):
                if tr_at or j > 0:
                    W(P.t, ac, ac.v)
                pe.inc()
                P.t(lambda e, j=j, bank=bank: e.transpose(
                    p_tr[:], s_kblk[bank][:, j * 128:(j + 1) * 128], s_ident[:]).then_inc(pe.h, 1))
                W(P.c, pe, pe.v)
                if j == 0 and n >= 2:
                    Sq.wait(P.c, sq_tok[n - 2])
                ac.inc()
                P.c(lambda e, j=j, stg=stg: e.activation(
                    stg[:, j * 128:(j + 1) * 128], p_tr[:], AF.Copy).then_inc(ac.h, 1))
            tr_at.append(pe.v)
            W(P.g, ac, ac.v)
            sq_tok.append(Sq.issue(P.g, lambda e, nd=nd, tb=tb, stg=stg: e.dma_start(
                d_qfm.ap()[nd * 512:(nd + 1) * 512, tb * 128:(tb + 1) * 128].rearrange(
                    "(j p) c -> p j c", p=128),
                stg[:].rearrange("p (j c) -> p j c", c=128))))
    PJ_Q_END = pe.v
    QWC_END = qwc_at[-1]

    gev_at, sg_tok = [], []
    gwc_at = []
    for hc in range(16):
        if hc == 0:
            W(P.sy, dv, QWC_END)
        else:
            W(P.sy, dv, gwc_at[-1])
        tokg = Lw.issue(P.sy, lambda e, hc=hc: e.dma_start(
            s_wgf.rearrange("p (kc c) -> p kc c", c=128),
            WgT.ap()[:, hc * 128:(hc + 1) * 128].rearrange("(kc p) c -> p kc c", p=128)))
        Lw.wait(P.v, tokg)
        W(P.v, pe, pe.v)
        W(P.v, dv, _dv_bg)
        P.v(lambda e: e.tensor_copy(s_wgc[:], s_wgf[:]))
        dv.inc()
        P.v(lambda e: e.drain().then_inc(dv.h, 1))
        gwc_at.append(dv.v)
        W(P.t, dv, dv.v)
        for tc in range(4):
            m = hc * 4 + tc
            bank = m % 2
            if m >= 2:
                W(P.t, ac, gev_at[m - 2])
            for kc in range(16):
                P.t(lambda e, kc=kc, tc=tc, bank=bank: e.matmul(
                    b[4 + bank][:], s_wgc[:, kc * 128:(kc + 1) * 128],
                    park[kc][:, tc * 512:(tc + 1) * 512],
                    start=(kc == 0), stop=False))
            pe.inc()
            P.t(lambda e, bank=bank, hc=hc: e.matmul(
                b[4 + bank][:], s_bgr[0:1, hc * 128:(hc + 1) * 128], s_ones5[:],
                start=False, stop=True).then_inc(pe.h, 1))
            W(P.c, pe, pe.v)
            if m >= 2:
                Sg.wait(P.c, sg_tok[m - 2])
            else:
                Sq.wait(P.c, sq_tok[62 + m])
            ac.inc()
            P.c(lambda e, bank=bank: e.activation(
                s_gst[bank][:], b[4 + bank][:], AF.Sigmoid).then_inc(ac.h, 1))
            gev_at.append(ac.v)
            W(P.g, ac, ac.v)
            sg_tok.append(Sg.issue(P.g, lambda e, hc=hc, tc=tc, bank=bank: e.dma_start(
                d_gfm.ap()[hc * 128:(hc + 1) * 128, tc * 512:(tc + 1) * 512], s_gst[bank][:])))
    PJ_G_END = pe.v
    GWC_END = gwc_at[-1]

    ktr_end, pjn_end, atm_end = {}, {}, {}
    pjn_at, go_at, suv_at, aev_at = {}, {}, {}, {}
    dnv_at, denv_at, rcp_at = {}, {}, {}
    atm_at = {}
    W(P.t, ac, ac.v)
    P.v(lambda e: e.memset(s_at[0][:], 0.0))
    P.v(lambda e: e.memset(s_at[1][:], 0.0))
    dv.inc()
    P.v(lambda e: e.drain().then_inc(dv.h, 1))
    AT_MS = dv.v
    for s in range(NSUP):
        pair = (s % 2) * 2
        if s == 0:
            Sq.wait_all(P.sy)
            Sg.wait_all(P.sy)
            W(P.sy, dv, B0_DONE)
        if s >= 2:
            W(P.sy, pe, ktr_end[s - 2])
            W(P.sy, pe, pjn_end[s - 2])
        kv_toks = [
            Lkv.issue(P.sy, lambda e, s=s, pair=pair: e.dma_start(
                s_ktn[pair], ktm.ap()[s * 256:s * 256 + 128, :])),
            Lkv.issue(P.sy, lambda e, s=s, pair=pair: e.dma_start(
                s_ktn[pair + 1], ktm.ap()[s * 256 + 128:s * 256 + 256, :])),
            Lkv.issue(P.sy, lambda e, s=s, pair=pair: e.dma_start(
                s_vtn[pair], vtm.ap()[s * 256:s * 256 + 128, :])),
            Lkv.issue(P.sy, lambda e, s=s, pair=pair: e.dma_start(
                s_vtn[pair + 1], vtm.ap()[s * 256 + 128:s * 256 + 256, :])),
        ]
        Lkv.wait(P.t, kv_toks[0])
        Lkv.wait(P.t, kv_toks[1])
        if s >= 1:
            W(P.t, pe, atm_end[s - 1])
        for hh in range(32):
            W(P.t, ac, ac.v)
            pe.inc()
            P.t(lambda e, hh=hh, pair=pair: e.transpose(
                p_tr[:], s_ktn[pair + (hh % 2)][:, (hh // 2) * 128:(hh // 2 + 1) * 128],
                s_ident[:]).then_inc(pe.h, 1))
            W(P.c, pe, pe.v)
            ac.inc()
            P.c(lambda e, hh=hh: e.activation(
                s_kfm[:, ((hh // 2) * 2 + (hh % 2)) * 128:((hh // 2) * 2 + (hh % 2) + 1) * 128],
                p_tr[:], AF.Copy).then_inc(ac.h, 1))
        ktr_end[s] = pe.v
        KT_EV = ac.v
        Lkv.wait(P.t, kv_toks[2])
        Lkv.wait(P.t, kv_toks[3])
        for h in range(H):
            n = s * H + h
            bq_ = n % 2
            if n >= 2:
                W(P.sy, pe, pjn_at[n - 2])
                W(P.sy, gp, go_at[n - 2])
                W(P.sy, dv, suv_at[n - 2])
            tq = Lq.issue(P.sy, lambda e, h=h, s=s, b2=bq_: e.dma_start(
                s_qs[b2][:], d_qfm.ap()[h * 128:(h + 1) * 128, s * 256:(s + 1) * 256]))
            tg = Lg.issue(P.sy, lambda e, h=h, s=s, b2=bq_: e.dma_start(
                s_gs[b2][:], d_gfm.ap()[h * 128:(h + 1) * 128, s * 256:(s + 1) * 256]))
            if s > 0:
                tsn = Lsn.issue(P.sy, lambda e, h=h, s=s, b2=bq_: e.dma_start(
                    s_sn[b2][:], snapO.ap()[(h * NSUP + s - 1) * DK:(h * NSUP + s) * DK, :]))
            if s > 0:
                Lsn.wait(P.v, tsn)
            if n >= 2:
                W(P.v, pe, pjn_at[n - 2])
            if s == 0:
                W(P.v, dv, B0_DONE)
                P.v(lambda e, h=h, b2=bq_: e.tensor_copy(s_su[b2][:], s_Sst[:, h * 128:(h + 1) * 128]))
                P.v(lambda e, h=h, b2=bq_: e.tensor_copy(s_zu[b2][:, 0:2], s_zst[:, h:h + 2]))
            else:
                P.v(lambda e, h=h, b2=bq_: e.tensor_tensor(
                    s_su[b2][:], s_Sst[:, h * 128:(h + 1) * 128], s_sn[b2][:], ALU.add))
                P.v(lambda e, h=h, s=s, b2=bq_: e.tensor_tensor(
                    s_zu[b2][:, 0:2], s_zst[:, h:h + 2],
                    s_zsn[:, (s - 1) * 16 + h:(s - 1) * 16 + h + 2], ALU.add))
            dv.inc()
            P.v(lambda e: e.drain().then_inc(dv.h, 1))
            suv_at[n] = dv.v
            if n >= 1:
                W(P.t, ac, aev_at[n - 1])
            Lq.wait(P.t, tq)
            for half, r0, c0 in ((0, 0, 0), (0, 64, 64), (1, 0, 128), (1, 64, 192)):
                pe.inc()
                P.t(lambda e, h=h, half=half, r0=r0, c0=c0, b2=bq_: e.matmul(
                    b[half][r0:r0 + 64, c0:256],
                    s_kfm[:, (h * 2 + half) * 128 + r0:(h * 2 + half) * 128 + r0 + 64],
                    s_qs[b2][:, c0:256], start=True, stop=True).then_inc(pe.h, 1))
            atm_at[n] = pe.v
            W(P.c, pe, atm_at[n])
            if n < 2:
                W(P.c, dv, AT_MS)
            if n >= 2:
                W(P.c, pe, pjn_at[n - 2])
            for half, r0, c0 in ((0, 0, 0), (0, 64, 64), (1, 0, 128), (1, 64, 192)):
                ac.inc()
                P.c(lambda e, half=half, r0=r0, c0=c0, b2=bq_: e.activation(
                    s_at[b2][r0:r0 + 64, half * 256 + c0:(half + 1) * 256],
                    b[half][r0:r0 + 64, c0:256], AF.Copy).then_inc(ac.h, 1))
            aev_at[n] = ac.v
            W(P.t, ac, aev_at[n])
            W(P.t, dv, suv_at[n])
            if n >= 1:
                W(P.t, gp, go_at[n - 1])
                W(P.t, ac, denv_at[n - 1])
            P.t(lambda e, b2=bq_: e.matmul(b[2][:, 0:256], s_su[b2][:], s_qs[b2][:],
                                           start=True, stop=False))
            P.t(lambda e, pair=pair, h=h, b2=bq_: e.matmul(
                b[2][:, 0:256], s_vtn[pair][:, h * 128:(h + 1) * 128], s_at[b2][:, 0:256],
                start=False, stop=False))
            pe.inc()
            P.t(lambda e, pair=pair, h=h, b2=bq_: e.matmul(
                b[2][:, 0:256], s_vtn[pair + 1][:, h * 128:(h + 1) * 128], s_at[b2][:, 256:512],
                start=False, stop=True).then_inc(pe.h, 1))
            P.t(lambda e, b2=bq_: e.matmul(p_sm[0:1, 0:256], s_zu[b2][:, 0:1], s_qs[b2][:],
                                           start=True, stop=False))
            P.t(lambda e, b2=bq_: e.matmul(p_sm[0:1, 0:256], s_ones_col[:], s_at[b2][:, 0:256],
                                           start=False, stop=False))
            pe.inc()
            P.t(lambda e, b2=bq_: e.matmul(p_sm[0:1, 0:256], s_ones_col[:], s_at[b2][:, 256:512],
                                           start=False, stop=True).then_inc(pe.h, 1))
            pjn_at[n] = pe.v
            W(P.c, pe, pjn_at[n])
            if n >= 2:
                W(P.c, dv, rcp_at[n - 2])
            ac.inc()
            P.c(lambda e, b2=bq_: e.activation(
                s_den[b2][:], p_sm[0:1, 0:256], AF.Copy, bias=1e-6).then_inc(ac.h, 1))
            denv_at[n] = ac.v
            W(P.v, ac, denv_at[n])
            P.v(lambda e, b2=bq_: e.reciprocal(s_dent[b2][:], s_den[b2][:]))
            P.v(lambda e: e.drain())
            dv.inc()
            P.v(lambda e, b2=bq_: e.tensor_copy(s_recb[b2][:], s_dent[b2][:]).then_inc(dv.h, 1))
            rcp_at[n] = dv.v
            W(P.t, dv, rcp_at[n])
            if n >= 1:
                W(P.t, dv, dnv_at[n - 1])
            pe.inc()
            P.t(lambda e, b2=bq_: e.matmul(b[3][:, 0:256], s_ones_row[:], s_recb[b2][:],
                                           start=True, stop=True).then_inc(pe.h, 1))
            pb_at = pe.v
            W(P.v, pe, pb_at)
            if n >= 2:
                W(P.v, gp, go_at[n - 2])
            P.v(lambda e, b2=bq_: e.tensor_copy(s_rb[b2][:], b[3][:, 0:256]))
            P.v(lambda e: e.drain())
            dv.inc()
            P.v(lambda e, b2=bq_: e.tensor_tensor(
                s_t1[b2][:], b[2][:, 0:256], s_rb[b2][:], ALU.mult).then_inc(dv.h, 1))
            dnv_at[n] = dv.v
            W(P.g, dv, dnv_at[n])
            Lg.wait(P.g, tg)
            gp.inc()
            P.g(lambda e, h=h, s=s, b2=bq_: e.tensor_tensor(
                park[h][:, s * 256:(s + 1) * 256], s_t1[b2][:], s_gs[b2][:],
                ALU.mult).then_inc(gp.h, 1))
            go_at[n] = gp.v
        atm_end[s] = atm_at[s * H + H - 1]
        pjn_end[s] = pjn_at[s * H + H - 1]
    SCAN_DONE = go_at[NSUP * H - 1]
    SUV_END = suv_at[NSUP * H - 1]
    PJN_END = pjn_end[NSUP - 1]

    oev_at, sap_tok, owc_at = [], [], []
    W(P.t, gp, SCAN_DONE)
    for nd in range(4):
        if nd == 0:
            W(P.sy, dv, GWC_END)
        else:
            W(P.sy, dv, owc_at[-1])
        tokf = Lw.issue(P.sy, lambda e, nd=nd: e.dma_start(
            s_wf32.rearrange("p (kc c) -> p kc c", c=512),
            WoT.ap()[:, nd * 512:(nd + 1) * 512].rearrange("(kc p) c -> p kc c", p=128)))
        Lw.wait(P.v, tokf)
        W(P.v, pe, pe.v)
        P.v(lambda e: e.tensor_copy(s_wb[:], s_wf32[:]))
        dv.inc()
        P.v(lambda e: e.drain().then_inc(dv.h, 1))
        owc_at.append(dv.v)
        W(P.t, dv, dv.v)
        for tc in range(NTB):
            p2 = nd * NTB + tc
            bank = tc % 2
            if p2 >= 2:
                W(P.t, ac, oev_at[p2 - 2])
            for kc in range(16):
                P.t(lambda e, kc=kc, tc=tc, bank=bank: e.matmul(
                    b[4 + bank][:], park[kc][:, tc * 128:(tc + 1) * 128],
                    s_wb[:, kc * 512:(kc + 1) * 512], start=(kc == 0), stop=False))
            pe.inc()
            P.t(lambda e, bank=bank, nd=nd: e.matmul(
                b[4 + bank][:], s_ones_row[:], s_bo[0:1, nd * 512:(nd + 1) * 512],
                start=False, stop=True).then_inc(pe.h, 1))
            W(P.c, pe, pe.v)
            if p2 == 0:
                W(P.c, dv, SUV_END)
            if p2 >= 2:
                Sap.wait(P.c, sap_tok[p2 - 2])
            ac.inc()
            P.c(lambda e, bank=bank: e.activation(
                s_apt[bank], b[4 + bank][:], AF.Copy).then_inc(ac.h, 1))
            oev_at.append(ac.v)
            W(P.g, ac, ac.v)
            sap_tok.append(Sap.issue(P.g, lambda e, nd=nd, tc=tc, bank=bank: e.dma_start(
                d_ap.ap()[tc * 128:(tc + 1) * 128, nd * 512:(nd + 1) * 512], s_apt[bank])))
    OPJ_END = pe.v
    OWC_END = owc_at[-1]

    Sap.wait_all(P.sy)
    W(P.sy, dv, OWC_END)
    W(P.c, pe, OPJ_END)
    W(P.t, ac, ac.v)
    lnv_ap, lnv_x1, lnv_xp, tr_b2, sx1_tok = {}, {}, {}, {}, []
    for tc in range(NTB):
        b2 = tc % 2
        if tc >= 2:
            W(P.sy, dv, lnv_xp[tc - 2])
            W(P.sy, dv, lnv_x1[tc - 2])
            Sx1.wait(P.sy, sx1_tok[tc - 2])
        t_ap = Lap.issue(P.sy, lambda e, tc=tc, b2=b2: e.dma_start(
            s_ap2[b2], d_ap.ap()[tc * 128:(tc + 1) * 128, :]))
        t_x = Lx.issue(P.sy, lambda e, tc=tc, b2=b2: e.dma_start(
            s_x1t[b2], xB.ap()[tc * 128:(tc + 1) * 128, :]))
        Lap.wait(P.c, t_ap)
        W(P.c, dv, dv.v)
        P.c(lambda e, b2=b2: e.activation(s_junk[:, 0:D], s_ap2[b2], AF.Copy, accum_out=s_sum[:]))
        ac.inc()
        P.c(lambda e: e.activation(s_mu[:], s_sum[:], AF.Copy, scale=float(np.float32(1.0 / D))).then_inc(ac.h, 1))
        W(P.v, ac, ac.v)
        dv.inc()
        P.v(lambda e, b2=b2: e.tensor_scalar(s_ap2[b2], s_ap2[b2], s_mu[:], None, ALU.subtract).then_inc(dv.h, 1))
        W(P.c, dv, dv.v)
        P.c(lambda e, b2=b2: e.activation(s_junk[:, 0:D], s_ap2[b2], AF.Square, accum_out=s_sq[:]))
        P.c(lambda e: e.activation(s_lnt[:], s_sq[:], AF.Ln, bias=s_eps[:], scale=float(np.float32(1.0 / D))))
        ac.inc()
        P.c(lambda e: e.activation(s_rstd[:], s_lnt[:], AF.Exp, scale=-0.5).then_inc(ac.h, 1))
        Lx.wait(P.v, t_x)
        W(P.v, ac, ac.v)
        P.v(lambda e, b2=b2: e.tensor_scalar(s_ap2[b2], s_ap2[b2], s_rstd[:], None, ALU.mult))
        P.v(lambda e: e.drain())
        dv.inc()
        P.v(lambda e, b2=b2: e.tensor_tensor(s_x1t[b2], s_x1t[b2], s_ap2[b2], ALU.add).then_inc(dv.h, 1))
        lnv_ap[tc] = dv.v
        W(P.g, dv, dv.v)
        sx1_tok.append(Sx1.issue(P.g, lambda e, tc=tc, b2=b2: e.dma_start(
            d_x1.ap()[tc * 128:(tc + 1) * 128, :], s_x1t[b2])))
        W(P.c, dv, dv.v)
        P.c(lambda e, b2=b2: e.activation(s_junk[:, 0:D], s_x1t[b2], AF.Copy, accum_out=s_sum[:]))
        ac.inc()
        P.c(lambda e: e.activation(s_mu[:], s_sum[:], AF.Copy, scale=float(np.float32(1.0 / D))).then_inc(ac.h, 1))
        W(P.v, ac, ac.v)
        dv.inc()
        P.v(lambda e, b2=b2: e.tensor_scalar(s_ap2[b2], s_x1t[b2], s_mu[:], None, ALU.subtract).then_inc(dv.h, 1))
        lnv_x1[tc] = dv.v
        W(P.c, dv, dv.v)
        P.c(lambda e, b2=b2: e.activation(s_junk[:, 0:D], s_ap2[b2], AF.Square, accum_out=s_sq[:]))
        P.c(lambda e: e.activation(s_lnt[:], s_sq[:], AF.Ln, bias=s_eps[:], scale=float(np.float32(1.0 / D))))
        ac.inc()
        P.c(lambda e: e.activation(s_rstd[:], s_lnt[:], AF.Exp, scale=-0.5).then_inc(ac.h, 1))
        W(P.v, ac, ac.v)
        if tc >= 1:
            W(P.v, pe, tr_b2[tc - 1])
        dv.inc()
        P.v(lambda e, b2=b2: e.tensor_scalar(s_x2pt[:], s_ap2[b2], s_rstd[:], None, ALU.mult).then_inc(dv.h, 1))
        lnv_xp[tc] = dv.v
        W(P.t, dv, dv.v)
        for kc in range(16):
            W(P.t, ac, ac.v)
            pe.inc()
            P.t(lambda e, kc=kc: e.transpose(p_tr[:], s_x2pt[:, kc * 128:(kc + 1) * 128],
                                             s_ident[:]).then_inc(pe.h, 1))
            W(P.c, pe, pe.v)
            ac.inc()
            P.c(lambda e, kc=kc, tc=tc: e.activation(
                park[kc][:, tc * 128:(tc + 1) * 128], p_tr[:], AF.Copy).then_inc(ac.h, 1))
        tr_b2[tc] = pe.v
    LNV_END = dv.v
    AC_X2P = ac.v
    DNV_END = dnv_at[NSUP * H - 1]

    W(P.t, ac, AC_X2P)
    fab_at, fevA_at, fevV_at, sh_tok = {}, {}, {}, []
    W(P.sy, pe, PJN_END)
    W(P.c, dv, LNV_END)
    W(P.v, pe, PJ_Q_END)
    l12_tok = {}
    for fc in range(44):
        if fc >= 2:
            W(P.sy, pe, fab_at[(fc - 2) * 4 + 3])
        l12_tok[fc] = (
            L12.issue(P.sy, lambda e, fc=fc, bb=fc % 2: e.dma_start(
                s_w1c[bb].rearrange("p (dc c) -> p dc c", c=128),
                w1T.ap()[:, fc * 128:(fc + 1) * 128].rearrange("(dc p) c -> p dc c", p=128))),
            L12.issue(P.sy, lambda e, fc=fc, bb=fc % 2: e.dma_start(
                s_w2c[bb].rearrange("p (dc c) -> p dc c", c=128),
                w2T.ap()[:, fc * 128:(fc + 1) * 128].rearrange("(dc p) c -> p dc c", p=128))),
        )
        L12.wait(P.t, l12_tok[fc][0])
        L12.wait(P.t, l12_tok[fc][1])
        for tb4 in range(4):
            g2 = fc * 4 + tb4
            ba = 4 if g2 % 2 == 0 else 2
            if g2 >= 2:
                W(P.t, ac, fevA_at[g2 - 2])
                W(P.t, dv, fevV_at[g2 - 2])
            elif g2 == 1:
                W(P.t, dv, DNV_END)
            for dc in range(16):
                P.t(lambda e, dc=dc, tb4=tb4, bb=fc % 2, ba=ba: e.matmul(
                    b[ba][:], s_w1c[bb][:, dc * 128:(dc + 1) * 128],
                    park[dc][:, tb4 * 512:(tb4 + 1) * 512], start=(dc == 0), stop=(dc == 15)))
            pe.inc()
            for dc in range(16):
                if dc == 15:
                    P.t(lambda e, dc=dc, tb4=tb4, bb=fc % 2, ba=ba: e.matmul(
                        b[ba + 1][:], s_w2c[bb][:, dc * 128:(dc + 1) * 128],
                        park[dc][:, tb4 * 512:(tb4 + 1) * 512],
                        start=False, stop=True).then_inc(pe.h, 1))
                else:
                    P.t(lambda e, dc=dc, tb4=tb4, bb=fc % 2, ba=ba: e.matmul(
                        b[ba + 1][:], s_w2c[bb][:, dc * 128:(dc + 1) * 128],
                        park[dc][:, tb4 * 512:(tb4 + 1) * 512], start=(dc == 0), stop=False))
            fab_at[g2] = pe.v
            W(P.c, pe, fab_at[g2])
            if g2 >= 2:
                W(P.c, dv, fevV_at[g2 - 2])
            ac.inc()
            P.c(lambda e, bb=g2 % 2, ba=ba: e.activation(s_sil[bb], b[ba][:], AF.Silu).then_inc(ac.h, 1))
            fevA_at[g2] = ac.v
            W(P.v, ac, fevA_at[g2])
            if g2 >= 2:
                Sh.wait(P.v, sh_tok[g2 - 2])
            dv.inc()
            P.v(lambda e, bb=g2 % 2, ba=ba: e.tensor_tensor(
                s_hfm[bb][:], s_sil[bb], b[ba + 1][:], ALU.mult).then_inc(dv.h, 1))
            fevV_at[g2] = dv.v
            W(P.g, dv, fevV_at[g2])
            sh_tok.append(Sh.issue(P.g, lambda e, fc=fc, tb4=tb4, bb=g2 % 2: e.dma_start(
                d_h.ap()[fc * 128:(fc + 1) * 128, tb4 * 512:(tb4 + 1) * 512], s_hfm[bb][:])))
    FAB_END = fab_at[43 * 4 + 3]

    Sh.wait_all(P.sy)
    W(P.sy, pe, FAB_END)
    ymm_at, yev_at, sy_tok = {}, {}, []
    ymm_end = {}
    for dq in range(4):
        if dq >= 1:
            W(P.sy, pe, ymm_end[dq - 1])
        w3_toks = []
        for j in range(11):
            w3_toks.append(L3s.issue(P.sy, lambda e, j=j, dq=dq: e.dma_start(
                park[j][:].rearrange("p (q c) -> p q c", c=512),
                w3T.ap()[j * 512:(j + 1) * 512, dq * 512:(dq + 1) * 512].rearrange(
                    "(q p) c -> p q c", p=128))))
        for tk in w3_toks:
            L3s.wait(P.t, tk)
        for tt in range(NTB):
            y2 = dq * NTB + tt
            if y2 >= 2:
                W(P.sy, pe, ymm_at[y2 - 2])
            th = Lh.issue(P.sy, lambda e, tt=tt, bb=tt % 2: e.dma_start(
                s_hst[bb].rearrange("p (fc c) -> p fc c", c=128),
                d_h.ap()[:, tt * 128:(tt + 1) * 128].rearrange("(fc p) c -> p fc c", p=128)))
            Lh.wait(P.t, th)
            if y2 >= 2:
                W(P.t, ac, yev_at[y2 - 2])
            for fc in range(44):
                if fc == 43:
                    pe.inc()
                    P.t(lambda e, fc=fc, bb=tt % 2, bk=tt % 2: e.matmul(
                        b[bk][:], s_hst[bb][:, fc * 128:(fc + 1) * 128],
                        park[fc // 4][:, (fc % 4) * 512:(fc % 4 + 1) * 512],
                        start=False, stop=True).then_inc(pe.h, 1))
                else:
                    P.t(lambda e, fc=fc, bb=tt % 2, bk=tt % 2: e.matmul(
                        b[bk][:], s_hst[bb][:, fc * 128:(fc + 1) * 128],
                        park[fc // 4][:, (fc % 4) * 512:(fc % 4 + 1) * 512],
                        start=(fc == 0), stop=False))
            ymm_at[y2] = pe.v
            W(P.c, pe, ymm_at[y2])
            if y2 >= 2:
                Sy.wait(P.c, sy_tok[y2 - 2])
            else:
                Sg.wait(P.c, sg_tok[62 + y2])
            ac.inc()
            P.c(lambda e, bk=tt % 2, bb=y2 % 2: e.activation(
                s_yst[bb][:], b[bk][:], AF.Copy).then_inc(ac.h, 1))
            yev_at[y2] = ac.v
            W(P.g, ac, ac.v)
            sy_tok.append(Sy.issue(P.g, lambda e, tt=tt, dq=dq, bb=y2 % 2: e.dma_start(
                d_y.ap()[tt * 128:(tt + 1) * 128, dq * 512:(dq + 1) * 512], s_yst[bb][:])))
        ymm_end[dq] = pe.v

    Sy.wait_all(P.sy)
    Sx1.wait_all(P.sy)
    fin_at, so_tok = {}, []
    for tt in range(NTB):
        b2 = tt % 2
        if tt >= 2:
            W(P.sy, dv, fin_at[tt - 2])
        ty = Ly.issue(P.sy, lambda e, tt=tt, b2=b2: e.dma_start(
            s_yt[b2], d_y.ap()[tt * 128:(tt + 1) * 128, :]))
        tx1 = Lx1.issue(P.sy, lambda e, tt=tt, b2=b2: e.dma_start(
            s_x1b[b2], d_x1.ap()[tt * 128:(tt + 1) * 128, :]))
        Ly.wait(P.c, ty)
        W(P.c, dv, dv.v)
        P.c(lambda e, b2=b2: e.activation(s_junk[:, 0:D], s_yt[b2], AF.Copy, accum_out=s_sum[:]))
        ac.inc()
        P.c(lambda e: e.activation(s_mu[:], s_sum[:], AF.Copy, scale=float(np.float32(1.0 / D))).then_inc(ac.h, 1))
        W(P.v, ac, ac.v)
        dv.inc()
        P.v(lambda e, b2=b2: e.tensor_scalar(s_yc, s_yt[b2], s_mu[:], None, ALU.subtract).then_inc(dv.h, 1))
        W(P.c, dv, dv.v)
        P.c(lambda e: e.activation(s_junk[:, 0:D], s_yc, AF.Square, accum_out=s_sq[:]))
        P.c(lambda e: e.activation(s_lnt[:], s_sq[:], AF.Ln, bias=s_eps[:], scale=float(np.float32(1.0 / D))))
        ac.inc()
        P.c(lambda e: e.activation(s_rstd[:], s_lnt[:], AF.Exp, scale=-0.5).then_inc(ac.h, 1))
        Lx1.wait(P.v, tx1)
        W(P.v, ac, ac.v)
        P.v(lambda e: e.tensor_scalar(s_yc, s_yc, s_rstd[:], None, ALU.mult))
        P.v(lambda e: e.drain())
        if tt >= 1:
            Sout.wait(P.v, so_tok[tt - 1])
        dv.inc()
        P.v(lambda e, b2=b2: e.tensor_tensor(s_ost, s_x1b[b2], s_yc, ALU.add).then_inc(dv.h, 1))
        fin_at[tt] = dv.v
        W(P.g, dv, fin_at[tt])
        so_tok.append(Sout.issue(P.g, lambda e, tt=tt: e.dma_start(
            o_out.ap()[tt * 128:(tt + 1) * 128, :], s_ost)))

    for stm in (Sq, Sg, Sap, Sx1, Sh, Sy, Sout):
        stm.wait_all(P.g)
    P.finalize()
    return nc


def _prep_inputs_B(inputs, resA):
    x = np.asarray(inputs["L0_attn_x"])[0]
    Wq = np.asarray(inputs["L0_attn_Wq"])
    Wg = np.asarray(inputs["L0_attn_Wg"])
    Wo = np.asarray(inputs["L0_attn_Wo"])
    bq = np.asarray(inputs["L0_attn_bq"])
    bg = np.asarray(inputs["L0_attn_bg"])
    bo = np.asarray(inputs["L0_attn_bo"])
    WqT = np.ascontiguousarray(Wq.T[:, PERM], np.float32)
    WgT = np.ascontiguousarray(Wg.T, np.float32)
    WoT = np.ascontiguousarray(Wo.T, np.float32)
    bqP = np.ascontiguousarray(bq[PERM], np.float32).reshape(1, D)
    bgv = np.ascontiguousarray(bg, np.float32).reshape(1, D)
    bov = np.ascontiguousarray(bo, np.float32).reshape(1, D)
    ident = np.eye(128, dtype=bf16)
    w1T = np.ascontiguousarray(np.concatenate([resA[r]["o_w1t"] for r in range(NCORES)], axis=1))
    w2T = np.ascontiguousarray(np.concatenate([resA[r]["o_w2t"] for r in range(NCORES)], axis=1))
    w3T = np.ascontiguousarray(np.concatenate([resA[r]["o_w3t"] for r in range(NCORES)], axis=1))
    maps = []
    for c in range(NCORES):
        cos, sin = _rope_tables(c)
        spv = np.zeros((7 * H * DK, DK), bf16)
        zpr = np.zeros((128, 32, 7), np.float32)
        for r in range(c):
            sn = np.asarray(resA[r]["o_ssnap"]).reshape(H, NSUP, DK, DK)[:, NSUP - 1]
            spv[r * H * DK:(r + 1) * H * DK] = sn.reshape(H * DK, DK)
            zf = np.asarray(resA[r]["o_zsnap"]).reshape(NSUP, H, 128)[NSUP - 1]
            zpr[:, :H, r] = zf.T
        zsn = np.ascontiguousarray(
            np.asarray(resA[c]["o_zsnap"]).reshape(NSUP, H, 128).transpose(2, 0, 1).reshape(128, NSUP * H),
            np.float32)
        maps.append({
            "xB": np.ascontiguousarray(x[c * TPC:(c + 1) * TPC], np.float32),
            "x2r": np.asarray(resA[c]["o_x2"]),
            "ktm": np.asarray(resA[c]["o_ktm"]),
            "vtm": np.asarray(resA[c]["o_vtm"]),
            "snapO": np.asarray(resA[c]["o_ssnap"]),
            "sprev": spv,
            "zsnC": zsn,
            "zprC": np.ascontiguousarray(zpr.reshape(128, 224), np.float32),
            "WqT": WqT, "WgT": WgT, "WoT": WoT,
            "bq": bqP, "bg": bgv, "bo": bov,
            "w1T": w1T, "w2T": w2T, "w3T": w3T,
            "cosA": cos, "sinA": sin, "ident": ident,
        })
    return maps


def kernel(**inputs):
    mapsA = _prep_inputs_A(inputs)
    ncA = build_A()
    resA = run_bass_kernel_spmd(ncA, mapsA, core_ids=list(range(NCORES))).results
    mapsB = _prep_inputs_B(inputs, resA)
    ncB = build_B()
    resB = run_bass_kernel_spmd(ncB, mapsB, core_ids=list(range(NCORES))).results
    out = np.concatenate([np.asarray(resB[c]["o_out"], np.float32) for c in range(NCORES)], axis=0)
    return out.reshape(1, T, D)


# version 03 (final)
# speedup vs baseline: 1.1051x; 1.0173x over previous
import os
import numpy as np
import ml_dtypes

import concourse.bass as bass
import concourse.mybir as mybir
from concourse.ap import AP as _AP
from concourse.bass_utils import run_bass_kernel_spmd

bf16 = ml_dtypes.bfloat16
F32 = mybir.dt.float32
BF16 = mybir.dt.bfloat16
AF = mybir.ActivationFunctionType
ALU = mybir.AluOpType
AX = mybir.AxisListType

NCORES = 8
T = 16384
TPC = T // NCORES
D = 2048
H = 16
DK = 128
SUP = 256
NSUP = TPC // SUP
NTB = TPC // 128
DFF = 5632
FS1 = DFF // NCORES
FS3 = D // NCORES
NSG = 88
NB_TOT = DFF * D // 64

_NF4 = np.array([-1.0, -0.6962, -0.5251, -0.3949, -0.2844, -0.1848, -0.0911,
                 0.0, 0.0796, 0.1609, 0.2461, 0.3379, 0.4407, 0.5626, 0.723,
                 1.0], dtype=bf16)


def _nf4_consts():
    bits = np.arange(65536, dtype=np.uint16)
    av = bits.view(bf16)
    fv = av.astype(np.float32)
    sel = np.isfinite(fv) & (np.abs(fv) <= 1.0)
    qs = np.sort(av[sel].astype(np.float32), kind="stable")
    d = np.abs((qs.astype(bf16)[:, None] - _NF4[None, :]).astype(bf16))
    idx = np.argmin(d.astype(np.float32), axis=1)
    thr = []
    for j in range(1, 16):
        k = np.searchsorted(idx, j, "left")
        thr.append(float(float(np.float32((float(qs[k - 1])) + float(qs[k])) / 2.0)))
    steps = []
    acc = _NF4[0]
    for j in range(1, 16):
        cand = np.array(float(_NF4[j]) - float(acc), dtype=bf16)
        if (acc + cand).astype(bf16) != _NF4[j]:
            cb = cand.view(np.uint16)
            for dlt in (1, -1, 2, -2):
                c2 = np.array(cb + dlt, dtype=np.uint16).view(bf16)[()]
                if (acc + c2).astype(bf16) == _NF4[j]:
                    cand = c2
                    break
        steps.append(float(float(np.float32(np.array(cand, dtype=bf16)))))
        acc = (acc + cand).astype(bf16)[()]
    return thr, steps


NF4_THR, NF4_STEP = _nf4_consts()
MAGIC = 12582912.0
EPS_LN = 1e-5


def _head_perm():
    p = np.zeros(D, np.int64)
    for h in range(H):
        b = h * DK
        p[b:b + 64] = b + np.arange(0, DK, 2)
        p[b + 64:b + DK] = b + np.arange(1, DK, 2)
    return p


PERM = _head_perm()


class Prog:

    ENGINES = ("sync", "vector", "scalar", "tensor", "gpsimd")

    def __init__(self, nc):
        self.nc = nc
        self.streams = {e: [] for e in self.ENGINES}

    def sy(self, fn): self.streams["sync"].append(fn)
    def v(self, fn): self.streams["vector"].append(fn)
    def c(self, fn): self.streams["scalar"].append(fn)
    def t(self, fn): self.streams["tensor"].append(fn)
    def g(self, fn): self.streams["gpsimd"].append(fn)

    def finalize(self):
        with self.nc.Block() as blk:
            for name in self.ENGINES:
                fns = self.streams[name]
                if not fns:
                    continue

                def body(e, fns=fns):
                    for fn in fns:
                        fn(e)

                getattr(blk, name)(body)


class Sem:
    def __init__(self, nc, name):
        self.h = nc.alloc_semaphore(name)
        self.v = 0

    def inc(self, n=1):
        self.v += n
        return self.v


class DStream:

    def __init__(self, nc, name, nbuf=2):
        self.sems = [Sem(nc, f"{name}_{i}") for i in range(nbuf)]
        self.n = 0

    def issue(self, P_eng, fn):
        s = self.sems[self.n % len(self.sems)]
        s.inc(16)
        v = s.v
        P_eng(lambda e, s=s: fn(e).then_inc(s.h, 16))
        self.n += 1
        return (s, v)

    def wait(self, P_eng, tok):
        s, v = tok
        P_eng(lambda e, s=s, v=v: e.wait_ge(s.h, v))

    def wait_all(self, P_eng):
        for s in self.sems:
            P_eng(lambda e, s=s, v=s.v: e.wait_ge(s.h, v))


def _tile_rows(R):
    out, r = [], 0
    while r < R:
        n = min(128, R - r)
        out.append((r, n))
        r += n
    return out


def build_A():
    nc = bass.Bass("TRN2", target_bir_lowering=False, detect_race_conditions=False)
    f = nc.dram_tensor

    xA = f("xA", [TPC, D], F32, kind="ExternalInput")
    WkT = f("WkT", [D, D], F32, kind="ExternalInput")
    WvT = f("WvT", [D, D], F32, kind="ExternalInput")
    bk = f("bk", [1, D], F32, kind="ExternalInput")
    bv = f("bv", [1, D], F32, kind="ExternalInput")
    cosA = f("cosA", [TPC, 64], F32, kind="ExternalInput")
    sinA = f("sinA", [TPC, 64], F32, kind="ExternalInput")
    ident = f("ident", [128, 128], BF16, kind="ExternalInput")
    w1f = f("w1f", [DFF, D], BF16, kind="ExternalInput")
    w2f = f("w2f", [DFF, D], BF16, kind="ExternalInput")
    w3f = f("w3f", [D, DFF], BF16, kind="ExternalInput")
    w1s = f("w1s", [FS1, D], BF16, kind="ExternalInput")
    w2s = f("w2s", [FS1, D], BF16, kind="ExternalInput")
    w3s = f("w3s", [FS3, DFF], BF16, kind="ExternalInput")

    o_x2 = f("o_x2", [D, TPC], BF16, kind="ExternalOutput")
    o_ktm = f("o_ktm", [TPC, D], BF16, kind="ExternalOutput")
    o_vtm = f("o_vtm", [TPC, D], BF16, kind="ExternalOutput")
    o_ssnap = f("o_ssnap", [H * NSUP * DK, DK], BF16, kind="ExternalOutput")
    o_zsnap = f("o_zsnap", [1, NSUP * D], F32, kind="ExternalOutput")
    o_w1t = f("o_w1t", [D, FS1], BF16, kind="ExternalOutput")
    o_dlvl = f("o_dlvl", [128, D], BF16, kind="ExternalOutput")
    o_dbgq = f("o_dbgq", [128, D], BF16, kind="ExternalOutput")
    o_dbgw = f("o_dbgw", [128, D], BF16, kind="ExternalOutput")
    o_dbgs = f("o_dbgs", [128, 128], BF16, kind="ExternalOutput")
    o_dbgl = f("o_dbgl", [128, D], BF16, kind="ExternalOutput")
    o_w2t = f("o_w2t", [D, FS1], BF16, kind="ExternalOutput")
    o_w3t = f("o_w3t", [DFF, FS3], BF16, kind="ExternalOutput")

    if os.environ.get("KA_FAKES1", "0") == "1":
        i_s1 = [f(f"i_s1_{i}", [1, FS1 * 32], BF16, kind="ExternalInput") for i in range(3)]
        i_dq = [f(f"i_dq_{i}", [1, FS1 * 32], BF16, kind="ExternalInput") for i in range(3)]
    d_s1 = [f(f"d_s1_{i}", [1, FS1 * 32 + 16], BF16) for i in range(3)]
    d_dq = [f(f"d_dq_{i}", [1, FS1 * 32 + 16], BF16) for i in range(3)]
    d_dq8 = [f(f"d_dq8_{i}", [1, FS1 * 32 * 8 + 256], BF16) for i in range(3)]

    sb = nc.alloc_sbuf_tensor
    s_cos = sb("s_cos", [128, NTB * 64], F32)
    s_sin = sb("s_sin", [128, NTB * 64], F32)
    s_ident = sb("s_ident", [128, 128], BF16)
    s_ones_row = sb("s_ones_row", [1, 128], BF16)
    s_ones_col = sb("s_ones_col", [128, 1], BF16)
    s_ones_f32 = sb("s_ones_f32", [128, 1], F32)
    s_bk = sb("s_bk", [1, D], BF16)
    s_bv = sb("s_bv", [1, D], BF16)


    s_wt = [sb(f"s_wt{i}", [128, DFF + 256], BF16) for i in range(2)]
    s_acc = [sb(f"s_acc{i}", [128, 1], F32) for i in range(3)]
    s_red = sb("s_red", [128, 96], F32)
    s_red2 = sb("s_red2", [128, 1], F32)
    s_sums = sb("s_sums", [1, 4], F32)
    s_smean = sb("s_smean", [1, 4], BF16)
    s_smeanB = sb("s_smeanB", [128, 3], F32)

    _pad0 = sb("_pad0", [128, 256], BF16)
    s_q = sb("s_q", [128, DFF + 256], BF16)
    s_lvl = sb("s_lvl", [128, DFF + 256], BF16)
    s_term = sb("s_term", [128, DFF + 256], BF16)
    s_s1 = sb("s_s1", [128, 96], F32)
    s_s1b = sb("s_s1b", [128, 96], BF16)
    s_r1 = sb("s_r1", [128, 96], F32)
    s_rt = sb("s_rt", [128, 96], F32)
    s_cent = sb("s_cent", [128, 96], BF16)
    s_dqs = sb("s_dqs", [128, 96], BF16)
    s_s1all = sb("s_s1all", [128, 1440], BF16)
    s_dq8 = sb("s_dq8", [NSG, 2048], BF16)
    s_dqs8 = sb("s_dqs8", [128, 704 + 128], BF16)
    s_grp = sb("s_grp", [NSG, 256], BF16)
    s_gax = sb("s_gax", [NSG, 1], BF16)
    s_ga32 = sb("s_ga32", [NSG, 1], F32)
    s_gq = sb("s_gq", [NSG, 1], F32)
    s_gq2 = sb("s_gq2", [NSG, 1], F32)
    s_gt = sb("s_gt", [NSG, 1], F32)
    s_qf = sb("s_qf", [NSG, 1], BF16)
    s_qf32 = sb("s_qf32", [NSG, 1], F32)
    s_prod = sb("s_prod", [NSG, 256], BF16)
    s_qsc = sb("s_qsc", [NSG, 256], F32)
    s_deqg = sb("s_deqg", [NSG, 256], BF16)
    s_wtr = [sb(f"s_wtr{i}", [128, 128], BF16) for i in range(2)]

    s_eps = sb("s_eps", [128, 1], F32)
    s_x = [sb(f"s_x{i}", [128, D], F32) for i in range(2)]
    s_b32 = s_x[0][0:1, 0:D]
    s_b32b = s_x[1][0:1, 0:D]
    s_xc = sb("s_xc", [128, D], F32)
    s_mu = sb("s_mu", [128, 1], F32)
    s_ss = sb("s_ss", [128, 1], F32)
    s_lnt = sb("s_lnt", [128, 1], F32)
    s_rstd = sb("s_rstd", [128, 1], F32)
    s_x2t = sb("s_x2t", [128, D], BF16)
    s_x2st = [sb(f"s_x2st{i}", [128, 128], BF16) for i in range(2)]
    s_x2b = [sb(f"s_x2b{i}", [128, 16 * 128], BF16) for i in range(2)]
    s_wf32 = sb("s_wf32", [128, 16 * 512], F32)
    s_wb = sb("s_wb", [128, 16 * 512], BF16)
    s_kblk = [sb(f"s_kblk{i}", [128, 512], BF16) for i in range(2)]
    s_ro1 = sb("s_ro1", [128, 256], BF16)
    s_ro2 = sb("s_ro2", [128, 256], BF16)
    s_el1 = sb("s_el1", [128, 512], BF16)
    s_el2 = sb("s_el2", [128, 512], BF16)
    s_zrow = sb("s_zrow", [1, 1024], F32)
    s_kv = [s_term[0:128, 0:D], s_term[0:128, D:2 * D],
            s_lvl[0:128, 0:D], s_lvl[0:128, D:2 * D]]
    s_vv = [s_wt[0][0:128, 0:D], s_wt[0][0:128, D:2 * D],
            s_wt[1][0:128, 0:D], s_wt[1][0:128, D:2 * D]]
    s_snap = None

    ps = nc.alloc_psum_tensor
    p_mm = [ps(f"p_mm{i}", [128, 512], F32) for i in range(2)]
    p_z = ps("p_z", [1, 512], F32)
    p_tr = ps("p_tr", [128, 128], BF16)
    p_s = [ps(f"p_s{i}", [128, 128], F32) for i in range(4)]
    p_one = p_z[0:1, 0:1]
    p_bc = p_mm[0][:, 0:1]

    P = Prog(nc)
    cst = Sem(nc, "cst")
    sm = Sem(nc, "sm")
    dq = Sem(nc, "dq")
    tr = Sem(nc, "tr")
    te = Sem(nc, "te")
    lnv = Sem(nc, "lnv")
    lns = Sem(nc, "lns")
    wcast = Sem(nc, "wcast")
    pj = Sem(nc, "pj")
    ev = Sem(nc, "ev")
    rp = Sem(nc, "rp")
    zs = Sem(nc, "zs")
    sn = Sem(nc, "sn")
    sne = Sem(nc, "sne")
    Lc = DStream(nc, "Lc", 1)
    Lw = DStream(nc, "Lw", 2)
    Lg = DStream(nc, "Lg", 1)
    L3 = DStream(nc, "L3", 1)
    Lx = DStream(nc, "Lx", 2)
    Lf = DStream(nc, "Lf", 1)
    L2 = DStream(nc, "L2", 2)
    Lkv = DStream(nc, "Lkv", 2)
    S1 = DStream(nc, "S1", 1)
    S2 = DStream(nc, "S2", 1)
    Swt = DStream(nc, "Swt", 2)
    Sx2 = DStream(nc, "Sx2", 2)
    Sspl = DStream(nc, "Sspl", 2)
    Szt = DStream(nc, "Szt", 2)
    Ssn = DStream(nc, "Ssn", 2)
    dbgs = Sem(nc, "dbgs")

    def W(P_eng, sem, val):
        P_eng(lambda e, s=sem.h, w=val: e.wait_ge(s, w))

    Lc.issue(P.sy, lambda e: e.dma_start(s_ident[:], ident.ap()))
    Lc.issue(P.sy, lambda e: e.dma_start(s_cos[:].rearrange("p (a f) -> p a f", f=64),
                                         cosA.ap().rearrange("(a p) f -> p a f", p=128)))
    Lc.issue(P.sy, lambda e: e.dma_start(s_sin[:].rearrange("p (a f) -> p a f", f=64),
                                         sinA.ap().rearrange("(a p) f -> p a f", p=128)))
    Lc.issue(P.sy, lambda e: e.dma_start(s_b32, bk.ap()))
    Lc.issue(P.sy, lambda e: e.dma_start(s_b32b, bv.ap()))
    Lc.wait_all(P.v)
    P.v(lambda e: e.memset(s_ones_row[:], 1.0))
    P.v(lambda e: e.memset(s_eps[:], float(np.float32(EPS_LN))))
    P.v(lambda e: e.memset(s_ones_col[:], 1.0))
    P.v(lambda e: e.memset(s_ones_f32[:], 1.0))
    P.v(lambda e: e.tensor_copy(s_bk[:], s_b32))
    P.v(lambda e: e.tensor_copy(s_bv[:], s_b32b))
    cst.inc()
    P.v(lambda e: e.drain().then_inc(cst.h, 1))

    _PH = int(os.environ.get("KA_PHASES", "9"))
    if os.environ.get("KA_NOPA1", "0") == "1":
        W(P.v, cst, cst.v)
        P.v(lambda e: e.memset(s_smeanB[:], 0.05))
        sm.inc()
        P.v(lambda e: e.drain().then_inc(sm.h, 1))
        mats = []
    else:
        mats = [(w1f, D, 44, 0), (w2f, D, 44, 1), (w3f, DFF, 16, 2)]
    sm_tile = 0
    for (wh, C, ntile, mi) in mats:
        G = C // 64
        P.v(lambda e, mi=mi: e.memset(s_acc[mi][:], 0.0))
        for t in range(ntile):
            if sm_tile >= 2:
                W(P.sy, sm, sm_tile - 1)
            tok = Lw.issue(P.sy, lambda e, wh=wh, t=t, C=C, b=sm_tile % 2: e.dma_start(
                s_wt[b][:, 0:C], wh.ap()[t * 128:(t + 1) * 128, :]))
            Lw.wait(P.v, tok)
            P.v(lambda e, b=sm_tile % 2, C=C, G=G: e.tensor_reduce(
                s_red[:, 0:G], s_wt[b][:, 0:C].rearrange("p (g k) -> p g k", k=64),
                AX.X, ALU.max, apply_absolute_value=True))
            P.v(lambda e: e.drain())
            P.v(lambda e, G=G: e.tensor_reduce(s_red2[:], s_red[:, 0:G], AX.X, ALU.add))
            P.v(lambda e: e.drain())
            P.v(lambda e, mi=mi: e.tensor_tensor(s_acc[mi][:], s_acc[mi][:], s_red2[:], ALU.add))
            sm.inc()
            P.v(lambda e: e.drain().then_inc(sm.h, 1))
            sm_tile += 1
    for mi in range(3):
        W(P.t, sm, sm.v)
        sm.inc()
        P.t(lambda e, mi=mi: e.matmul(p_one, s_ones_f32[:], s_acc[mi][:], start=True, stop=True).then_inc(sm.h, 1))
        W(P.c, sm, sm.v)
        sm.inc()
        P.c(lambda e, mi=mi: e.activation(s_sums[:, mi:mi + 1], p_one, AF.Copy).then_inc(sm.h, 1))
    W(P.v, sm, sm.v)
    P.v(lambda e: e.tensor_scalar(s_smean[:, 0:3], s_sums[:, 0:3], float(np.float32(1.0 / NB_TOT)), None, ALU.mult))
    sm.inc()
    P.v(lambda e: e.drain().then_inc(sm.h, 1))
    for mi in range(3):
        W(P.t, sm, sm.v)
        sm.inc()
        P.t(lambda e, mi=mi: e.matmul(p_bc, s_ones_row[:], s_smean[:, mi:mi + 1], start=True, stop=True).then_inc(sm.h, 1))
        W(P.c, sm, sm.v)
        sm.inc()
        P.c(lambda e, mi=mi: e.activation(s_smeanB[:, mi:mi + 1], p_bc, AF.Copy).then_inc(sm.h, 1))
    SMEAN_DONE = sm.v

    if _PH < 2:
        for stm in (S1, S2, Swt, Sx2, Sspl, Szt, Ssn):
            stm.wait_all(P.g)
        P.finalize()
        return nc
    shards = [(w1s, d_s1[0], d_dq[0], o_w1t, FS1, D, 0),
              (w2s, d_s1[1], d_dq[1], o_w2t, FS1, D, 1),
              (w3s, d_s1[2], d_dq[2], o_w3t, FS3, DFF, 2)]

    _fake = os.environ.get("KA_FAKES1", "0") == "1"
    if _fake:
        shards = [(w1s, i_s1[0], i_dq[0], o_w1t, FS1, D, 0),
                  (w2s, i_s1[1], i_dq[1], o_w2t, FS1, D, 1),
                  (w3s, i_s1[2], i_dq[2], o_w3t, FS3, DFF, 2)]
    _s1off = {}
    _o = 0
    for (_ws, _d1, _d2, _ow, _R, _C, _mi) in shards:
        for _ti, (_r0, _rn) in enumerate(_tile_rows(_R)):
            _s1off[(_mi, _ti)] = _o
            _o += 96
    _shards_p1 = [(w1s, d_s1[0], d_dq[0], o_w1t, FS1, D, 0),
                  (w2s, d_s1[1], d_dq[1], o_w2t, FS1, D, 1),
                  (w3s, d_s1[2], d_dq[2], o_w3t, FS3, DFF, 2)]
    _runp1 = (not _fake) or os.environ.get("KA_RUNP1", "0") == "1"
    p1 = 0
    for (ws, ds1, ddq, owt, R, C, mi) in (_shards_p1 if _runp1 else []):
        G = C // 64
        for ti, (r0, rn) in enumerate(_tile_rows(R)):
            if p1 >= 1:
                W(P.sy, dq, dq.v)
            tok = Lw.issue(P.sy, lambda e, ws=ws, r0=r0, rn=rn, C=C, b=Lw.n % 2: e.dma_start(
                s_wt[b][0:rn, 0:C], ws.ap()[r0:r0 + rn, :]))
            bb = (Lw.n - 1) % 2
            Lw.wait(P.v, tok)
            if p1 == 0:
                W(P.v, sm, SMEAN_DONE)
            if p1 >= 1:
                W(P.v, S1.sems[0], S1.sems[0].v)
            if p1 == 0:
                P.v(lambda e: e.memset(s_s1[:], 1.0))
                P.v(lambda e: e.drain())
            P.v(lambda e, b=bb, rn=rn, C=C, G=G: e.tensor_reduce(
                s_s1[0:rn, 0:G], s_wt[b][0:rn, 0:C].rearrange("p (g k) -> p g k", k=64),
                AX.X, ALU.max, apply_absolute_value=True))
            P.v(lambda e: e.drain())
            _so = _s1off[(mi, ti)]
            P.v(lambda e, rn=rn, so=_so: e.tensor_copy(s_s1all[0:rn, so:so + 96], s_s1[0:rn, 0:96]))
            P.v(lambda e, rn=rn, mi=mi: e.tensor_scalar(
                s_cent[0:rn, 0:96], s_s1[0:rn, 0:96], s_smeanB[0:rn, mi:mi + 1], None,
                ALU.subtract))
            dq.inc()
            P.v(lambda e: e.drain().then_inc(dq.h, 1))
            W(P.c, dq, dq.v)
            S1.issue(P.c, lambda e, ddq=ddq, r0=r0, rn=rn, G=G: e.dma_start(
                ddq.ap()[:, r0 * G:(r0 + rn) * G].rearrange("() (p g) -> p g", g=G),
                s_cent[0:rn, 0:G]))
            p1 += 1

    _runp2 = (not _fake) or os.environ.get("KA_RUNP2", "0") == "1"
    for gi, (ws, ds1, ddq, owt, R, C, mi) in enumerate(_shards_p1 if _runp2 else []):
        nb = R * (C // 64)
        S1.wait_all(P.c)
        tok = Lg.issue(P.c, lambda e, ddq=ddq, nb=nb: e.dma_start(
            s_grp[:], ddq.ap()[:, 0:nb].rearrange("() (sg n) -> sg n", n=256)))
        Lg.wait(P.v, tok)
        if gi >= 1:
            W(P.v, S2.sems[0], S2.sems[0].v)
        for _op in [
            lambda e: e.tensor_reduce(s_gax[:], s_grp[:], AX.X, ALU.max, apply_absolute_value=True),
            lambda e: e.tensor_copy(s_ga32[:], s_gax[:]),
            lambda e: e.tensor_scalar(s_gq[:], s_ga32[:], 2.0, None, ALU.mult),
            lambda e: e.reciprocal(s_gq2[:], s_gq[:]),
            lambda e: e.tensor_tensor(s_gt[:], s_gq[:], s_gq2[:], ALU.mult),
            lambda e: e.tensor_scalar(s_gt[:], s_gt[:], -1.0, 2.0, ALU.mult, ALU.add),
            lambda e: e.tensor_tensor(s_gq2[:], s_gq2[:], s_gt[:], ALU.mult),
            lambda e: e.tensor_scalar(s_qf[:], s_gq2[:], 256.0, None, ALU.mult),
            lambda e: e.tensor_copy(s_qf32[:], s_qf[:]),
            lambda e: e.tensor_scalar(s_prod[:], s_grp[:], s_qf32[:], None, ALU.mult),
            lambda e: e.tensor_scalar(s_qsc[:], s_prod[:], MAGIC, MAGIC, ALU.add, ALU.subtract),
            lambda e: e.tensor_scalar(s_qsc[:], s_qsc[:], -128.0, 127.0, ALU.max, ALU.min),
            lambda e: e.reciprocal(s_gq[:], s_qf32[:]),
            lambda e: e.tensor_tensor(s_gt[:], s_qf32[:], s_gq[:], ALU.mult),
            lambda e: e.tensor_scalar(s_gt[:], s_gt[:], -1.0, 2.0, ALU.mult, ALU.add),
            lambda e: e.tensor_tensor(s_gq[:], s_gq[:], s_gt[:], ALU.mult),
            lambda e: e.tensor_scalar(s_deqg[:], s_qsc[:], s_gq[:], None, ALU.mult),
        ]:
            P.v(_op)
            P.v(lambda e: e.drain())
        P.v(lambda e, mi=mi: e.tensor_scalar(s_deqg[:], s_deqg[:], s_smeanB[0:NSG, mi:mi + 1], None, ALU.add))
        P.v(lambda e: e.drain())
        P.v(lambda e: e.tensor_copy(
            s_dq8[:].rearrange("p (n e) -> p n e", e=8),
            s_deqg[:].rearrange("p n -> p n ()").broadcast_to([NSG, 256, 8])))
        dq.inc()
        P.v(lambda e: e.drain().then_inc(dq.h, 1))
        W(P.c, dq, dq.v)
        gi8 = {0: d_dq8[0], 1: d_dq8[1], 2: d_dq8[2]}[mi]
        S2.issue(P.c, lambda e, gi8=gi8, nb=nb: e.dma_start(
            gi8.ap()[:, 0:nb * 8].rearrange("() (sg n) -> sg n", n=2048), s_dq8[:]))


    def expand64(src_ap_fn, dst, rn, G, dtype_note=""):
        v3 = dst.rearrange("p (g k) -> p g k", k=64)
        P.v(lambda e: e.tensor_copy(v3[0:rn, 0:G, 0:1], src_ap_fn()))
        P.v(lambda e: e.drain())
        w = 1
        while w < 64:
            ww = w
            P.v(lambda e, ww=ww: e.tensor_copy(v3[0:rn, 0:G, ww:2 * ww], v3[0:rn, 0:G, 0:ww]))
            P.v(lambda e: e.drain())
            w *= 2
    tcnt = 0
    wt_tok = []
    p3 = 0
    for (ws, ds1, ddq, owt, R, C, mi) in shards:
        G = C // 64
        for ti, (r0, rn) in enumerate(_tile_rows(R)):
            if p3 == 0:
                S2.wait_all(P.sy)
                W(P.sy, dq, dq.v)
            else:
                W(P.sy, dq, dq.v)
            Lw.issue(P.sy, lambda e, ws=ws, r0=r0, rn=rn, C=C, b=Lw.n % 2: e.dma_start(
                s_wt[b][0:rn, 0:C], ws.ap()[r0:r0 + rn, :]))
            bb = (Lw.n - 1) % 2
            dd8 = {0: d_dq8[0], 1: d_dq8[1], 2: d_dq8[2]}[mi]
            L3.issue(P.sy, lambda e, dd8=dd8, r0=r0, rn=rn, G=G: e.dma_start(
                s_dqs8[0:rn, 0:G * 8 + 128],
                _AP(tensor=dd8.ap().tensor, offset=r0 * G * 8, ap=[[G * 8, rn], [1, G * 8 + 128]])))
            Lw.wait_all(P.v)
            L3.wait_all(P.v)
            if p3 >= 1:
                W(P.v, tr, tr.v)
            _so = _s1off[(mi, ti)]
            for _mk in [
                lambda rn, G, so=_so: (lambda e: e.tensor_copy(s_s1[0:rn, 0:96], s_s1all[0:rn, so:so + 96])),
                lambda rn, G: (lambda e: e.reciprocal(s_r1[0:rn, 0:96], s_s1[0:rn, 0:96])),
                lambda rn, G: (lambda e: e.tensor_tensor(s_rt[0:rn, 0:96], s_s1[0:rn, 0:96], s_r1[0:rn, 0:96], ALU.mult)),
                lambda rn, G: (lambda e: e.tensor_scalar(s_rt[0:rn, 0:96], s_rt[0:rn, 0:96], -1.0, 2.0, ALU.mult, ALU.add)),
                lambda rn, G: (lambda e: e.tensor_tensor(s_r1[0:rn, 0:96], s_r1[0:rn, 0:96], s_rt[0:rn, 0:96], ALU.mult)),
            ]:
                P.v(_mk(rn, G))
                P.v(lambda e: e.drain())
            P.v(lambda e, b=bb, rn=rn, C=C, G=G: e.tensor_tensor(
                s_q[0:rn, 0:C].rearrange("p (g k) -> p g k", k=64),
                s_wt[b][0:rn, 0:C].rearrange("p (g k) -> p g k", k=64),
                s_r1[0:rn, 0:G].rearrange("p g -> p g ()").broadcast_to([rn, G, 64]),
                ALU.mult))
            P.v(lambda e: e.drain())
            P.v(lambda e, rn=rn, C=C: e.memset(s_lvl[0:rn, 0:C], -1.0))
            for j in range(15):
                P.v(lambda e, j=j, rn=rn, C=C: e.tensor_scalar(
                    s_term[0:rn, 0:C], s_q[0:rn, 0:C], NF4_THR[j], NF4_STEP[j], ALU.is_gt, ALU.mult))
                P.v(lambda e, rn=rn, C=C: e.tensor_tensor(s_lvl[0:rn, 0:C], s_lvl[0:rn, 0:C], s_term[0:rn, 0:C], ALU.add))
            P.v(lambda e, rn=rn, C=C, G=G: e.tensor_tensor(
                s_lvl[0:rn, 0:C].rearrange("p (g k) -> p g k", k=64),
                s_lvl[0:rn, 0:C].rearrange("p (g k) -> p g k", k=64),
                s_dqs8[0:rn, 0:G * 8].rearrange("p (g e) -> p g e", e=8)[:, :, 0:1].broadcast_to([rn, G, 64]),
                ALU.mult))
            P.v(lambda e: e.drain())
            dq.inc()
            P.v(lambda e: e.drain().then_inc(dq.h, 1))
            if mi == 1 and ti == 0 and os.environ.get("KA_DL", "0") == "1":
                W(P.g, dq, dq.v)
                dbgs.inc(16)
                P.g(lambda e: e.dma_start(o_dlvl.ap(), s_lvl[0:128, 0:D]).then_inc(dbgs.h, 16))
                W(P.t, dbgs, dbgs.v)
            W(P.t, dq, dq.v)
            _notr = os.environ.get("KA_NOTR", "0") == "1"
            for cix in ([] if _notr else range(C // 128)):
                if tcnt >= 1:
                    W(P.t, te, te.v)
                tr.inc()
                P.t(lambda e, cix=cix, rn=rn: e.transpose(
                    p_tr[0:128, 0:rn], s_lvl[0:rn, cix * 128:(cix + 1) * 128],
                    s_ident[0:rn, 0:rn]).then_inc(tr.h, 1))
                W(P.c, tr, tr.v)
                if tcnt >= 2:
                    Swt.wait(P.c, wt_tok[tcnt - 2])
                te.inc()
                P.c(lambda e, b=tcnt % 2, rn=rn: e.activation(
                    s_wtr[b][0:128, 0:rn], p_tr[0:128, 0:rn], AF.Copy).then_inc(te.h, 1))
                wt_tok.append(Swt.issue(P.c, lambda e, owt=owt, cix=cix, r0=r0, rn=rn, b=tcnt % 2: e.dma_start(
                    owt.ap()[cix * 128:(cix + 1) * 128, r0:r0 + rn],
                    s_wtr[b][0:128, 0:rn])))
                tcnt += 1
            p3 += 1

    if _PH < 3:
        for stm in (S1, S2, Swt, Sx2, Sspl, Szt, Ssn):
            stm.wait_all(P.g)
        P.finalize()
        return nc
    tr_after_tile = []
    x2_tok = []
    x2st_n = 0
    W(P.sy, cst, cst.v)
    for tb in range(NTB):
        if tb >= 2:
            W(P.sy, lnv, 2 * (tb - 2) + 1)
        tokx = Lx.issue(P.sy, lambda e, tb=tb: e.dma_start(
            s_x[tb % 2][:], xA.ap()[tb * 128:(tb + 1) * 128, :]))
        Lx.wait(P.v, tokx)
        if tb >= 1:
            W(P.v, tr, tr_after_tile[tb - 1])
        xt = s_x[tb % 2]
        Lx.wait(P.c, tokx)
        P.c(lambda e, xt=xt: e.activation(s_term[0:128, 0:D], xt[:], AF.Copy, accum_out=s_ss[:]))
        lns.inc()
        P.c(lambda e: e.activation(s_mu[:], s_ss[:], AF.Copy, scale=float(np.float32(1.0 / D))).then_inc(lns.h, 1))
        W(P.v, lns, lns.v)
        lnv.inc()
        P.v(lambda e, xt=xt: e.tensor_scalar(s_xc[:], xt[:], s_mu[:], None, ALU.subtract).then_inc(lnv.h, 1))
        W(P.c, lnv, lnv.v)
        P.c(lambda e: e.activation(s_term[0:128, 0:D], s_xc[:], AF.Square, accum_out=s_ss[:]))
        P.c(lambda e: e.activation(s_lnt[:], s_ss[:], AF.Ln, bias=s_eps[:], scale=float(np.float32(1.0 / D))))
        lns.inc()
        P.c(lambda e: e.activation(s_rstd[:], s_lnt[:], AF.Exp, scale=-0.5).then_inc(lns.h, 1))
        W(P.v, lns, lns.v)
        lnv.inc()
        P.v(lambda e: e.tensor_scalar(s_x2t[:], s_xc[:], s_rstd[:], None, ALU.mult).then_inc(lnv.h, 1))
        W(P.t, lnv, lnv.v)
        for kc in range(16):
            W(P.t, te, te.v)
            tr.inc()
            P.t(lambda e, kc=kc: e.transpose(p_tr[:], s_x2t[:, kc * 128:(kc + 1) * 128], s_ident[:]).then_inc(tr.h, 1))
            W(P.c, tr, tr.v)
            if x2st_n >= 2:
                Sx2.wait(P.c, x2_tok[x2st_n - 2])
            te.inc()
            P.c(lambda e, b=x2st_n % 2: e.activation(s_x2st[b][:], p_tr[:], AF.Copy).then_inc(te.h, 1))
            W(P.g, te, te.v)
            x2_tok.append(Sx2.issue(P.g, lambda e, kc=kc, tb=tb, b=x2st_n % 2: e.dma_start(
                o_x2.ap()[kc * 128:(kc + 1) * 128, tb * 128:(tb + 1) * 128], s_x2st[b][:])))
            x2st_n += 1
        tr_after_tile.append(tr.v)

    if _PH < 4:
        for stm in (S1, S2, Swt, Sx2, Sspl, Szt, Ssn):
            stm.wait_all(P.g)
        P.finalize()
        return nc
    x2ld_n = 0
    x2pj_at = []
    zst_n = 0
    zt_tok = []
    spl_tok = []
    for which, WT, brow, rope, o_dst in (("k", WkT, s_bk, True, o_ktm),
                                         ("v", WvT, s_bv, False, o_vtm)):
        for nd in range(4):
            if wcast.v >= 1:
                W(P.sy, wcast, wcast.v)
            tokf = Lf.issue(P.sy, lambda e, WT=WT, nd=nd: e.dma_start(
                s_wf32[:].rearrange("p (kc c) -> p kc c", c=512),
                WT.ap()[:, nd * 512:(nd + 1) * 512].rearrange("(kc p) c -> p kc c", p=128)))
            Lf.wait(P.v, tokf)
            if wcast.v > 0:
                W(P.v, pj, pj.v)
            P.v(lambda e: e.tensor_copy(s_wb[:], s_wf32[:]))
            wcast.inc()
            P.v(lambda e: e.drain().then_inc(wcast.h, 1))
            W(P.t, wcast, wcast.v)
            for tb in range(NTB):
                bank = tb % 2
                if x2ld_n == 0:
                    Sx2.wait_all(P.sy)
                if x2ld_n >= 2:
                    W(P.sy, pj, x2pj_at[x2ld_n - 2])
                tok2 = L2.issue(P.sy, lambda e, tb=tb, b=x2ld_n % 2: e.dma_start(
                    s_x2b[b][:].rearrange("p (kc c) -> p kc c", c=128),
                    o_x2.ap()[:, tb * 128:(tb + 1) * 128].rearrange("(kc p) c -> p kc c", p=128)))
                L2.wait(P.t, tok2)
                W(P.t, ev, max(0, ev.v - 1))
                for kc in range(16):
                    P.t(lambda e, kc=kc, bank=bank, b=x2ld_n % 2: e.matmul(
                        p_mm[bank][:], s_x2b[b][:, kc * 128:(kc + 1) * 128],
                        s_wb[:, kc * 512:(kc + 1) * 512], start=(kc == 0), stop=False))
                pj.inc()
                P.t(lambda e, bank=bank, brow=brow, nd=nd: e.matmul(
                    p_mm[bank][:], s_ones_row[:], brow[0:1, nd * 512:(nd + 1) * 512],
                    start=False, stop=True).then_inc(pj.h, 1))
                x2pj_at.append(pj.v)
                x2ld_n += 1
                W(P.c, pj, pj.v)
                if len(spl_tok) >= 2:
                    Sspl.wait(P.c, spl_tok[-2])
                ev.inc()
                P.c(lambda e, bank=bank: e.activation(
                    s_kblk[bank][:], p_mm[bank][:], AF.Copy).then_inc(ev.h, 1))
                if rope:
                    W(P.v, ev, ev.v)
                    kb = s_kblk[bank]
                    cb = s_cos[:, tb * 64:(tb + 1) * 64].rearrange("p f -> p () f").broadcast_to([128, 4, 64])
                    sbb = s_sin[:, tb * 64:(tb + 1) * 64].rearrange("p f -> p () f").broadcast_to([128, 4, 64])
                    ev_v = kb[:].rearrange("p (h two f) -> p h two f", two=2, f=64)[:, :, 0, :]
                    od_v = kb[:].rearrange("p (h two f) -> p h two f", two=2, f=64)[:, :, 1, :]
                    r1v = s_ro1[:].rearrange("p (h f) -> p h f", f=64)
                    r2v = s_ro2[:].rearrange("p (h f) -> p h f", f=64)
                    P.v(lambda e, a=r1v, b=ev_v, c=sbb: e.tensor_tensor(a, b, c, ALU.mult))
                    P.v(lambda e, a=r2v, b=od_v, c=sbb: e.tensor_tensor(a, b, c, ALU.mult))
                    P.v(lambda e: e.drain())
                    P.v(lambda e, a=ev_v, c=cb: e.tensor_tensor(a, a, c, ALU.mult))
                    P.v(lambda e: e.drain())
                    P.v(lambda e, a=ev_v, b=r2v: e.tensor_tensor(a, a, b, ALU.subtract))
                    P.v(lambda e, a=od_v, c=cb: e.tensor_tensor(a, a, c, ALU.mult))
                    P.v(lambda e: e.drain())
                    P.v(lambda e, a=od_v, b=r1v: e.tensor_tensor(a, a, b, ALU.add))
                    P.v(lambda e: e.drain())
                    P.v(lambda e, kb=kb: e.tensor_scalar(s_el1[:], kb[:], 0.0, None, ALU.min))
                    rp.inc()
                    P.v(lambda e: e.drain().then_inc(rp.h, 1))
                    W(P.c, rp, rp.v)
                    lns.inc()
                    P.c(lambda e: e.activation(s_el2[:], s_el1[:], AF.Exp).then_inc(lns.h, 1))
                    W(P.v, lns, lns.v)
                    P.v(lambda e, kb=kb: e.tensor_scalar(s_el1[:], kb[:], 0.0, None, ALU.max))
                    P.v(lambda e: e.drain())
                    P.v(lambda e, kb=kb: e.tensor_tensor(kb[:], s_el1[:], s_el2[:], ALU.add))
                    rp.inc()
                    P.v(lambda e: e.drain().then_inc(rp.h, 1))
                    if tb % 2 == 0 and zs.v > 0:
                        W(P.t, zs, zs.v)
                    W(P.t, rp, rp.v)
                    pj.inc()
                    P.t(lambda e, bank=bank, tb=tb: e.matmul(
                        p_z[:], s_ones_col[:], s_kblk[bank][:],
                        start=(tb == 0), stop=(tb % 2 == 1),
                        skip_group_check=(tb >= 2)).then_inc(pj.h, 1))
                    if tb % 2 == 1:
                        sup = tb // 2
                        W(P.c, pj, pj.v)
                        if zst_n >= 2:
                            Szt.wait(P.c, zt_tok[zst_n - 2])
                        zs.inc()
                        P.c(lambda e, b=zst_n % 2: e.activation(
                            s_zrow[0:1, b * 512:(b + 1) * 512], p_z[:], AF.Copy).then_inc(zs.h, 1))
                        W(P.g, zs, zs.v)
                        zt_tok.append(Szt.issue(P.g, lambda e, sup=sup, nd=nd, b=zst_n % 2: e.dma_start(
                            o_zsnap.ap()[:, sup * D + nd * 512: sup * D + (nd + 1) * 512],
                            s_zrow[0:1, b * 512:(b + 1) * 512])))
                        zst_n += 1
                    W(P.g, rp, rp.v)
                else:
                    W(P.g, ev, ev.v)
                spl_tok.append(Sspl.issue(P.g, lambda e, o_dst=o_dst, tb=tb, nd=nd, bank=bank: e.dma_start(
                    o_dst.ap()[tb * 128:(tb + 1) * 128, nd * 512:(nd + 1) * 512],
                    s_kblk[bank][:])))

    if _PH < 5:
        for stm in (S1, S2, Swt, Sx2, Sspl, Szt, Ssn):
            stm.wait_all(P.g)
        P.finalize()
        return nc
    Sspl.wait_all(P.sy)
    sn_tok = []
    _NHG = int(os.environ.get("KA_HG", "4"))
    _NS5 = int(os.environ.get("KA_NS", str(NSUP)))
    for hg in range(_NHG):
        heads = list(range(hg * 4, hg * 4 + 4))
        for s in range(_NS5):
            pair = (s % 2) * 2
            if hg > 0 and s < 2:
                W(P.sy, sn, hg * 32)
            elif s >= 2:
                W(P.sy, sn, hg * 32 + (s - 1) * 4)
            kv_toks = [
                Lkv.issue(P.sy, lambda e, s=s, pair=pair: e.dma_start(
                    s_kv[pair][:], o_ktm.ap()[s * 256:s * 256 + 128, :])),
                Lkv.issue(P.sy, lambda e, s=s, pair=pair: e.dma_start(
                    s_kv[pair + 1][:], o_ktm.ap()[s * 256 + 128:s * 256 + 256, :])),
            ]
            kv_toks.append(Lkv.issue(P.sy, lambda e, s=s, pair=pair: e.dma_start(
                s_vv[pair][:], o_vtm.ap()[s * 256:s * 256 + 128, :])))
            kv_toks.append(Lkv.issue(P.sy, lambda e, s=s, pair=pair: e.dma_start(
                s_vv[pair + 1][:], o_vtm.ap()[s * 256 + 128:s * 256 + 256, :])))
            for tk in kv_toks:
                Lkv.wait(P.t, tk)
            for hi, h in enumerate(heads):
                if s == 0 and hg > 0:
                    W(P.t, sne, (hg - 1) * 32 + 29 + hi)
                elif s > 0:
                    W(P.t, sne, sne.v - 3)
                P.t(lambda e, h=h, hi=hi, pair=pair, s=s: e.matmul(
                    p_s[hi][:], s_kv[pair][:, h * 128:(h + 1) * 128],
                    s_vv[pair][:, h * 128:(h + 1) * 128],
                    start=(s == 0), stop=False, skip_group_check=(s > 0)))
                sn.inc()
                P.t(lambda e, h=h, hi=hi, pair=pair, s=s: e.matmul(
                    p_s[hi][:], s_kv[pair + 1][:, h * 128:(h + 1) * 128],
                    s_vv[pair + 1][:, h * 128:(h + 1) * 128],
                    start=False, stop=True, skip_group_check=(s > 0)).then_inc(sn.h, 1))
                _M5 = os.environ.get("KA_PA5_MODE", "full")
                if _M5 == "mm":
                    sne.inc()
                    P.c(lambda e: e.engine_nop().then_inc(sne.h, 1))
                    continue
                _EVE = os.environ.get("KA_EVENG", "c")
                PE_ = P.v if _EVE == "v" else P.c
                W(PE_, sn, sn.v)
                if sne.v >= 2:
                    Ssn.wait(PE_, sn_tok[sne.v - 2])
                sne.inc()
                if _EVE == "v":
                    P.v(lambda e, hi=hi, b=(sne.v - 1) % 2: e.tensor_copy(s_x2st[b][:], p_s[hi][:]).then_inc(sne.h, 1))
                else:
                    P.c(lambda e, hi=hi, b=(sne.v - 1) % 2: e.activation(s_x2st[b][:], p_s[hi][:], AF.Copy).then_inc(sne.h, 1))
                if _M5 == "ev":
                    sn_tok.append((Ssn.sems[0], 0))
                    continue
                W(P.g, sne, sne.v)
                sn_tok.append(Ssn.issue(P.g, lambda e, h=h, s=s, b=(sne.v - 1) % 2: e.dma_start(
                    o_ssnap.ap()[(h * NSUP + s) * DK:(h * NSUP + s + 1) * DK, :],
                    s_x2st[b][:])))

    for stm in (Swt, Sx2, Sspl, Szt, Ssn, S1, S2):
        stm.wait_all(P.g)

    P.finalize()
    return nc


def _prep_inputs_A(inputs):
    x = np.asarray(inputs["L0_attn_x"])[0]
    Wk = np.asarray(inputs["L0_attn_Wk"])
    Wv = np.asarray(inputs["L0_attn_Wv"])
    bk = np.asarray(inputs["L0_attn_bk"])
    bv = np.asarray(inputs["L0_attn_bv"])
    w1 = np.asarray(inputs["L0_ffn_weight1"])
    w2 = np.asarray(inputs["L0_ffn_weight2"])
    w3 = np.asarray(inputs["L0_ffn_weight3"])
    WkT = np.ascontiguousarray(Wk.T[:, PERM], np.float32)
    WvT = np.ascontiguousarray(Wv.T, np.float32)
    bkP = np.ascontiguousarray(bk[PERM], np.float32).reshape(1, D)
    bvv = np.ascontiguousarray(bv, np.float32).reshape(1, D)
    ident = np.eye(128, dtype=bf16)
    maps = []
    for c in range(NCORES):
        cos, sin = _rope_tables(c)
        maps.append({
            "xA": np.ascontiguousarray(x[c * TPC:(c + 1) * TPC], np.float32),
            "WkT": WkT, "WvT": WvT, "bk": bkP, "bv": bvv,
            "cosA": cos, "sinA": sin, "ident": ident,
            "w1f": w1, "w2f": w2, "w3f": w3,
            "w1s": np.ascontiguousarray(w1[c * FS1:(c + 1) * FS1]),
            "w2s": np.ascontiguousarray(w2[c * FS1:(c + 1) * FS1]),
            "w3s": np.ascontiguousarray(w3[c * FS3:(c + 1) * FS3]),
        })
    return maps


def _rope_tables(core):
    pos = np.arange(TPC, dtype=np.float64) + core * TPC
    inv = 1.0 / (10000.0 ** (np.arange(0, DK, 2, dtype=np.float64) / DK))
    ang = pos[:, None] * inv[None, :]
    return (np.ascontiguousarray(np.cos(ang), np.float32),
            np.ascontiguousarray(np.sin(ang), np.float32))


def build_B():
    nc = bass.Bass("TRN2", target_bir_lowering=False, detect_race_conditions=False)
    f = nc.dram_tensor

    xB = f("xB", [TPC, D], F32, kind="ExternalInput")
    x2r = f("x2r", [D, TPC], BF16, kind="ExternalInput")
    ktm = f("ktm", [TPC, D], BF16, kind="ExternalInput")
    vtm = f("vtm", [TPC, D], BF16, kind="ExternalInput")
    snapO = f("snapO", [H * NSUP * DK, DK], BF16, kind="ExternalInput")
    sprev = f("sprev", [7 * H * DK, DK], BF16, kind="ExternalInput")
    zsnC = f("zsnC", [128, NSUP * 16], F32, kind="ExternalInput")
    zprC = f("zprC", [128, 32 * 7], F32, kind="ExternalInput")
    WqT = f("WqT", [D, D], F32, kind="ExternalInput")
    WgT = f("WgT", [D, D], F32, kind="ExternalInput")
    WoT = f("WoT", [D, D], F32, kind="ExternalInput")
    bq = f("bq", [1, D], F32, kind="ExternalInput")
    bg = f("bg", [1, D], F32, kind="ExternalInput")
    bo = f("bo", [1, D], F32, kind="ExternalInput")
    w1T = f("w1T", [D, DFF], BF16, kind="ExternalInput")
    w2T = f("w2T", [D, DFF], BF16, kind="ExternalInput")
    w3T = f("w3T", [DFF, D], BF16, kind="ExternalInput")
    cosA = f("cosA", [TPC, 64], F32, kind="ExternalInput")
    sinA = f("sinA", [TPC, 64], F32, kind="ExternalInput")
    ident = f("ident", [128, 128], BF16, kind="ExternalInput")

    o_out = f("o_out", [TPC, D], F32, kind="ExternalOutput")

    _dbg = {}
    if os.environ.get("KB_DBG", "0") == "1":
        _dbg = {"kind": "ExternalOutput"}
    d_qfm = f("d_qfm", [D, TPC], BF16, **_dbg)
    d_gfm = f("d_gfm", [D, TPC], BF16, **_dbg)
    d_ap = f("d_ap", [TPC, D], F32, **_dbg)
    d_x1 = f("d_x1", [TPC, D], F32, **_dbg)
    d_h = f("d_h", [DFF, TPC], BF16, **_dbg)
    d_y = f("d_y", [TPC, D], BF16, **_dbg)

    sb = nc.alloc_sbuf_tensor
    park = [sb(f"park{i}", [128, TPC], BF16) for i in range(16)]
    KV = sb("KV", [128, 16384], BF16)
    A3 = sb("A3", [128, 8192], F32)
    WB = sb("WB", [128, 8192], BF16)
    s_Sst = sb("s_Sst", [128, H * 128], F32)
    s_kfm = sb("s_kfm", [128, 32 * 128], BF16)
    s_cos = sb("s_cos", [128, NTB * 64], F32)
    s_sin = sb("s_sin", [128, NTB * 64], F32)
    s_ktn = [KV[0:128, i * 2048:(i + 1) * 2048] for i in range(4)]
    s_vtn = [KV[0:128, 8192 + i * 2048:8192 + (i + 1) * 2048] for i in range(4)]
    s_spr = [KV[0:128, b * 2048:(b + 1) * 2048] for b in range(2)]
    s_w1c = [KV[0:128, b * 4096:b * 4096 + 2048] for b in range(2)]
    s_w2c = [KV[0:128, b * 4096 + 2048:b * 4096 + 4096] for b in range(2)]
    s_hst = [KV[0:128, 0:5632], KV[0:128, 8192:8192 + 5632]]
    s_wf32 = A3[0:128, 0:8192]
    s_wgf = A3[0:128, 0:2048]
    s_ap2 = [A3[0:128, b * 2048:(b + 1) * 2048] for b in range(2)]
    s_x1t = [A3[0:128, 4096 + b * 2048:4096 + (b + 1) * 2048] for b in range(2)]
    s_sil = [A3[0:128, b * 512:(b + 1) * 512] for b in range(2)]
    s_x1b = [A3[0:128, 4096 + b * 2048:4096 + (b + 1) * 2048] for b in range(2)]
    s_yc = A3[0:128, 2048:4096]
    s_ost = A3[0:128, 0:2048]
    s_wb = WB[0:128, 0:8192]
    s_wgc = WB[0:128, 0:2048]
    s_junk = WB[0:128, 0:2048]
    s_x2pt = WB[0:128, 2048:4096]
    s_yt = [WB[0:128, 4096 + b * 2048:4096 + (b + 1) * 2048] for b in range(2)]
    s_apt = [s_Sst[0:128, b * 512:(b + 1) * 512] for b in range(2)]
    s_ident = sb("s_ident", [128, 128], BF16)
    s_ones_row = sb("s_ones_row", [1, 128], BF16)
    s_ones_col = sb("s_ones_col", [128, 1], BF16)
    s_eps = sb("s_eps", [128, 1], F32)
    s_eps6 = sb("s_eps6", [1, 1], F32)
    s_bq = sb("s_bq", [1, D], BF16)
    s_bo = sb("s_bo", [1, D], BF16)
    s_bgr = sb("s_bgr", [1, D], BF16)
    s_ones5 = sb("s_ones5", [1, 512], BF16)
    s_b32 = A3[0:1, 0:D]
    s_zpr = sb("s_zpr", [128, 32 * 7], F32)
    s_zsn = sb("s_zsn", [128, NSUP * 16], F32)
    s_zst = sb("s_zst", [128, 32], F32)
    s_zu = [sb(f"s_zu{b}", [128, 2], BF16) for b in range(2)]
    s_kblk = [sb(f"s_kblk{i}", [128, 512], BF16) for i in range(2)]
    s_ro1 = sb("s_ro1", [128, 256], BF16)
    s_ro2 = sb("s_ro2", [128, 256], BF16)
    s_el1 = sb("s_el1", [128, 512], BF16)
    s_el2 = sb("s_el2", [128, 512], BF16)
    s_qst = [sb(f"s_qst{i}", [128, 512], BF16) for i in range(2)]
    s_gst = s_qst
    s_qs = [sb(f"s_qs{i}", [128, 256], BF16) for i in range(2)]
    s_gs = [sb(f"s_gs{i}", [128, 256], BF16) for i in range(2)]
    s_at = [sb(f"s_at{i}", [128, 512], BF16) for i in range(2)]
    s_den = [sb(f"s_den{i}", [1, 256], F32) for i in range(2)]
    s_dent = [sb(f"s_dent{i}", [1, 256], F32) for i in range(2)]
    s_recb = [sb(f"s_recb{i}", [1, 256], BF16) for i in range(2)]
    s_num = [sb(f"s_num{i}", [128, 256], F32) for i in range(2)]
    s_rbb = [sb(f"s_rbb{i}", [128, 256], BF16) for i in range(2)]
    s_t1 = [sb(f"s_t1{i}", [128, 256], F32) for i in range(2)]
    s_zuall = sb("s_zuall", [128, 128], BF16)
    s_zero64 = sb("s_zero64", [1, 64], BF16)
    s_suA = [WB[0:128, b * 2048:(b + 1) * 2048] for b in range(2)]
    s_snA = WB[0:128, 4096:6144]
    s_zual32 = None
    s_hfm = s_kblk
    s_yst = s_qst
    s_x2st = [s_ro1[0:128, 0:128], s_ro1[0:128, 128:256]]
    s_sum = sb("s_sum", [128, 1], F32)
    s_sq = sb("s_sq", [128, 1], F32)
    s_mu = sb("s_mu", [128, 1], F32)
    s_lnt = sb("s_lnt", [128, 1], F32)
    s_rstd = sb("s_rstd", [128, 1], F32)

    ps = nc.alloc_psum_tensor
    b = [ps(f"bk{i}", [128, 512], F32) for i in range(6)]
    p_tr = ps("p_tr", [128, 128], BF16)
    p_sm = ps("p_sm", [1, 512], F32)

    P = Prog(nc)
    pe = Sem(nc, "pe")
    ac = Sem(nc, "ac")
    dv = Sem(nc, "dv")
    gp = Sem(nc, "gp")

    Lc = DStream(nc, "Lc", 1)
    Lpk = DStream(nc, "Lpk", 1)
    Lw = DStream(nc, "Lw", 1)
    Lkv = DStream(nc, "Lkv", 8)
    Lq = DStream(nc, "Lq", 2)
    Lg = DStream(nc, "Lg", 2)
    Lsn = DStream(nc, "Lsn", 2)
    Lap = DStream(nc, "Lap", 2)
    Lx = DStream(nc, "Lx", 2)
    L12 = DStream(nc, "L12", 2)
    L3s = DStream(nc, "L3s", 1)
    Lh = DStream(nc, "Lh", 2)
    Ly = DStream(nc, "Ly", 2)
    Lx1 = DStream(nc, "Lx1", 2)
    Sq = DStream(nc, "Sq", 2)
    Sg = DStream(nc, "Sg", 2)
    Sap = DStream(nc, "Sap", 2)
    Sx1 = DStream(nc, "Sx1", 2)
    Sh = DStream(nc, "Sh", 2)
    Sy = DStream(nc, "Sy", 2)
    Sout = DStream(nc, "Sout", 2)

    def W(P_eng, sem, val):
        P_eng(lambda e, s=sem.h, w=val: e.wait_ge(s, w))

    Lc.issue(P.sy, lambda e: e.dma_start(s_ident[:], ident.ap()))
    Lc.issue(P.sy, lambda e: e.dma_start(s_cos[:].rearrange("p (a f) -> p a f", f=64),
                                         cosA.ap().rearrange("(a p) f -> p a f", p=128)))
    Lc.issue(P.sy, lambda e: e.dma_start(s_sin[:].rearrange("p (a f) -> p a f", f=64),
                                         sinA.ap().rearrange("(a p) f -> p a f", p=128)))
    Lc.issue(P.sy, lambda e: e.dma_start(s_zpr[:], zprC.ap()))
    Lc.issue(P.sy, lambda e: e.dma_start(s_zsn[:], zsnC.ap()))
    Lc.issue(P.sy, lambda e: e.dma_start(s_b32[:], bq.ap()))
    Lc.wait_all(P.v)
    P.v(lambda e: e.memset(s_ones_row[:], 1.0))
    P.v(lambda e: e.memset(s_ones_col[:], 1.0))
    P.v(lambda e: e.memset(s_eps[:], float(np.float32(EPS_LN))))
    P.v(lambda e: e.memset(s_eps6[:], 1e-6))
    P.v(lambda e: e.tensor_copy(s_bq[:], s_b32[:]))
    P.v(lambda e: e.drain())
    P.v(lambda e: e.tensor_reduce(s_zst[:], s_zpr[:].rearrange("p (h r) -> p h r", r=7),
                                  AX.X, ALU.add))
    dv.inc()
    P.v(lambda e, w=dv.v: e.drain().then_inc(dv.h, 1))
    _dv_bq = dv.v
    W(P.sy, dv, _dv_bq)
    Lc.issue(P.sy, lambda e: e.dma_start(s_b32[:], bg.ap()))
    Lc.wait_all(P.v)
    W(P.v, dv, _dv_bq)
    P.v(lambda e: e.memset(s_ones5[:], 1.0))
    P.v(lambda e: e.tensor_copy(s_bgr[:], s_b32[:]))
    dv.inc()
    P.v(lambda e: e.drain().then_inc(dv.h, 1))
    _dv_bg = dv.v
    W(P.sy, dv, _dv_bg)
    Lc.issue(P.sy, lambda e: e.dma_start(s_b32[:], bo.ap()))
    Lc.wait_all(P.v)
    W(P.v, dv, _dv_bg)
    P.v(lambda e: e.tensor_copy(s_bo[:], s_b32[:]))
    P.v(lambda e: e.memset(s_Sst[:], 0.0))
    dv.inc()
    P.v(lambda e: e.drain().then_inc(dv.h, 1))
    _dv_b0c = dv.v
    spr_dv = []
    for r in range(7):
        if r >= 2:
            W(P.sy, dv, spr_dv[r - 2])
        tok = Lsn.issue(P.sy, lambda e, r=r, bb=r % 2: e.dma_start(
            s_spr[bb].rearrange("p (h c) -> p h c", c=128),
            sprev.ap()[r * H * DK:(r + 1) * H * DK, :].rearrange("(h p) c -> p h c", p=128)))
        Lsn.wait(P.v, tok)
        P.v(lambda e, bb=r % 2: e.tensor_tensor(s_Sst[:], s_Sst[:], s_spr[bb], ALU.add))
        dv.inc()
        P.v(lambda e: e.drain().then_inc(dv.h, 1))
        spr_dv.append(dv.v)
    B0_DONE = dv.v

    for kc in range(16):
        Lpk.issue(P.sy, lambda e, kc=kc: e.dma_start(
            park[kc][:], x2r.ap()[kc * 128:(kc + 1) * 128, :]))
    Lpk.wait_all(P.t)

    ev_at, tr_at, sq_tok = [], [], []
    qwc_at = []
    for nd in range(4):
        if nd > 0:
            W(P.sy, dv, qwc_at[-1])
        else:
            W(P.sy, dv, _dv_b0c)
        tokf = Lw.issue(P.sy, lambda e, nd=nd: e.dma_start(
            s_wf32.rearrange("p (kc c) -> p kc c", c=512),
            WqT.ap()[:, nd * 512:(nd + 1) * 512].rearrange("(kc p) c -> p kc c", p=128)))
        Lw.wait(P.v, tokf)
        if nd > 0:
            W(P.v, pe, pe.v)
        P.v(lambda e: e.tensor_copy(s_wb[:], s_wf32[:]))
        dv.inc()
        P.v(lambda e: e.drain().then_inc(dv.h, 1))
        qwc_at.append(dv.v)
        W(P.t, dv, dv.v)
        for tb in range(NTB):
            n = nd * NTB + tb
            bank = tb % 2
            if n >= 2:
                W(P.t, ac, ev_at[n - 2])
            for kc in range(16):
                P.t(lambda e, kc=kc, tb=tb, bank=bank: e.matmul(
                    b[4 + bank][:], park[kc][:, tb * 128:(tb + 1) * 128],
                    s_wb[:, kc * 512:(kc + 1) * 512], start=(kc == 0), stop=False))
            pe.inc()
            P.t(lambda e, bank=bank, nd=nd: e.matmul(
                b[4 + bank][:], s_ones_row[:], s_bq[0:1, nd * 512:(nd + 1) * 512],
                start=False, stop=True).then_inc(pe.h, 1))
            W(P.c, pe, pe.v)
            if n >= 2:
                W(P.c, pe, tr_at[n - 2])
            ac.inc()
            P.c(lambda e, bank=bank: e.activation(
                s_kblk[bank][:], b[4 + bank][:], AF.Copy).then_inc(ac.h, 1))
            ev_at.append(ac.v)
            W(P.v, ac, ac.v)
            kb = s_kblk[bank]
            cb = s_cos[:, tb * 64:(tb + 1) * 64].rearrange("p f -> p () f").broadcast_to([128, 4, 64])
            sbb = s_sin[:, tb * 64:(tb + 1) * 64].rearrange("p f -> p () f").broadcast_to([128, 4, 64])
            ev_v = kb[:].rearrange("p (h two f) -> p h two f", two=2, f=64)[:, :, 0, :]
            od_v = kb[:].rearrange("p (h two f) -> p h two f", two=2, f=64)[:, :, 1, :]
            r1v = s_ro1[:].rearrange("p (h f) -> p h f", f=64)
            r2v = s_ro2[:].rearrange("p (h f) -> p h f", f=64)
            P.v(lambda e, a=r1v, x=ev_v, c=sbb: e.tensor_tensor(a, x, c, ALU.mult))
            P.v(lambda e, a=r2v, x=od_v, c=sbb: e.tensor_tensor(a, x, c, ALU.mult))
            P.v(lambda e: e.drain())
            P.v(lambda e, x=ev_v, c=cb: e.tensor_tensor(x, x, c, ALU.mult))
            P.v(lambda e: e.drain())
            P.v(lambda e, x=ev_v, a=r2v: e.tensor_tensor(x, x, a, ALU.subtract))
            P.v(lambda e, x=od_v, c=cb: e.tensor_tensor(x, x, c, ALU.mult))
            P.v(lambda e: e.drain())
            P.v(lambda e, x=od_v, a=r1v: e.tensor_tensor(x, x, a, ALU.add))
            P.v(lambda e: e.drain())
            P.v(lambda e, kb=kb: e.tensor_scalar(s_el1[:], kb[:], 0.0, None, ALU.min))
            dv.inc()
            P.v(lambda e: e.drain().then_inc(dv.h, 1))
            W(P.c, dv, dv.v)
            ac.inc()
            P.c(lambda e: e.activation(s_el2[:], s_el1[:], AF.Exp).then_inc(ac.h, 1))
            W(P.v, ac, ac.v)
            P.v(lambda e, kb=kb: e.tensor_scalar(s_el1[:], kb[:], 0.0, None, ALU.max))
            P.v(lambda e: e.drain())
            dv.inc()
            P.v(lambda e, kb=kb: e.tensor_tensor(kb[:], s_el1[:], s_el2[:], ALU.add).then_inc(dv.h, 1))
            W(P.t, dv, dv.v)
            stg = s_qst[tb % 2]
            for j in range(4):
                if tr_at or j > 0:
                    W(P.t, ac, ac.v)
                pe.inc()
                P.t(lambda e, j=j, bank=bank: e.transpose(
                    p_tr[:], s_kblk[bank][:, j * 128:(j + 1) * 128], s_ident[:]).then_inc(pe.h, 1))
                W(P.c, pe, pe.v)
                if j == 0 and n >= 2:
                    Sq.wait(P.c, sq_tok[n - 2])
                ac.inc()
                P.c(lambda e, j=j, stg=stg: e.activation(
                    stg[:, j * 128:(j + 1) * 128], p_tr[:], AF.Copy).then_inc(ac.h, 1))
            tr_at.append(pe.v)
            W(P.g, ac, ac.v)
            sq_tok.append(Sq.issue(P.g, lambda e, nd=nd, tb=tb, stg=stg: e.dma_start(
                d_qfm.ap()[nd * 512:(nd + 1) * 512, tb * 128:(tb + 1) * 128].rearrange(
                    "(j p) c -> p j c", p=128),
                stg[:].rearrange("p (j c) -> p j c", c=128))))
    PJ_Q_END = pe.v
    QWC_END = qwc_at[-1]

    gev_at, sg_tok = [], []
    gwc_at = []
    for hc in range(16):
        if hc == 0:
            W(P.sy, dv, QWC_END)
        else:
            W(P.sy, dv, gwc_at[-1])
        tokg = Lw.issue(P.sy, lambda e, hc=hc: e.dma_start(
            s_wgf.rearrange("p (kc c) -> p kc c", c=128),
            WgT.ap()[:, hc * 128:(hc + 1) * 128].rearrange("(kc p) c -> p kc c", p=128)))
        Lw.wait(P.v, tokg)
        W(P.v, pe, pe.v)
        W(P.v, dv, _dv_bg)
        P.v(lambda e: e.tensor_copy(s_wgc[:], s_wgf[:]))
        dv.inc()
        P.v(lambda e: e.drain().then_inc(dv.h, 1))
        gwc_at.append(dv.v)
        W(P.t, dv, dv.v)
        for tc in range(4):
            m = hc * 4 + tc
            bank = m % 2
            if m >= 2:
                W(P.t, ac, gev_at[m - 2])
            for kc in range(16):
                P.t(lambda e, kc=kc, tc=tc, bank=bank: e.matmul(
                    b[4 + bank][:], s_wgc[:, kc * 128:(kc + 1) * 128],
                    park[kc][:, tc * 512:(tc + 1) * 512],
                    start=(kc == 0), stop=False))
            pe.inc()
            P.t(lambda e, bank=bank, hc=hc: e.matmul(
                b[4 + bank][:], s_bgr[0:1, hc * 128:(hc + 1) * 128], s_ones5[:],
                start=False, stop=True).then_inc(pe.h, 1))
            W(P.c, pe, pe.v)
            if m >= 2:
                Sg.wait(P.c, sg_tok[m - 2])
            else:
                Sq.wait(P.c, sq_tok[62 + m])
            ac.inc()
            P.c(lambda e, bank=bank: e.activation(
                s_gst[bank][:], b[4 + bank][:], AF.Sigmoid).then_inc(ac.h, 1))
            gev_at.append(ac.v)
            W(P.g, ac, ac.v)
            sg_tok.append(Sg.issue(P.g, lambda e, hc=hc, tc=tc, bank=bank: e.dma_start(
                d_gfm.ap()[hc * 128:(hc + 1) * 128, tc * 512:(tc + 1) * 512], s_gst[bank][:])))
    PJ_G_END = pe.v
    GWC_END = gwc_at[-1]

    ktr_end, pjn_end, atm_end = {}, {}, {}
    pjn_at, go_at, aev_at, atm_at = {}, {}, {}, {}
    denv_at, rcp_at, recb_at = {}, {}, {}
    numev_at, rbev_at, pb_at, t1_at = {}, {}, {}, {}
    su_at = {}
    W(P.t, ac, ac.v)
    zual32 = s_zpr[0:128, 0:128]
    W(P.v, dv, B0_DONE)
    P.v(lambda e: e.memset(s_zero64[:], 0.0))
    P.v(lambda e, z=zual32: e.tensor_copy(
        z.rearrange("p (a c) -> p a c", c=16),
        s_zst[:, 0:16].rearrange("p c -> p () c").broadcast_to([128, 8, 16])))
    P.v(lambda e: e.drain())
    P.v(lambda e, z=zual32: e.tensor_tensor(z[:, 16:128], z[:, 16:128], s_zsn[:, 0:112], ALU.add))
    P.v(lambda e: e.drain())
    P.v(lambda e, z=zual32: e.tensor_copy(s_zuall[:], z))
    dv.inc()
    P.v(lambda e: e.drain().then_inc(dv.h, 1))
    ZU_RDY = dv.v
    for s in range(NSUP):
        pair = (s % 2) * 2
        if s == 0:
            Sq.wait_all(P.sy)
            Sg.wait_all(P.sy)
            W(P.sy, dv, B0_DONE)
            W(P.sy, pe, PJ_G_END)
        if s >= 2:
            W(P.sy, pe, ktr_end[s - 2])
            W(P.sy, pe, pjn_end[s - 2])
        kv_toks = [
            Lkv.issue(P.sy, lambda e, s=s, pair=pair: e.dma_start(
                s_ktn[pair], ktm.ap()[s * 256:s * 256 + 128, :])),
            Lkv.issue(P.sy, lambda e, s=s, pair=pair: e.dma_start(
                s_ktn[pair + 1], ktm.ap()[s * 256 + 128:s * 256 + 256, :])),
            Lkv.issue(P.sy, lambda e, s=s, pair=pair: e.dma_start(
                s_vtn[pair], vtm.ap()[s * 256:s * 256 + 128, :])),
            Lkv.issue(P.sy, lambda e, s=s, pair=pair: e.dma_start(
                s_vtn[pair + 1], vtm.ap()[s * 256 + 128:s * 256 + 256, :])),
        ]
        if s == 0:
            W(P.v, pe, PJ_G_END)
            dv.inc()
            P.v(lambda e: e.tensor_copy(s_suA[0], s_Sst[:]).then_inc(dv.h, 1))
            su_at[0] = ("dv", dv.v)
        else:
            if s >= 2:
                W(P.sy, gp, su_at[s - 1][1] if su_at[s - 1][0] == "gp" else 0)
            tok_sn = Lsn.issue(P.sy, lambda e, s=s: e.dma_start(
                s_snA.rearrange("p (h c) -> p h c", c=128),
                _AP(tensor=snapO.ap().tensor, offset=(s - 1) * DK * DK,
                    ap=[[128, 128], [NSUP * DK * DK, 16], [1, 128]])))
            Lsn.wait(P.g, tok_sn)
            if s >= 2:
                W(P.g, pe, pjn_end[s - 2])
            gp.inc()
            P.g(lambda e, b2=s % 2: e.tensor_tensor(
                s_suA[b2], s_Sst[:], s_snA, ALU.add).then_inc(gp.h, 1))
            su_at[s] = ("gp", gp.v)
        Lkv.wait(P.t, kv_toks[0])
        Lkv.wait(P.t, kv_toks[1])
        if s >= 1:
            W(P.t, pe, atm_end[s - 1])
        for hh in range(32):
            W(P.t, ac, ac.v)
            pe.inc()
            P.t(lambda e, hh=hh, pair=pair: e.transpose(
                p_tr[:], s_ktn[pair + (hh % 2)][:, (hh // 2) * 128:(hh // 2 + 1) * 128],
                s_ident[:]).then_inc(pe.h, 1))
            W(P.c, pe, pe.v)
            ac.inc()
            P.c(lambda e, hh=hh: e.activation(
                s_kfm[:, ((hh // 2) * 2 + (hh % 2)) * 128:((hh // 2) * 2 + (hh % 2) + 1) * 128],
                p_tr[:], AF.Copy).then_inc(ac.h, 1))
        ktr_end[s] = pe.v
        Lkv.wait(P.t, kv_toks[2])
        Lkv.wait(P.t, kv_toks[3])
        for h in range(H):
            n = s * H + h
            bq_ = n % 2
            if n >= 2:
                W(P.sy, pe, pjn_at[n - 2])
                W(P.sy, gp, go_at[n - 2])
            tq = Lq.issue(P.sy, lambda e, h=h, s=s, b2=bq_: e.dma_start(
                s_qs[b2][:], d_qfm.ap()[h * 128:(h + 1) * 128, s * 256:(s + 1) * 256]))
            tg = Lg.issue(P.sy, lambda e, h=h, s=s, b2=bq_: e.dma_start(
                s_gs[b2][:], d_gfm.ap()[h * 128:(h + 1) * 128, s * 256:(s + 1) * 256]))
            if n >= 1:
                W(P.t, ac, aev_at[n - 1])
            if n == 0:
                W(P.t, dv, ZU_RDY)
            Lq.wait(P.t, tq)
            for half, r0, c0 in ((0, 0, 0), (0, 64, 64), (1, 0, 128), (1, 64, 192)):
                P.t(lambda e, h=h, half=half, r0=r0, c0=c0, b2=bq_: e.matmul(
                    b[half][r0:r0 + 64, c0:256],
                    s_kfm[:, (h * 2 + half) * 128 + r0:(h * 2 + half) * 128 + r0 + 64],
                    s_qs[b2][:, c0:256], start=True, stop=True))
            P.t(lambda e: e.matmul(b[0][64:128, 0:64], s_zero64[:], s_ones5[0:1, 0:64],
                                   start=True, stop=True))
            P.t(lambda e: e.matmul(b[1][0:64, 0:128], s_zero64[:], s_ones5[0:1, 0:128],
                                   start=True, stop=True))
            pe.inc()
            P.t(lambda e: e.matmul(b[1][64:128, 0:192], s_zero64[:], s_ones5[0:1, 0:192],
                                   start=True, stop=True).then_inc(pe.h, 1))
            atm_at[n] = pe.v
            W(P.c, pe, atm_at[n])
            if n >= 2:
                W(P.c, pe, pjn_at[n - 2])
            for half in range(2):
                ac.inc()
                P.c(lambda e, half=half, b2=bq_: e.activation(
                    s_at[b2][:, half * 256:(half + 1) * 256], b[half][:, 0:256],
                    AF.Copy).then_inc(ac.h, 1))
            aev_at[n] = ac.v
            W(P.t, ac, aev_at[n])
            if s == 0 or su_at[s][0] == "dv":
                W(P.t, dv, su_at[s][1])
            else:
                W(P.t, gp, su_at[s][1])
            if n >= 1:
                W(P.t, ac, numev_at[n - 1])
                W(P.t, ac, denv_at[n - 1])
            P.t(lambda e, h=h, s=s, b2=bq_: e.matmul(
                b[2][:, 0:256], s_suA[s % 2][:, h * 128:(h + 1) * 128], s_qs[b2][:],
                start=True, stop=False))
            P.t(lambda e, pair=pair, h=h, b2=bq_: e.matmul(
                b[2][:, 0:256], s_vtn[pair][:, h * 128:(h + 1) * 128], s_at[b2][:, 0:256],
                start=False, stop=False))
            pe.inc()
            P.t(lambda e, pair=pair, h=h, b2=bq_: e.matmul(
                b[2][:, 0:256], s_vtn[pair + 1][:, h * 128:(h + 1) * 128], s_at[b2][:, 256:512],
                start=False, stop=True).then_inc(pe.h, 1))
            P.t(lambda e, s=s, h=h, b2=bq_: e.matmul(
                p_sm[0:1, 0:256], s_zuall[:, s * 16 + h:s * 16 + h + 1], s_qs[b2][:],
                start=True, stop=False))
            P.t(lambda e, b2=bq_: e.matmul(p_sm[0:1, 0:256], s_ones_col[:], s_at[b2][:, 0:256],
                                           start=False, stop=False))
            pe.inc()
            P.t(lambda e, b2=bq_: e.matmul(p_sm[0:1, 0:256], s_ones_col[:], s_at[b2][:, 256:512],
                                           start=False, stop=True).then_inc(pe.h, 1))
            pjn_at[n] = pe.v
            W(P.c, pe, pjn_at[n])
            if n >= 2:
                W(P.c, dv, rcp_at[n - 2])
            ac.inc()
            P.c(lambda e, b2=bq_: e.activation(
                s_den[b2][:], p_sm[0:1, 0:256], AF.Copy, bias=1e-6).then_inc(ac.h, 1))
            denv_at[n] = ac.v
            W(P.v, ac, denv_at[n])
            if n >= 2:
                W(P.v, ac, recb_at[n - 2])
            dv.inc()
            P.v(lambda e, b2=bq_: e.reciprocal(s_dent[b2][:], s_den[b2][:]).then_inc(dv.h, 1))
            rcp_at[n] = dv.v
            W(P.c, dv, rcp_at[n])
            if n >= 2:
                W(P.c, pe, pb_at[n - 2])
            ac.inc()
            P.c(lambda e, b2=bq_: e.activation(s_recb[b2][:], s_dent[b2][:], AF.Copy).then_inc(ac.h, 1))
            recb_at[n] = ac.v
            if n >= 2:
                W(P.c, gp, t1_at[n - 2])
            ac.inc()
            P.c(lambda e, b2=bq_: e.activation(s_num[b2][:], b[2][:, 0:256], AF.Copy).then_inc(ac.h, 1))
            numev_at[n] = ac.v
            W(P.t, ac, recb_at[n])
            if n >= 1:
                W(P.t, ac, rbev_at[n - 1])
            pe.inc()
            P.t(lambda e, b2=bq_: e.matmul(b[3][:, 0:256], s_ones_row[:], s_recb[b2][:],
                                           start=True, stop=True).then_inc(pe.h, 1))
            pb_at[n] = pe.v
            W(P.c, pe, pb_at[n])
            ac.inc()
            P.c(lambda e, b2=bq_: e.activation(s_rbb[b2][:], b[3][:, 0:256], AF.Copy).then_inc(ac.h, 1))
            rbev_at[n] = ac.v
            W(P.g, ac, rbev_at[n])
            gp.inc()
            P.g(lambda e, b2=bq_: e.tensor_tensor(
                s_t1[b2][:], s_num[b2][:], s_rbb[b2][:], ALU.mult).then_inc(gp.h, 1))
            t1_at[n] = gp.v
            Lg.wait(P.g, tg)
            gp.inc()
            P.g(lambda e, h=h, s=s, b2=bq_: e.tensor_tensor(
                park[h][:, s * 256:(s + 1) * 256], s_t1[b2][:], s_gs[b2][:],
                ALU.mult).then_inc(gp.h, 1))
            go_at[n] = gp.v
        atm_end[s] = atm_at[s * H + H - 1]
        pjn_end[s] = pjn_at[s * H + H - 1]
    SCAN_DONE = go_at[NSUP * H - 1]
    su_at_last = su_at[NSUP - 1][1]
    PJN_END = pjn_end[NSUP - 1]

    oev_at, sap_tok, owc_at = [], [], []
    W(P.t, gp, SCAN_DONE)
    for nd in range(4):
        if nd == 0:
            W(P.sy, dv, GWC_END)
        else:
            W(P.sy, dv, owc_at[-1])
        tokf = Lw.issue(P.sy, lambda e, nd=nd: e.dma_start(
            s_wf32.rearrange("p (kc c) -> p kc c", c=512),
            WoT.ap()[:, nd * 512:(nd + 1) * 512].rearrange("(kc p) c -> p kc c", p=128)))
        Lw.wait(P.v, tokf)
        W(P.v, pe, pe.v)
        P.v(lambda e: e.tensor_copy(s_wb[:], s_wf32[:]))
        dv.inc()
        P.v(lambda e: e.drain().then_inc(dv.h, 1))
        owc_at.append(dv.v)
        W(P.t, dv, dv.v)
        for tc in range(NTB):
            p2 = nd * NTB + tc
            bank = tc % 2
            if p2 >= 2:
                W(P.t, ac, oev_at[p2 - 2])
            for kc in range(16):
                P.t(lambda e, kc=kc, tc=tc, bank=bank: e.matmul(
                    b[4 + bank][:], park[kc][:, tc * 128:(tc + 1) * 128],
                    s_wb[:, kc * 512:(kc + 1) * 512], start=(kc == 0), stop=False))
            pe.inc()
            P.t(lambda e, bank=bank, nd=nd: e.matmul(
                b[4 + bank][:], s_ones_row[:], s_bo[0:1, nd * 512:(nd + 1) * 512],
                start=False, stop=True).then_inc(pe.h, 1))
            W(P.c, pe, pe.v)
            if p2 == 0:
                W(P.c, gp, su_at_last)
            if p2 >= 2:
                Sap.wait(P.c, sap_tok[p2 - 2])
            ac.inc()
            P.c(lambda e, bank=bank: e.activation(
                s_apt[bank], b[4 + bank][:], AF.Copy).then_inc(ac.h, 1))
            oev_at.append(ac.v)
            W(P.g, ac, ac.v)
            sap_tok.append(Sap.issue(P.g, lambda e, nd=nd, tc=tc, bank=bank: e.dma_start(
                d_ap.ap()[tc * 128:(tc + 1) * 128, nd * 512:(nd + 1) * 512], s_apt[bank])))
    OPJ_END = pe.v
    OWC_END = owc_at[-1]

    Sap.wait_all(P.sy)
    W(P.sy, dv, OWC_END)
    W(P.c, pe, OPJ_END)
    W(P.t, ac, ac.v)
    lnv_ap, lnv_x1, lnv_xp, tr_b2, sx1_tok = {}, {}, {}, {}, []
    for tc in range(NTB):
        b2 = tc % 2
        if tc >= 2:
            W(P.sy, dv, lnv_xp[tc - 2])
            W(P.sy, dv, lnv_x1[tc - 2])
            Sx1.wait(P.sy, sx1_tok[tc - 2])
        t_ap = Lap.issue(P.sy, lambda e, tc=tc, b2=b2: e.dma_start(
            s_ap2[b2], d_ap.ap()[tc * 128:(tc + 1) * 128, :]))
        t_x = Lx.issue(P.sy, lambda e, tc=tc, b2=b2: e.dma_start(
            s_x1t[b2], xB.ap()[tc * 128:(tc + 1) * 128, :]))
        Lap.wait(P.c, t_ap)
        W(P.c, dv, dv.v)
        P.c(lambda e, b2=b2: e.activation(s_junk[:, 0:D], s_ap2[b2], AF.Copy, accum_out=s_sum[:]))
        ac.inc()
        P.c(lambda e: e.activation(s_mu[:], s_sum[:], AF.Copy, scale=float(np.float32(1.0 / D))).then_inc(ac.h, 1))
        W(P.v, ac, ac.v)
        dv.inc()
        P.v(lambda e, b2=b2: e.tensor_scalar(s_ap2[b2], s_ap2[b2], s_mu[:], None, ALU.subtract).then_inc(dv.h, 1))
        W(P.c, dv, dv.v)
        P.c(lambda e, b2=b2: e.activation(s_junk[:, 0:D], s_ap2[b2], AF.Square, accum_out=s_sq[:]))
        P.c(lambda e: e.activation(s_lnt[:], s_sq[:], AF.Ln, bias=s_eps[:], scale=float(np.float32(1.0 / D))))
        ac.inc()
        P.c(lambda e: e.activation(s_rstd[:], s_lnt[:], AF.Exp, scale=-0.5).then_inc(ac.h, 1))
        Lx.wait(P.v, t_x)
        W(P.v, ac, ac.v)
        P.v(lambda e, b2=b2: e.tensor_scalar(s_ap2[b2], s_ap2[b2], s_rstd[:], None, ALU.mult))
        P.v(lambda e: e.drain())
        dv.inc()
        P.v(lambda e, b2=b2: e.tensor_tensor(s_x1t[b2], s_x1t[b2], s_ap2[b2], ALU.add).then_inc(dv.h, 1))
        lnv_ap[tc] = dv.v
        W(P.g, dv, dv.v)
        sx1_tok.append(Sx1.issue(P.g, lambda e, tc=tc, b2=b2: e.dma_start(
            d_x1.ap()[tc * 128:(tc + 1) * 128, :], s_x1t[b2])))
        W(P.c, dv, dv.v)
        P.c(lambda e, b2=b2: e.activation(s_junk[:, 0:D], s_x1t[b2], AF.Copy, accum_out=s_sum[:]))
        ac.inc()
        P.c(lambda e: e.activation(s_mu[:], s_sum[:], AF.Copy, scale=float(np.float32(1.0 / D))).then_inc(ac.h, 1))
        W(P.v, ac, ac.v)
        dv.inc()
        P.v(lambda e, b2=b2: e.tensor_scalar(s_ap2[b2], s_x1t[b2], s_mu[:], None, ALU.subtract).then_inc(dv.h, 1))
        lnv_x1[tc] = dv.v
        W(P.c, dv, dv.v)
        P.c(lambda e, b2=b2: e.activation(s_junk[:, 0:D], s_ap2[b2], AF.Square, accum_out=s_sq[:]))
        P.c(lambda e: e.activation(s_lnt[:], s_sq[:], AF.Ln, bias=s_eps[:], scale=float(np.float32(1.0 / D))))
        ac.inc()
        P.c(lambda e: e.activation(s_rstd[:], s_lnt[:], AF.Exp, scale=-0.5).then_inc(ac.h, 1))
        W(P.v, ac, ac.v)
        if tc >= 1:
            W(P.v, pe, tr_b2[tc - 1])
        dv.inc()
        P.v(lambda e, b2=b2: e.tensor_scalar(s_x2pt[:], s_ap2[b2], s_rstd[:], None, ALU.mult).then_inc(dv.h, 1))
        lnv_xp[tc] = dv.v
        W(P.t, dv, dv.v)
        for kc in range(16):
            W(P.t, ac, ac.v)
            pe.inc()
            P.t(lambda e, kc=kc: e.transpose(p_tr[:], s_x2pt[:, kc * 128:(kc + 1) * 128],
                                             s_ident[:]).then_inc(pe.h, 1))
            W(P.c, pe, pe.v)
            ac.inc()
            P.c(lambda e, kc=kc, tc=tc: e.activation(
                park[kc][:, tc * 128:(tc + 1) * 128], p_tr[:], AF.Copy).then_inc(ac.h, 1))
        tr_b2[tc] = pe.v
    LNV_END = dv.v
    AC_X2P = ac.v
    DNV_END = t1_at[NSUP * H - 1]

    W(P.t, ac, AC_X2P)
    fab_at, fevA_at, fevV_at, sh_tok = {}, {}, {}, []
    W(P.sy, pe, PJN_END)
    W(P.c, dv, LNV_END)
    W(P.v, pe, PJ_Q_END)
    l12_tok = {}
    for fc in range(44):
        if fc >= 2:
            W(P.sy, pe, fab_at[(fc - 2) * 4 + 3])
        l12_tok[fc] = (
            L12.issue(P.sy, lambda e, fc=fc, bb=fc % 2: e.dma_start(
                s_w1c[bb].rearrange("p (dc c) -> p dc c", c=128),
                w1T.ap()[:, fc * 128:(fc + 1) * 128].rearrange("(dc p) c -> p dc c", p=128))),
            L12.issue(P.sy, lambda e, fc=fc, bb=fc % 2: e.dma_start(
                s_w2c[bb].rearrange("p (dc c) -> p dc c", c=128),
                w2T.ap()[:, fc * 128:(fc + 1) * 128].rearrange("(dc p) c -> p dc c", p=128))),
        )
        L12.wait(P.t, l12_tok[fc][0])
        L12.wait(P.t, l12_tok[fc][1])
        for tb4 in range(4):
            g2 = fc * 4 + tb4
            ba = 4 if g2 % 2 == 0 else 2
            if g2 >= 2:
                W(P.t, ac, fevA_at[g2 - 2])
                W(P.t, dv, fevV_at[g2 - 2])
            elif g2 == 1:
                W(P.t, gp, DNV_END)
            for dc in range(16):
                P.t(lambda e, dc=dc, tb4=tb4, bb=fc % 2, ba=ba: e.matmul(
                    b[ba][:], s_w1c[bb][:, dc * 128:(dc + 1) * 128],
                    park[dc][:, tb4 * 512:(tb4 + 1) * 512], start=(dc == 0), stop=(dc == 15)))
            pe.inc()
            for dc in range(16):
                if dc == 15:
                    P.t(lambda e, dc=dc, tb4=tb4, bb=fc % 2, ba=ba: e.matmul(
                        b[ba + 1][:], s_w2c[bb][:, dc * 128:(dc + 1) * 128],
                        park[dc][:, tb4 * 512:(tb4 + 1) * 512],
                        start=False, stop=True).then_inc(pe.h, 1))
                else:
                    P.t(lambda e, dc=dc, tb4=tb4, bb=fc % 2, ba=ba: e.matmul(
                        b[ba + 1][:], s_w2c[bb][:, dc * 128:(dc + 1) * 128],
                        park[dc][:, tb4 * 512:(tb4 + 1) * 512], start=(dc == 0), stop=False))
            fab_at[g2] = pe.v
            W(P.c, pe, fab_at[g2])
            if g2 >= 2:
                W(P.c, dv, fevV_at[g2 - 2])
            ac.inc()
            P.c(lambda e, bb=g2 % 2, ba=ba: e.activation(s_sil[bb], b[ba][:], AF.Silu).then_inc(ac.h, 1))
            fevA_at[g2] = ac.v
            W(P.v, ac, fevA_at[g2])
            if g2 >= 2:
                Sh.wait(P.v, sh_tok[g2 - 2])
            dv.inc()
            P.v(lambda e, bb=g2 % 2, ba=ba: e.tensor_tensor(
                s_hfm[bb][:], s_sil[bb], b[ba + 1][:], ALU.mult).then_inc(dv.h, 1))
            fevV_at[g2] = dv.v
            W(P.g, dv, fevV_at[g2])
            sh_tok.append(Sh.issue(P.g, lambda e, fc=fc, tb4=tb4, bb=g2 % 2: e.dma_start(
                d_h.ap()[fc * 128:(fc + 1) * 128, tb4 * 512:(tb4 + 1) * 512], s_hfm[bb][:])))
    FAB_END = fab_at[43 * 4 + 3]

    Sh.wait_all(P.sy)
    W(P.sy, pe, FAB_END)
    ymm_at, yev_at, sy_tok = {}, {}, []
    ymm_end = {}
    for dq in range(4):
        if dq >= 1:
            W(P.sy, pe, ymm_end[dq - 1])
        w3_toks = []
        for j in range(11):
            w3_toks.append(L3s.issue(P.sy, lambda e, j=j, dq=dq: e.dma_start(
                park[j][:].rearrange("p (q c) -> p q c", c=512),
                w3T.ap()[j * 512:(j + 1) * 512, dq * 512:(dq + 1) * 512].rearrange(
                    "(q p) c -> p q c", p=128))))
        for tk in w3_toks:
            L3s.wait(P.t, tk)
        for tt in range(NTB):
            y2 = dq * NTB + tt
            if y2 >= 2:
                W(P.sy, pe, ymm_at[y2 - 2])
            th = Lh.issue(P.sy, lambda e, tt=tt, bb=tt % 2: e.dma_start(
                s_hst[bb].rearrange("p (fc c) -> p fc c", c=128),
                d_h.ap()[:, tt * 128:(tt + 1) * 128].rearrange("(fc p) c -> p fc c", p=128)))
            Lh.wait(P.t, th)
            if y2 >= 2:
                W(P.t, ac, yev_at[y2 - 2])
            for fc in range(44):
                if fc == 43:
                    pe.inc()
                    P.t(lambda e, fc=fc, bb=tt % 2, bk=tt % 2: e.matmul(
                        b[bk][:], s_hst[bb][:, fc * 128:(fc + 1) * 128],
                        park[fc // 4][:, (fc % 4) * 512:(fc % 4 + 1) * 512],
                        start=False, stop=True).then_inc(pe.h, 1))
                else:
                    P.t(lambda e, fc=fc, bb=tt % 2, bk=tt % 2: e.matmul(
                        b[bk][:], s_hst[bb][:, fc * 128:(fc + 1) * 128],
                        park[fc // 4][:, (fc % 4) * 512:(fc % 4 + 1) * 512],
                        start=(fc == 0), stop=False))
            ymm_at[y2] = pe.v
            W(P.c, pe, ymm_at[y2])
            if y2 >= 2:
                Sy.wait(P.c, sy_tok[y2 - 2])
            else:
                Sg.wait(P.c, sg_tok[62 + y2])
            ac.inc()
            P.c(lambda e, bk=tt % 2, bb=y2 % 2: e.activation(
                s_yst[bb][:], b[bk][:], AF.Copy).then_inc(ac.h, 1))
            yev_at[y2] = ac.v
            W(P.g, ac, ac.v)
            sy_tok.append(Sy.issue(P.g, lambda e, tt=tt, dq=dq, bb=y2 % 2: e.dma_start(
                d_y.ap()[tt * 128:(tt + 1) * 128, dq * 512:(dq + 1) * 512], s_yst[bb][:])))
        ymm_end[dq] = pe.v

    Sy.wait_all(P.sy)
    Sx1.wait_all(P.sy)
    fin_at, so_tok = {}, []
    for tt in range(NTB):
        b2 = tt % 2
        if tt >= 2:
            W(P.sy, dv, fin_at[tt - 2])
        ty = Ly.issue(P.sy, lambda e, tt=tt, b2=b2: e.dma_start(
            s_yt[b2], d_y.ap()[tt * 128:(tt + 1) * 128, :]))
        tx1 = Lx1.issue(P.sy, lambda e, tt=tt, b2=b2: e.dma_start(
            s_x1b[b2], d_x1.ap()[tt * 128:(tt + 1) * 128, :]))
        Ly.wait(P.c, ty)
        W(P.c, dv, dv.v)
        P.c(lambda e, b2=b2: e.activation(s_junk[:, 0:D], s_yt[b2], AF.Copy, accum_out=s_sum[:]))
        ac.inc()
        P.c(lambda e: e.activation(s_mu[:], s_sum[:], AF.Copy, scale=float(np.float32(1.0 / D))).then_inc(ac.h, 1))
        W(P.v, ac, ac.v)
        dv.inc()
        P.v(lambda e, b2=b2: e.tensor_scalar(s_yc, s_yt[b2], s_mu[:], None, ALU.subtract).then_inc(dv.h, 1))
        W(P.c, dv, dv.v)
        P.c(lambda e: e.activation(s_junk[:, 0:D], s_yc, AF.Square, accum_out=s_sq[:]))
        P.c(lambda e: e.activation(s_lnt[:], s_sq[:], AF.Ln, bias=s_eps[:], scale=float(np.float32(1.0 / D))))
        ac.inc()
        P.c(lambda e: e.activation(s_rstd[:], s_lnt[:], AF.Exp, scale=-0.5).then_inc(ac.h, 1))
        Lx1.wait(P.v, tx1)
        W(P.v, ac, ac.v)
        P.v(lambda e: e.tensor_scalar(s_yc, s_yc, s_rstd[:], None, ALU.mult))
        P.v(lambda e: e.drain())
        if tt >= 1:
            Sout.wait(P.v, so_tok[tt - 1])
        dv.inc()
        P.v(lambda e, b2=b2: e.tensor_tensor(s_ost, s_x1b[b2], s_yc, ALU.add).then_inc(dv.h, 1))
        fin_at[tt] = dv.v
        W(P.g, dv, fin_at[tt])
        so_tok.append(Sout.issue(P.g, lambda e, tt=tt: e.dma_start(
            o_out.ap()[tt * 128:(tt + 1) * 128, :], s_ost)))

    for stm in (Sq, Sg, Sap, Sx1, Sh, Sy, Sout):
        stm.wait_all(P.g)
    P.finalize()
    return nc


def _prep_inputs_B(inputs, resA):
    x = np.asarray(inputs["L0_attn_x"])[0]
    Wq = np.asarray(inputs["L0_attn_Wq"])
    Wg = np.asarray(inputs["L0_attn_Wg"])
    Wo = np.asarray(inputs["L0_attn_Wo"])
    bq = np.asarray(inputs["L0_attn_bq"])
    bg = np.asarray(inputs["L0_attn_bg"])
    bo = np.asarray(inputs["L0_attn_bo"])
    WqT = np.ascontiguousarray(Wq.T[:, PERM], np.float32)
    WgT = np.ascontiguousarray(Wg.T, np.float32)
    WoT = np.ascontiguousarray(Wo.T, np.float32)
    bqP = np.ascontiguousarray(bq[PERM], np.float32).reshape(1, D)
    bgv = np.ascontiguousarray(bg, np.float32).reshape(1, D)
    bov = np.ascontiguousarray(bo, np.float32).reshape(1, D)
    ident = np.eye(128, dtype=bf16)
    w1T = np.ascontiguousarray(np.concatenate([resA[r]["o_w1t"] for r in range(NCORES)], axis=1))
    w2T = np.ascontiguousarray(np.concatenate([resA[r]["o_w2t"] for r in range(NCORES)], axis=1))
    w3T = np.ascontiguousarray(np.concatenate([resA[r]["o_w3t"] for r in range(NCORES)], axis=1))
    maps = []
    for c in range(NCORES):
        cos, sin = _rope_tables(c)
        spv = np.zeros((7 * H * DK, DK), bf16)
        zpr = np.zeros((128, 32, 7), np.float32)
        for r in range(c):
            sn = np.asarray(resA[r]["o_ssnap"]).reshape(H, NSUP, DK, DK)[:, NSUP - 1]
            spv[r * H * DK:(r + 1) * H * DK] = sn.reshape(H * DK, DK)
            zf = np.asarray(resA[r]["o_zsnap"]).reshape(NSUP, H, 128)[NSUP - 1]
            zpr[:, :H, r] = zf.T
        zsn = np.ascontiguousarray(
            np.asarray(resA[c]["o_zsnap"]).reshape(NSUP, H, 128).transpose(2, 0, 1).reshape(128, NSUP * H),
            np.float32)
        maps.append({
            "xB": np.ascontiguousarray(x[c * TPC:(c + 1) * TPC], np.float32),
            "x2r": np.asarray(resA[c]["o_x2"]),
            "ktm": np.asarray(resA[c]["o_ktm"]),
            "vtm": np.asarray(resA[c]["o_vtm"]),
            "snapO": np.asarray(resA[c]["o_ssnap"]),
            "sprev": spv,
            "zsnC": zsn,
            "zprC": np.ascontiguousarray(zpr.reshape(128, 224), np.float32),
            "WqT": WqT, "WgT": WgT, "WoT": WoT,
            "bq": bqP, "bg": bgv, "bo": bov,
            "w1T": w1T, "w2T": w2T, "w3T": w3T,
            "cosA": cos, "sinA": sin, "ident": ident,
        })
    return maps


def kernel(**inputs):
    mapsA = _prep_inputs_A(inputs)
    ncA = build_A()
    resA = run_bass_kernel_spmd(ncA, mapsA, core_ids=list(range(NCORES))).results
    mapsB = _prep_inputs_B(inputs, resA)
    ncB = build_B()
    resB = run_bass_kernel_spmd(ncB, mapsB, core_ids=list(range(NCORES))).results
    out = np.concatenate([np.asarray(resB[c]["o_out"], np.float32) for c in range(NCORES)], axis=0)
    return out.reshape(1, T, D)
